# version 16
# baseline (speedup 1.0000x reference)
import numpy as np
import concourse.bass as bass
import concourse.mybir as mybir
from concourse.bass_utils import run_bass_kernel_spmd

F32 = mybir.dt.float32
BF16 = mybir.dt.bfloat16
U8 = mybir.dt.uint8
OP = mybir.AluOpType
AF = mybir.ActivationFunctionType
AX = mybir.AxisListType.X

RC = 12582912.0
WCLIP = 1.4999999
C52 = 6755399441055744.0
X52 = C52 - RC
D, S, INNER, H = 1024, 2048, 2048, 4096
DC, IC, HC = D // 128, INNER // 128, H // 128
TW, TST, NT = 258, 256, 8

ENGINES = ("sync", "vector", "scalar", "tensor", "gpsimd")


class Sem:
    def __init__(self, nc, name):
        self.h = nc.alloc_semaphore(name)
        self.n = 0

    @property
    def tok(self):
        return (self, self.n)


class Prog:

    def __init__(self, nc):
        self.nc = nc
        self.items = {e: [] for e in ENGINES}
        self.waited = {e: {} for e in ENGINES}

    def emit(self, eng, fn, after=(), inc=None, inck=1):
        waits = []
        wm = self.waited[eng]
        for t in after:
            if t is None:
                continue
            s, v = t
            if v <= 0:
                continue
            if v > wm.get(s, 0):
                waits.append((s.h, v))
                wm[s] = v
        tok = None
        pair = None
        if inc is not None:
            inc.n += inck
            tok = (inc, inc.n)
            pair = (inc.h, inck)
        self.items[eng].append((waits, fn, pair))
        return tok

    def barrier(self, sems):
        toks = [s.tok for s in sems]
        for e in ENGINES:
            for s, v in toks:
                if v > 0 and v > self.waited[e].get(s, 0):
                    self.items[e].append(([(s.h, v)], None, None))
                    self.waited[e][s] = v


def build(phases=3, dbg=False, ntc=None):
    import os
    if ntc is None:
        ntc = int(os.environ.get("K_NTC", "99"))
    kstage = int(os.environ.get("K_STAGE", "99"))
    nc = bass.Bass("TRN2", target_bir_lowering=False)
    p = Prog(nc)

    dt_in = {}

    def din(name, shape, dtype=F32):
        dt_in[name] = nc.dram_tensor(name, list(shape), dtype,
                                     kind="ExternalInput").ap()
        return dt_in[name]

    xT = din("xT", [D, S])
    w1T = din("w1T", [D, H])
    w2T = din("w2T", [H, D])
    in_ws = din("in_ws", [IC, 128, DC * 128])
    gate_ws = din("gate_ws", [IC, 128, DC * 128])
    b_ws = din("b_ws", [IC, 128, IC * 128])
    c_ws = din("c_ws", [IC, 128, IC * 128])
    d_ws = din("d_ws", [IC, 128, IC * 128])
    out_ws = din("out_ws", [DC, 128, IC * 128])
    in_bc = din("in_bc", [128, IC])
    gate_bc = din("gate_bc", [128, IC])
    b_br = din("b_br", [1, INNER])
    c_bc = din("c_bc", [128, IC])
    d_bc = din("d_bc", [128, IC])
    out_bc = din("out_bc", [128, DC])
    nw_c = din("nw_c", [128, DC])
    alog_c = din("alog_c", [128, IC])
    tap0_c = din("tap0_c", [128, HC])
    tap1_c = din("tap1_c", [128, HC])
    tap2_c = din("tap2_c", [128, HC])
    convb_c = din("convb_c", [128, HC])
    ident_d = din("ident_d", [128, 128])

    outT = nc.dram_tensor("outT", [D, S], F32, kind="ExternalOutput").ap()
    dbg_h = nc.dram_tensor("dbg_h", [128, HC * TW], F32,
                           kind="ExternalOutput").ap() if dbg else None
    dbg_r = nc.dram_tensor("dbg_r", [128, 3 * TW], F32,
                           kind="ExternalOutput").ap() if dbg else None
    dbg_c = nc.dram_tensor("dbg_c", [128, 64], F32,
                           kind="ExternalOutput").ap() if dbg else None
    dbg_x2 = nc.dram_tensor("dbg_x2", [128, 2 * TW], F32,
                            kind="ExternalOutput").ap() if dbg else None
    dbg_xn = nc.dram_tensor("dbg_xn", [128, DC * S], BF16,
                            kind="ExternalOutput").ap() if dbg else None
    dbg_s2 = nc.dram_tensor("dbg_s2", [1, TW], F32,
                            kind="ExternalOutput").ap() if dbg else None
    xs1 = nc.dram_tensor("xs1", [D, S], F32, kind="Internal").ap()
    rs = nc.dram_tensor("rs", [D, S], F32, kind="Internal").ap()
    xs2 = nc.dram_tensor("xs2", [D, S], F32, kind="Internal").ap()
    w1q_d = nc.dram_tensor("w1q_d", [128, DC * H], BF16, kind="Internal").ap()
    w2q_d = nc.dram_tensor("w2q_d", [128, HC * D], BF16, kind="Internal").ap()

    ident = nc.alloc_sbuf_tensor("ident", [128, 128], F32).ap()
    ones128 = nc.alloc_sbuf_tensor("ones128", [128, 128], F32).ap()
    onesc = nc.alloc_sbuf_tensor("onesc", [128, 1], F32).ap()
    onesc16 = nc.alloc_sbuf_tensor("onesc16", [128, 1], BF16).ap()
    ones16 = nc.alloc_sbuf_tensor("ones16", [1, 512], BF16).ap()
    bbr16 = nc.alloc_sbuf_tensor("bbr16", [1, INNER], BF16).ap()
    kc = nc.alloc_sbuf_tensor("kc", [128, 8], F32).ap()
    sb_inb = nc.alloc_sbuf_tensor("sb_inb", [128, IC], F32).ap()
    sb_gateb = nc.alloc_sbuf_tensor("sb_gateb", [128, IC], F32).ap()
    sb_cb = nc.alloc_sbuf_tensor("sb_cb", [128, IC], F32).ap()
    sb_db = nc.alloc_sbuf_tensor("sb_db", [128, IC], F32).ap()
    sb_outb = nc.alloc_sbuf_tensor("sb_outb", [128, DC], F32).ap()
    sb_nw = nc.alloc_sbuf_tensor("sb_nw", [128, DC], F32).ap()
    sb_alog = nc.alloc_sbuf_tensor("sb_alog", [128, IC], F32).ap()
    sb_acol = nc.alloc_sbuf_tensor("sb_acol", [128, IC], F32).ap()
    sb_t0 = nc.alloc_sbuf_tensor("sb_t0", [128, HC], F32).ap()
    sb_t1 = nc.alloc_sbuf_tensor("sb_t1", [128, HC], F32).ap()
    sb_w2t = nc.alloc_sbuf_tensor("sb_w2t", [128, HC], F32).ap()
    sb_cb2 = nc.alloc_sbuf_tensor("sb_cb2", [128, HC], F32).ap()
    sb_r02 = nc.alloc_sbuf_tensor("sb_r02", [128, HC], F32).ap()
    sb_r12 = nc.alloc_sbuf_tensor("sb_r12", [128, HC], F32).ap()
    sb_mw = nc.alloc_sbuf_tensor("sb_mw", [128, 8], F32).ap()
    cols = nc.alloc_sbuf_tensor("cols", [128, 64], F32).ap()
    rowb = nc.alloc_sbuf_tensor("rowb", [1, 512], F32).ap()

    AR = 198 * 1024
    backing = nc.alloc_sbuf_tensor("arena", [128, AR], U8)
    base = nc.lookup_mloc(backing).addr

    def place(name, shape, dtype, off):
        per = int(np.prod(shape[1:])) * mybir.dt.size(dtype)
        assert off >= 0 and off + per <= AR, f"{name}: {off}+{per} > {AR}"
        h = nc.virtual_sbuf_tensor(name, list(shape), dtype)
        nc._tensor(h.name, h.shape, h.dtype, type="SB")
        mloc = nc.lookup_mloc(h)
        mloc.allocated = True
        mloc.addr = base + off
        return h.ap()

    o = 0
    A_w1q = place("A_w1q", [128, DC * H], BF16, o); o += DC * H * 2
    o_w2q = o
    A_w2q = place("A_w2q", [128, HC * D], BF16, o); o += HC * D * 2
    A_n2B = [place(f"A_n2B_{k}", [128, 512], BF16, o_w2q + k * 1024)
             for k in range(HC)]
    A_wr = [place(f"A_wr_{i}", [128, 4096], BF16,
                  o_w2q + HC * 1024 + i * 8192) for i in range(2)]
    A_r4B = place("A_r4B", [128, 512], F32, o_w2q + HC * 1024 + 16384)
    o_xw3 = o_w2q + HC * 1024 + 16384 + 2048
    A_xw2 = [place(f"A_xw{i}", [128, DC * TW], F32, o + i * DC * TW * 4)
             for i in range(2)]
    o += 2 * DC * TW * 4
    A_xw2.append(place("A_xw2b", [128, DC * TW], F32, o_xw3))
    assert o_xw3 + DC * TW * 4 <= o_w2q + HC * D * 2
    o_h = o
    A_h = place("A_h", [128, HC * TW], F32, o); o += HC * TW * 4
    A_n2 = [place(f"A_n2_{m}", [128, 256], BF16, o_h + m * TW * 4)
            for m in range(HC)]
    A_wc = [place(f"A_wc{i}", [128, 1024], F32, o_h + i * 4096) for i in range(4)]
    A_n1 = place("A_n1", [128, DC * TW], BF16, o); o += DC * TW * 2
    o_qt = o
    A_qt = place("A_qt", [128, 256], F32, o); o += 1024
    A_eq = place("A_eq", [128, 256], F32, o); o += 1024
    A_qe = place("A_qe", [128, 512], F32, o_qt)
    o_t1 = o
    A_t1 = [place(f"A_t1_{i}", [128, TW], F32, o + i * TW * 4) for i in range(2)]
    o += 2 * TW * 4
    o_t2 = o
    A_t2 = [place(f"A_t2_{i}", [128, TW], F32, o + i * TW * 4) for i in range(2)]
    o += 2 * TW * 4
    A_t1b = [place(f"A_t1b_{i}", [128, TW], BF16, o_t1 + i * TW * 4)
             for i in range(2)]
    A_t2b = [place(f"A_t2b_{i}", [128, TW], BF16, o_t2 + i * TW * 2)
             for i in range(2)]
    A_mx = place("A_mx", [128, TW], F32, o); o += TW * 4
    A_mn = place("A_mn", [128, TW], F32, o); o += TW * 4
    A_r0 = place("A_r0", [128, TW], F32, o); o += TW * 4
    A_r1 = place("A_r1", [128, TW], F32, o); o += TW * 4
    A_r2 = place("A_r2", [128, TW], F32, o); o += TW * 4
    A_bt = place("A_bt", [128, 9 * 128], F32, o)
    o += 9 * 128 * 4
    A_r3 = place("A_r3", [128, 256], F32, o); o += 1024
    A_r4 = place("A_r4", [128, 256], F32, o); o += 1024
    assert o <= AR, o

    o = 0
    o_xn = o
    B_xn = place("B_xn", [128, DC * S], BF16, o); o += DC * S * 2
    o_u = o
    B_u = place("B_u", [128, IC * S], BF16, o); o += IC * S * 2
    o_y = o
    B_y = place("B_y", [128, IC * S], BF16, o); o += IC * S * 2
    B_hs = place("B_hs", [128, S], F32, o); o += S * 4
    B_scr = place("B_scr", [128, S], F32, o); o += S * 4
    o_bg = o
    B_g = place("B_g", [128, 2 * 1024], F32, o); o += 2 * 1024 * 4
    B_gb = [place(f"B_gb_{i}", [128, 512], BF16, o_bg + i * 1024)
            for i in range(2)]
    B_gxb = [place(f"B_gxb_{i}", [128, 512], BF16, o_bg + 2048 + i * 1024)
             for i in range(2)]
    assert o <= AR, o
    B_x1 = place("B_x1", [128, DC * 512], F32, o_u)
    B_xl = place("B_xl", [128, DC * 512], F32, o_u + DC * 512 * 4)
    B_rows = place("B_rows", [128, 3 * 512], F32, o_y)
    B_bt = place("B_bt", [128, 4 * 128], F32, o_y + 3 * 512 * 4)
    B_igs = [place(f"B_igs{i}", [128, DC * 128], BF16,
                   o_y + 3 * 512 * 4 + 4 * 128 * 4 + i * DC * 256)
             for i in range(4)]
    B_bcds = [place(f"B_bcds{i}", [128, IC * 128], BF16, o_xn + i * IC * 256)
              for i in range(6)]
    B_outs = [place(f"B_outs{i}", [128, IC * 128], BF16,
                    o_xn + (6 + i) * IC * 256)
              for i in range(2)]
    B_rr = [place(f"B_rr{i}", [128, S], F32, o_u + i * S * 4) for i in range(2)]

    PA = nc.alloc_psum_tensor("PA", [128, 2048], F32).ap()
    PB = nc.alloc_psum_tensor("PB", [128, 2048], F32).ap()

    sDX = Sem(nc, "sDX")
    sDW = Sem(nc, "sDW")
    sDS = Sem(nc, "sDS")
    sDG = Sem(nc, "sDG")
    sTE = Sem(nc, "sTE")
    sV = Sem(nc, "sV")
    sSC = Sem(nc, "sSC")
    sGP = Sem(nc, "sGP")
    sW = [Sem(nc, f"sW{i}") for i in range(4)]
    sG2 = [Sem(nc, "sG0"), Sem(nc, "sG1")]
    sB2 = [Sem(nc, "sB0"), Sem(nc, "sB1")]
    sO2 = [Sem(nc, "sO0"), Sem(nc, "sO1")]
    sR2 = [Sem(nc, "sR0"), Sem(nc, "sR1")]
    sMISC = Sem(nc, "sMISC")
    sS2 = [Sem(nc, "sS0"), Sem(nc, "sS1")]
    sX2 = [Sem(nc, "sX0"), Sem(nc, "sX1"), Sem(nc, "sX2b")]
    sWR = [Sem(nc, "sWR0"), Sem(nc, "sWR1")]
    ALLS = [sDX, sDW, sDS, sDG, sTE, sV, sSC, sGP,
            sW[0], sW[1], sW[2], sW[3], sG2[0], sG2[1], sB2[0], sB2[1],
            sO2[0], sO2[1], sR2[0], sR2[1], sMISC, sS2[0], sS2[1],
            sX2[0], sX2[1], sX2[2], sWR[0], sWR[1]]

    def Y(fn, after=(), sem=None):
        return p.emit("sync", fn, after, inc=sem if sem else sDX, inck=16)

    def GD(fn, after=(), sem=None):
        return p.emit("gpsimd", fn, after, inc=sem if sem else sDG, inck=16)

    def SD(fn, after=(), sem=None):
        return p.emit("scalar", fn, after, inc=sem if sem else sDW, inck=16)

    def V(fn, after=()):
        return p.emit("vector", fn, after, inc=sV)

    def SC(fn, after=()):
        return p.emit("scalar", fn, after, inc=sSC)

    def T(fn, after=(), inc=True):
        return p.emit("tensor", fn, after, inc=sTE if inc else None)

    def G(fn, after=()):
        return p.emit("gpsimd", fn, after, inc=sGP)

    for name, dst in [("ident_d", ident), ("in_bc", sb_inb),
                      ("gate_bc", sb_gateb), ("c_bc", sb_cb), ("d_bc", sb_db),
                      ("out_bc", sb_outb), ("nw_c", sb_nw), ("alog_c", sb_alog),
                      ("tap0_c", sb_t0), ("tap1_c", sb_t1), ("tap2_c", sb_w2t),
                      ("convb_c", sb_cb2)]:
        src = dt_in[name]
        Y(lambda e, d=dst, s=src: e.dma_start(out=d, in_=s))
    t_consts = sDX.tok
    V(lambda e: e.memset(kc[:, 0:1], 1.0))
    V(lambda e: e.memset(kc[:, 1:2], -1.0))
    V(lambda e: e.memset(kc[:, 2:3], -0.5))
    V(lambda e: e.memset(kc[:, 3:4], 0.0))
    V(lambda e: e.memset(kc[:, 4:5], RC))
    V(lambda e: e.memset(kc[:, 5:6], -RC))
    tk_ = V(lambda e: e.memset(kc[:, 7:8], 1.0))
    tk_ = V(lambda e: e.tensor_scalar(kc[:, 7:8], kc[:, 7:8], C52, C52,
                                      OP.add, OP.subtract), after=[tk_])
    t_ceff = V(lambda e: e.tensor_scalar(kc[:, 6:7], kc[:, 7:8], X52, RC,
                                         OP.mult, OP.add), after=[tk_])
    V(lambda e: e.memset(onesc, 1.0))
    V(lambda e: e.memset(onesc16, 1.0))
    V(lambda e: e.memset(ones128, 1.0))
    V(lambda e: e.memset(ones16, 1.0))
    V(lambda e: e.memset(rowb, 0.0))
    V(lambda e: e.memset(cols, 0.0))
    t_bb = GD(lambda e: e.dma_start(out=bbr16, in_=dt_in["b_br"]), sem=sMISC)
    tv = V(lambda e: e.reciprocal(sb_r02, sb_w2t), after=[t_consts])
    tv2 = V(lambda e: e.tensor_mul(sb_r12, sb_t1, sb_r02), after=[tv])
    V(lambda e: e.tensor_mul(sb_r02, sb_t0, sb_r02), after=[tv2])
    c0, c1 = cols[:, 0:IC], cols[:, 16:32]
    tv = V(lambda e: e.tensor_scalar(c0, sb_alog, -1.0, None, OP.mult),
           after=[t_consts])
    tv = V(lambda e: e.tensor_tensor(c0, c0, sb_alog, OP.max), after=[tv])
    ts = SC(lambda e: e.activation(c1, c0, AF.Exp, scale=kc[:, 1:2]),
            after=[tv])
    ts = SC(lambda e: e.activation(c1, c1, AF.Ln, bias=kc[:, 0:1]), after=[ts])
    tv = V(lambda e: e.tensor_scalar(c0, sb_alog, 0.0, None, OP.max),
           after=[t_consts, tv, ts])
    tv = V(lambda e: e.tensor_tensor(c0, c0, c1, OP.add), after=[tv, ts])
    t_acol = SC(lambda e: e.activation(sb_acol, c0, AF.Exp, scale=kc[:, 1:2]),
                after=[tv])

    pb_last = {2: None, 3: None}
    def rsqrt_rows(dst, src, tmp, deps):
        t1 = V(lambda e: e.tensor_scalar(src, src, 1e-30, None, OP.max),
               after=deps)
        t1 = SC(lambda e: e.activation(tmp, src, AF.Ln), after=[t1])
        return SC(lambda e: e.activation(dst, tmp, AF.Exp, scale=kc[:, 2:3]),
                  after=[t1])

    def bcast_multi(jobs, bt, deps, gen_yield=False):
        it = _bcast_multi_gen(jobs, bt, deps)
        if gen_yield:
            return it
        last = None
        for last in it:
            pass
        return last

    def _bcast_multi_gen(jobs, bt, deps):
        blocks = []
        for dst, src_col in jobs:
            nblk = src_col.shape[1]
            for c_ in range(nblk):
                blocks.append((dst, src_col, c_, min(128, dst.shape[1] - 128 * c_)))
        nb = len(blocks)
        t0 = V(lambda e, bl=list(blocks): [
            None], after=deps) if False else None
        off = 0
        copy_toks = []
        offs = []
        tprev = None
        for dst, src_col in jobs:
            nblk = src_col.shape[1]
            tprev = V(lambda e, sc=src_col, o=off, nn=nblk: e.tensor_copy(
                bt[:, o:o + nn * 128].rearrange("p (c j) -> p c j", c=nn),
                sc.unsqueeze(2).broadcast_to([128, nn, 128])),
                after=deps + [tprev])
            offs.append(off)
            off += nblk * 128
        t0 = tprev
        assert off <= bt.shape[1], (off, bt.shape)
        i = 0
        last = None
        for ji, (dst, src_col) in enumerate(jobs):
            nblk = src_col.shape[1]
            for c_ in range(nblk):
                n = min(128, dst.shape[1] - 128 * c_)
                bk = 2 + (i % 2)
                pbase = 1024 + 512 * (bk - 2) + 256
                tt = T(lambda e, o=offs[ji] + 128 * c_, pb=pbase: e.transpose(
                    PB[:, pb:pb + 128], bt[:, o:o + 128], ident),
                    after=[t0, t_consts, pb_last[bk]])
                last = V(lambda e, c=c_, nn=n, d=dst, pb=pbase: e.tensor_copy(
                    d[:, 128 * c:128 * c + nn], PB[:, pb:pb + nn]),
                    after=[tt, last])
                pb_last[bk] = last
                i += 1
                yield last

    def bcast_row(dst, src_col, nblk, bt, deps, upto=99):
        return bcast_multi([(dst, src_col[:, 0:nblk])], bt, deps)

    def row_transposes(src_row, offs, dst_cols, deps):
        toks = []
        for j, so in enumerate(offs):
            tt = T(lambda e, jj=j, s=so: e.transpose(
                PB[:, 1024 + jj:1024 + jj + 1], src_row[0:1, s:s + 128],
                ident[0:1, 0:1]),
                after=deps + [t_consts, pb_last[2]])
            toks.append(tt)
        ev = V(lambda e, n=len(offs): e.tensor_copy(
            dst_cols, PB[:, 1024:1024 + n]), after=[toks[-1]])
        pb_last[2] = ev
        return ev

    def ffn_phase(src_dram, dst_dram, quantize):
        t_w1q = t_w2q = t_w2d = None

        def gen_Q(wi):
            nonlocal t_w1q, t_w2q, t_w2d
            wT, wq, wq_dr = [(w1T, A_w1q, w1q_d),
                             (w2T, A_w2q, w2q_d)][wi]
            nch = 32
            ncol = H if wi == 0 else D
            nq = ncol // 1024
            sumc = cols[:, 32:64]
            toks = []
            for i in range(nch):
                kcb, q = divmod(i, nq)
                src = wT[128 * kcb:128 * kcb + 128, 1024 * q:1024 * q + 1024]
                dep = [toks[i - 4][1]] if i >= 4 else [t_acol]
                tl = Y(lambda e, d=A_wc[i % 4], s=src:
                       e.dma_start(out=d, in_=s), after=dep, sem=sW[i % 4])
                tr = V(lambda e, d=A_wc[i % 4], c=sumc[:, i:i + 1]:
                       e.reduce_sum(c, d, AX, apply_absolute_value=True),
                       after=[tl])
                toks.append((tl, tr))
                yield
            tv_ = V(lambda e: e.reduce_sum(cols[:, 30:31], sumc, AX),
                    after=[toks[-1][1]])
            tm_ = T(lambda e: e.matmul(PB[:, 1024:1025], ones128,
                                       cols[:, 30:31],
                                       start=True, stop=True),
                    after=[tv_, t_consts, pb_last[2]])
            tv_ = V(lambda e: e.tensor_copy(cols[:, 31:32], PB[:, 1024:1025]),
                    after=[tm_])
            pb_last[2] = tv_
            tv_ = V(lambda e, nrm=1.0 / (1024 * 4096): e.tensor_scalar(
                cols[:, 30:31], cols[:, 31:32], nrm, 1e-5,
                OP.mult, OP.max), after=[tv_])
            tv_ = V(lambda e, m=wi: e.reciprocal(sb_mw[:, m:m + 1],
                                                 cols[:, 30:31]),
                    after=[tv_])
            if wi == 0:
                tsc = V(lambda e: e.tensor_scalar(
                    sb_mw[:, 2:3], cols[:, 30:31], 1.0 / 127, None,
                    OP.mult), after=[tv_])
            else:
                tsc = V(lambda e: e.tensor_scalar(
                    sb_mw[:, 3:4], cols[:, 30:31], 0.5 / 127, None,
                    OP.mult), after=[tv_])
            qtk = []
            for i in range(nch):
                kcb, q = divmod(i, nq)
                src = wT[128 * kcb:128 * kcb + 128, 1024 * q:1024 * q + 1024]
                dep = [qtk[i - 4]] if i >= 4 else [tsc]
                tl = Y(lambda e, d=A_wc[i % 4], s=src:
                       e.dma_start(out=d, in_=s), after=dep, sem=sW[i % 4])
                tq = V(lambda e, d=A_wc[i % 4], m=wi: e.tensor_scalar(
                    d, d, sb_mw[:, m:m + 1], WCLIP, OP.mult, OP.min),
                    after=[tl, tsc])
                tq = V(lambda e, d=A_wc[i % 4]: e.tensor_scalar(
                    d, d, -WCLIP, None, OP.max), after=[tq])
                tq = V(lambda e, d=A_wc[i % 4],
                       w=wq[:, 1024 * i:1024 * (i + 1)]:
                       e.tensor_scalar(w, d, kc[:, 6:7], kc[:, 6:7],
                                       OP.add, OP.subtract),
                       after=[tq, t_ceff])
                qtk.append(tq)
                yield
            td = Y(lambda e, w=wq, dr=wq_dr: e.dma_start(out=dr, in_=w),
                   after=[qtk[-1]], sem=sDS)
            if wi == 0:
                t_w1q = qtk[-1]
            else:
                t_w2q = qtk[-1]
                t_w2d = td

        evtok = {}
        st_hist = {}
        state = {}
        n2hist = {}
        wr_used = {}
        base_sS2 = [sS2[0].n, sS2[1].n]
        assert NT % 2 == 0
        ctx = {"rows_user": None, "conv_last": None, "silu_last": None,
               "evac_last": None}

        def gen_A(tc):
            st = state.setdefault(tc, {})
            A_xw = A_xw2[tc % 3]
            t0 = TST * tc - 1
            lo, hi = max(t0, 0), min(t0 + TW, S)
            woff = lo - t0
            xl = None
            for dcc in range(DC):
                xl = Y(lambda e,
                       d=A_xw[:, dcc * TW + woff:dcc * TW + woff + hi - lo],
                       s=src_dram[128 * dcc:128 * dcc + 128, lo:hi]:
                       e.dma_start(out=d, in_=s),
                       after=[st_hist.get(tc - 3), t_w2d],
                       sem=sX2[tc % 3])
            t_xl = xl
            if tc == 0:
                ap0 = bass.AP(A_xw.tensor, A_xw.offset,
                              [[A_xw.ap[0][0], 128], [TW, DC], [1, 1]])
                t_xl = V(lambda e, a=ap0: e.memset(a, 0.0), after=[t_xl])
            if tc == NT - 1:
                ap1 = bass.AP(A_xw.tensor, A_xw.offset + TW - 1,
                              [[A_xw.ap[0][0], 128], [TW, DC], [1, 1]])
                t_xl = V(lambda e, a=ap1: e.memset(a, 0.0), after=[t_xl])
            yield
            mm_s = []
            for dcc in range(DC):
                x_t = A_xw[:, dcc * TW:(dcc + 1) * TW]
                x2t = A_t1b[dcc % 2]
                xbt = A_t2b[dcc % 2]
                dep = [t_xl, ctx["conv_last"]]
                if dcc >= 2:
                    dep.append(mm_s[dcc - 2])
                tcst = SC(lambda e, o_=xbt, i_=x_t: e.activation(
                    o_, i_, AF.Identity), after=dep)
                tq = SC(lambda e, o_=x2t, i_=x_t: e.activation(o_, i_,
                                                               AF.Square),
                        after=[tcst])
                T(lambda e, i_=xbt, f=(dcc == 0), l=(dcc == DC - 1):
                  e.matmul(PA[0:1, 1024:1024 + TW], onesc16, i_,
                           start=f, stop=l),
                  after=[tcst, t_consts, ctx["rows_user"],
                         ctx["evac_last"]], inc=False)
                tm2 = T(lambda e, i_=x2t, f=(dcc == 0), l=(dcc == DC - 1):
                        e.matmul(PA[0:1, 1536:1536 + TW], onesc16, i_,
                                 start=f, stop=l),
                        after=[tq, ctx["evac_last"]], inc=True)
                mm_s.append(tm2)
                if dcc % 3 == 2:
                    yield
            tv_ = V(lambda e: e.tensor_copy(rowb[0:1, 0:TW],
                                            PA[0:1, 1024:1024 + TW]),
                    after=[mm_s[-1]])
            yield
            t_s1 = row_transposes(rowb, [0, 128, 256], cols[:, 0:3], [tv_])
            tv_ = V(lambda e: e.tensor_copy(rowb[0:1, 0:TW],
                                            PA[0:1, 1536:1536 + TW]),
                    after=[tv_, t_s1])
            ctx["rows_user"] = tv_
            yield
            t_scols = row_transposes(rowb, [0, 128, 256], cols[:, 3:6],
                                     [tv_])
            xv = bass.AP(A_xw.tensor, A_xw.offset,
                         [[A_xw.ap[0][0], 128], [1, TW], [TW, DC]])
            tmx = V(lambda e, v=xv: e.tensor_reduce(A_mx, v, AX.X, OP.max),
                    after=[t_xl])
            tmn = V(lambda e, v=xv: e.tensor_reduce(A_mn, v, AX.X, OP.min),
                    after=[t_xl, tmx])
            yield
            stat_toks = {}
            for sidx, (buf, ci, rop, dep0) in enumerate(
                    [(A_mx, 6, OP.max, tmx), (A_mn, 9, OP.min, tmn)]):
                for blk in range(3):
                    n = min(128, TW - 128 * blk)
                    tt = T(lambda e, b=blk, bf=buf, nn=n: e.transpose(
                        PB[0:nn, 1280:1280 + 128],
                        bf[:, 128 * b:128 * b + nn],
                        ident), after=[dep0, pb_last[2], t_consts])
                    pb_last[2] = V(lambda e, b=blk, cc=ci, nn=n, op=rop:
                                   e.tensor_reduce(
                                       cols[0:nn, cc + b:cc + b + 1],
                                       PB[0:nn, 1280:1280 + 128], AX, op),
                                   after=[tt])
                    if blk == 1:
                        yield
                stat_toks[sidx] = pb_last[2]
                yield
            t_mxc, t_mnc = stat_toks[0], stat_toks[1]
            CW = 3
            m_c = cols[:, 12:12 + CW]; inv_c = cols[:, 15:15 + CW]
            var_c = cols[:, 18:18 + CW]; ra_c = cols[:, 21:21 + CW]
            q1_c = cols[:, 24:24 + CW]; o1_c = cols[:, 27:27 + CW]
            mq_c = cols[:, 44:44 + CW]; tmpc = cols[:, 48:48 + CW]
            tv_ = V(lambda e: e.tensor_scalar(m_c, cols[:, 0:3], 1.0 / D,
                                              None, OP.mult),
                    after=[t_scols])
            tv_ = V(lambda e: e.tensor_tensor(var_c, m_c, m_c, OP.mult),
                    after=[tv_])
            tv_ = V(lambda e: e.tensor_scalar(tmpc, cols[:, 3:6], 1.0 / D,
                                              None, OP.mult), after=[tv_])
            tv_ = V(lambda e: e.tensor_tensor(var_c, tmpc, var_c,
                                              OP.subtract), after=[tv_])
            tv_ = V(lambda e: e.tensor_scalar(var_c, var_c, 1e-5, None,
                                              OP.add), after=[tv_])
            tinv = rsqrt_rows(inv_c, var_c, tmpc, [tv_])
            yield
            tv_ = V(lambda e: e.tensor_tensor(ra_c, cols[:, 6:9], m_c,
                                              OP.subtract),
                    after=[t_mxc, tinv])
            tv_ = V(lambda e: e.tensor_tensor(tmpc, m_c, cols[:, 9:12],
                                              OP.subtract),
                    after=[t_mnc, tv_])
            tv_ = V(lambda e: e.tensor_tensor(ra_c, ra_c, tmpc, OP.max),
                    after=[tv_])
            tv_ = V(lambda e: e.tensor_tensor(ra_c, ra_c, inv_c, OP.mult),
                    after=[tv_])
            tv_ = V(lambda e: e.tensor_scalar(ra_c, ra_c, 1e-5, None,
                                              OP.max), after=[tv_])
            tq1 = V(lambda e: e.reciprocal(q1_c, ra_c), after=[tv_])
            tq1 = V(lambda e: e.tensor_tensor(q1_c, q1_c, inv_c, OP.mult),
                    after=[tq1])
            tq1 = V(lambda e: e.tensor_scalar(q1_c, q1_c, 127.0, None,
                                              OP.mult), after=[tq1])
            to1 = V(lambda e: e.tensor_scalar(o1_c, ra_c, sb_mw[:, 2:3],
                                              None, OP.mult),
                    after=[tq1, t_w1q])
            tmq = V(lambda e: e.tensor_tensor(mq_c, m_c, q1_c, OP.mult),
                    after=[to1])
            yield
            tB = bcast_multi([(A_r0, q1_c), (A_r1, o1_c), (A_r2, mq_c)],
                             A_bt, [tq1, to1, tmq], gen_yield=True)
            tB0 = None
            for tB0 in tB:
                yield
            st["tB1"] = tB0
            tv_ = tB0
            n1_toks = []
            n1s = A_t2[1]
            for dcc in range(DC):
                x_t = A_xw[:, dcc * TW:(dcc + 1) * TW]
                n1t = A_n1[:, dcc * TW:(dcc + 1) * TW]
                tv_ = V(lambda e, i_=x_t: e.tensor_tensor(
                    n1s, i_, A_r0, OP.mult),
                    after=[tB0, t_xl, tv_, ctx["conv_last"],
                           ctx["silu_last"]])
                tv_ = V(lambda e: e.tensor_tensor(n1s, n1s, A_r2,
                                                  OP.subtract),
                        after=[tv_])
                tv_ = V(lambda e, o_=n1t: e.tensor_scalar(
                    o_, n1s, kc[:, 6:7], kc[:, 6:7], OP.add, OP.subtract),
                    after=[tv_, t_ceff])
                n1_toks.append(tv_)
                if dcc % 2 == 1:
                    yield
            st["n1"] = n1_toks
            st["xw"] = A_xw

        def drive(g, n):
            if g is None:
                return
            for _ in range(n):
                if next(g, "END") == "END":
                    return

        def claim(dep):
            tm_ = dep
            for k in range(HC):
                tm_ = V(lambda e, kk=k: e.memset(A_n2B[kk], 0.0),
                        after=[tm_])
            tm_ = V(lambda e: e.memset(A_r4B, 0.0), after=[tm_])
            return V(lambda e: e.memset(A_xw2[2], 0.0), after=[tm_])

        pend = None
        if quantize:
            for _ in gen_Q(0):
                pass
            gen0 = gen_A(0)
            nq2 = 0
            for _ in gen_Q(1):
                nq2 += 1
                if nq2 <= 32:
                    if nq2 % 3 == 0 and nq2 <= 30:
                        drive(gen0, 1)
                elif nq2 % 2 == 0:
                    drive(gen0, 1)
            t_w2d = claim(t_w2d)
        else:
            t_w1q = Y(lambda e: e.dma_start(out=A_w1q, in_=w1q_d),
                      sem=sW[0])
            t_w2d = claim(None)
            gen0 = gen_A(0)
        drive(gen0, 999)
        pend = None
        for tc in range(min(NT, ntc)):
            if tc > 0:
                drive(pend, 999)
            st = state[tc]
            A_xw = st["xw"]
            n1_toks = st["n1"]
            tB1 = st["tB1"]
            h_toks = {}
            cv_toks = {}
            for m in range(HC):
                pab = m % 4
                deps = [t_w1q, evtok.get(("h", m - 4))]
                if m in (2, 3):
                    deps.append(ctx["rows_user"])
                for kcb in range(DC):
                    tm = T(lambda e, k=kcb, mm=m, b=pab, f=(kcb == 0),
                           l=(kcb == DC - 1):
                           e.matmul(PA[:, 512 * b:512 * b + TW],
                                    A_w1q[:, k * H + 128 * mm:
                                          k * H + 128 * mm + 128],
                                    A_n1[:, k * TW:(k + 1) * TW],
                                    start=f, stop=l),
                           after=(deps + [n1_toks[kcb]]) if kcb == 0
                           else [n1_toks[kcb]],
                           inc=(kcb == DC - 1))
                h_t = A_h[:, m * TW:(m + 1) * TW]
                tev = V(lambda e, o_=h_t, b=pab: e.tensor_tensor(
                    o_, PA[:, 512 * b:512 * b + TW], A_r1, OP.mult),
                    after=[tm, tB1, t_w2q])
                evtok[("h", m)] = tev
                t1b = A_t1[m % 2]
                tc1 = V(lambda e, hh=h_t, mm=m, tb=t1b:
                        e.scalar_tensor_tensor(
                    tb[:, 0:256], hh[:, 0:256], sb_r02[:, mm:mm + 1],
                    hh[:, 2:258], OP.mult, OP.add),
                    after=[tev, cv_toks.get(m - 2)])
                tc2 = V(lambda e, hh=h_t, mm=m, ta=t1b:
                        e.scalar_tensor_tensor(
                    hh[:, 1:257], hh[:, 1:257], sb_r12[:, mm:mm + 1],
                    ta[:, 0:256], OP.mult, OP.add),
                    after=[tc1])
                cv_toks[m] = tc2
                ctx["conv_last"] = tc2
                tsl = SC(lambda e, hh=h_t, mm=m: e.activation(
                    hh[:, 1:257], hh[:, 1:257], AF.Silu,
                    bias=sb_cb2[:, mm:mm + 1],
                    scale=sb_w2t[:, mm:mm + 1]), after=[tc2])
                h_toks[m] = tsl
                ctx["silu_last"] = tsl
                ctx["evac_last"] = evtok[("h", m)]
            pend = gen_A(tc + 1) if tc + 1 < min(NT, ntc) else None
            drive(pend, 6)
            hv = bass.AP(A_h.tensor, A_h.offset + 1,
                         [[A_h.ap[0][0], 128], [1, 256], [TW, HC]])
            hdep = [h_toks[m] for m in range(HC)]
            tmabs = V(lambda e: e.tensor_reduce(
                A_eq, hv, AX.X, OP.max,
                apply_absolute_value=True), after=hdep)
            dep = tmabs
            for blk in range(2):
                tt = T(lambda e, b=blk: e.transpose(
                    PB[:, 1280:1280 + 128],
                    A_eq[:, 128 * b:128 * b + 128],
                    ident), after=[dep, pb_last[2], t_consts])
                pb_last[2] = V(lambda e, b=blk: e.tensor_reduce(
                    cols[:, 32 + b:33 + b:1], PB[:, 1280:1280 + 128],
                    AX.X, OP.max), after=[tt])
                dep = pb_last[2]
            t_mh = dep
            CW2 = 2
            ra2 = cols[:, 34:34 + CW2]; q2_c = cols[:, 36:36 + CW2]
            o2_c = cols[:, 38:38 + CW2]
            tv_ = V(lambda e: e.tensor_scalar(ra2, cols[:, 32:32 + CW2], 1e-5,
                                              None, OP.max), after=[t_mh])
            tq2 = V(lambda e: e.reciprocal(q2_c, ra2), after=[tv_])
            tq2 = V(lambda e: e.tensor_scalar(q2_c, q2_c, 127.0, None,
                                              OP.mult), after=[tq2])
            to2 = V(lambda e: e.tensor_scalar(o2_c, ra2, sb_mw[:, 3:4],
                                              None, OP.mult),
                    after=[tq2, t_w2q])
            par = tc % 2
            drive(pend, 2)
            tO2 = bcast_multi([(A_r3, q2_c),
                               (A_r4B[:, 256 * par:256 * par + 256], o2_c)],
                              A_bt, [tq2, to2, t_w2d])
            drive(pend, 2)
            n2_toks = []
            tv_ = tO2
            for m in range(HC):
                h_t = A_h[:, m * TW:(m + 1) * TW]
                tv_ = V(lambda e, hh=h_t: e.tensor_tensor(
                    A_qt, hh[:, 1:257], A_r3, OP.mult),
                    after=[tO2, tv_])
                tv_ = V(lambda e, o_=A_n2B[m][:, 256 * par:256 * par + 256]:
                        e.tensor_scalar(
                    o_, A_qt, kc[:, 6:7], kc[:, 6:7],
                    OP.add, OP.subtract), after=[tv_, t_ceff, t_w2d])
                n2_toks.append(tv_)
                n2hist[(tc, m)] = tv_
                if m % 4 == 3:
                    drive(pend, 1)
            if par == 1:
                for mo in range(2):
                    wsrc = bass.AP(w2q_d.tensor, w2q_d.offset + 128 * mo,
                                   [[w2q_d.ap[0][0], 128], [D, HC], [1, 128]])
                    Y(lambda e, s=A_wr[mo % 2], ws=wsrc: e.dma_start(
                        out=s.rearrange("p (k j) -> p k j", k=HC), in_=ws),
                      after=[t_w2d, wr_used.get(mo - 2)], sem=sWR[mo % 2])
                st_e = st_o = None
                res_prev = None
                for mo in range(DC):
                    pbb = mo % 2
                    deps = [evtok.get(("o", mo - 2))]
                    wtok = (sWR[mo % 2], sWR[mo % 2].n)
                    for kcb in range(HC):
                        tm = T(lambda e, k=kcb, b=pbb, mm=mo, f=(kcb == 0),
                               l=(kcb == HC - 1):
                               e.matmul(PB[:, 512 * b:512 * b + 512],
                                        A_wr[mm % 2][:, 128 * k:128 * k + 128],
                                        A_n2B[k], start=f, stop=l),
                               after=(deps + [wtok, n2_toks[kcb],
                                              n2hist[(tc - 1, kcb)]])
                               if kcb == 0 else [n2_toks[kcb]],
                               inc=(kcb == HC - 1))
                    wr_used[mo] = tm
                    if mo + 2 < DC:
                        wsrc = bass.AP(w2q_d.tensor,
                                       w2q_d.offset + 128 * (mo + 2),
                                       [[w2q_d.ap[0][0], 128], [D, HC],
                                        [1, 128]])
                        Y(lambda e, s=A_wr[mo % 2], ws=wsrc: e.dma_start(
                            out=s.rearrange("p (k j) -> p k j", k=HC),
                            in_=ws),
                          after=[tm, wr_used.get(mo)], sem=sWR[mo % 2])
                    x_e = A_xw2[(tc - 1) % 3][:, mo * TW + 1:mo * TW + 257]
                    x_o = A_xw2[tc % 3][:, mo * TW + 1:mo * TW + 257]
                    tev = V(lambda e, b=pbb: e.tensor_tensor(
                        A_qe, PB[:, 512 * b:512 * b + 512],
                        A_r4B, OP.mult), after=[tm, tO2, res_prev])
                    evtok[("o", mo)] = tev
                    tev = V(lambda e, xx=x_e: e.tensor_tensor(
                        xx, A_qe[:, 0:256], xx, OP.add), after=[tev])
                    tev = V(lambda e, xx=x_o: e.tensor_tensor(
                        xx, A_qe[:, 256:512], xx, OP.add), after=[tev])
                    res_prev = tev
                    st_e = Y(lambda e, xx=x_e,
                             d=dst_dram[128 * mo:128 * mo + 128,
                                        TST * (tc - 1):TST * (tc - 1) + 256]:
                             e.dma_start(out=d, in_=xx), after=[tev],
                             sem=sS2[(tc - 1) % 2])
                    st_o = Y(lambda e, xx=x_o,
                             d=dst_dram[128 * mo:128 * mo + 128,
                                        TST * tc:TST * tc + 256]:
                             e.dma_start(out=d, in_=xx), after=[tev],
                             sem=sS2[tc % 2])
                    drive(pend, 6)
                st_hist[tc - 1] = st_e
                st_hist[tc] = st_o

    def attn_phase(src_dram, dst_dram):
        prev_st = None
        xn_done = None
        for qc in range(4):
            t0 = 512 * qc
            xl = None
            for dcc in range(DC):
                xl = Y(lambda e, d=B_x1[:, dcc * 512:dcc * 512 + 512],
                       s=src_dram[128 * dcc:128 * dcc + 128, t0:t0 + 512]:
                       e.dma_start(out=d, in_=s), after=[prev_st])
            t_xl = xl
            mm_s = []
            for dcc in range(DC):
                x_t = B_x1[:, dcc * 512:dcc * 512 + 512]
                x2t = B_gb[dcc % 2]
                xbt = B_gxb[dcc % 2]
                dep = [t_xl, xn_done]
                if dcc >= 2:
                    dep.append(mm_s[dcc - 2])
                tcst = SC(lambda e, o_=xbt, i_=x_t: e.activation(
                    o_, i_, AF.Identity), after=dep)
                tq = SC(lambda e, o_=x2t, i_=x_t: e.activation(o_, i_,
                                                               AF.Square),
                        after=[tcst])
                T(lambda e, i_=xbt, f=(dcc == 0), l=(dcc == DC - 1):
                  e.matmul(PA[0:1, 1024:1536], onesc16, i_, start=f, stop=l),
                  after=[tcst, t_consts], inc=False)
                tm2 = T(lambda e, i_=x2t, f=(dcc == 0), l=(dcc == DC - 1):
                        e.matmul(PA[0:1, 1536:2048], onesc16, i_,
                                 start=f, stop=l),
                        after=[tq], inc=True)
                mm_s.append(tm2)
            tv_ = V(lambda e: e.tensor_copy(rowb[0:1, 0:512],
                                            PA[0:1, 1024:1536]),
                    after=[mm_s[-1]])
            t_s1 = row_transposes(rowb, [0, 128, 256, 384], cols[:, 0:4],
                                  [tv_])
            tv_ = V(lambda e: e.tensor_copy(rowb[0:1, 0:512],
                                            PA[0:1, 1536:2048]),
                    after=[tv_, t_s1])
            t_scols = row_transposes(rowb, [0, 128, 256, 384], cols[:, 4:8],
                                     [tv_])
            CW = 4
            m_c = cols[:, 12:12 + CW]; inv_c = cols[:, 16:16 + CW]
            var_c = cols[:, 20:20 + CW]; tmpc = cols[:, 48:48 + CW]
            tv_ = V(lambda e: e.tensor_scalar(m_c, cols[:, 0:4], 1.0 / D,
                                              None, OP.mult), after=[t_scols])
            tv_ = V(lambda e: e.tensor_tensor(var_c, m_c, m_c, OP.mult),
                    after=[tv_])
            tv_ = V(lambda e: e.tensor_scalar(tmpc, cols[:, 4:8], 1.0 / D,
                                              None, OP.mult), after=[tv_])
            tv_ = V(lambda e: e.tensor_tensor(var_c, tmpc, var_c, OP.subtract),
                    after=[tv_])
            tv_ = V(lambda e: e.tensor_scalar(var_c, var_c, 1e-5, None,
                                              OP.add), after=[tv_])
            tinv = rsqrt_rows(inv_c, var_c, tmpc, [tv_])
            tBm = bcast_row(B_rows[:, 0:512], m_c, 4, B_bt, [tinv])
            tBi = bcast_row(B_rows[:, 512:1024], inv_c, 4, B_bt, [tinv, tBm])
            mmq = []
            tv_ = tBi
            r_st = None
            for dcc in range(DC):
                x_t = B_x1[:, dcc * 512:dcc * 512 + 512]
                xl_t = B_xl[:, dcc * 512:dcc * 512 + 512]
                tv_ = V(lambda e, xx=x_t, o_=xl_t: e.tensor_tensor(
                    o_, xx, B_rows[:, 0:512], OP.subtract),
                    after=[tBm, t_xl, tv_])
                tv_ = V(lambda e, o_=xl_t: e.tensor_tensor(
                    o_, o_, B_rows[:, 512:1024], OP.mult), after=[tBi, tv_])
                tv_ = V(lambda e, xx=x_t, o_=xl_t: e.tensor_tensor(
                    xx, xx, o_, OP.add), after=[tv_])
                r_st = Y(lambda e, xx=x_t,
                         d=rs[128 * dcc:128 * dcc + 128, t0:t0 + 512]:
                         e.dma_start(out=d, in_=xx), after=[tv_], sem=sDS)
                x2t = B_gb[dcc % 2]
                dep = [tv_]
                if dcc >= 2:
                    dep.append(mmq[dcc - 2])
                tq = SC(lambda e, o_=x2t, i_=xl_t: e.activation(
                    o_, i_, AF.Square), after=dep)
                tm2 = T(lambda e, i_=x2t, f=(dcc == 0), l=(dcc == DC - 1):
                        e.matmul(PA[0:1, 1024:1536], onesc16, i_,
                                 start=f, stop=l),
                        after=[tq], inc=True)
                mmq.append(tm2)
            prev_st = r_st
            tv_ = V(lambda e: e.tensor_copy(rowb[0:1, 0:512],
                                            PA[0:1, 1024:1536]),
                    after=[mmq[-1]])
            t_rms = row_transposes(rowb, [0, 128, 256, 384], cols[:, 0:4],
                                   [tv_])
            tv_ = V(lambda e: e.tensor_scalar(m_c, cols[:, 0:4], 1.0 / D,
                                              1e-6, OP.mult, OP.add),
                    after=[t_rms])
            tinv2 = rsqrt_rows(inv_c, m_c, tmpc, [tv_])
            tBr = bcast_row(B_rows[:, 1024:1536], inv_c, 4, B_bt,
                            [tinv2, tBi])
            tv_ = tBr
            for dcc in range(DC):
                xl_t = B_xl[:, dcc * 512:dcc * 512 + 512]
                xn_t = B_xn[:, dcc * S + t0:dcc * S + t0 + 512]
                tv_ = V(lambda e, i_=xl_t: e.tensor_tensor(
                    B_g[:, 0:512], i_, B_rows[:, 1024:1536], OP.mult),
                    after=[tBr, tv_])
                tv_ = V(lambda e, o_=xn_t, dd=dcc: e.tensor_scalar(
                    o_, B_g[:, 0:512], sb_nw[:, dd:dd + 1], None, OP.mult),
                    after=[tv_])
            xn_done = tv_
        t_rs_done = sDS.tok
        t_xn = xn_done
        if dbg:
            t_xn = Y(lambda e: e.dma_start(out=dbg_xn, in_=B_xn),
                     after=[xn_done], sem=sDW)

        ig_ld = {}
        u_toks = {}
        for m in range(IC):
            w_in, w_g = B_igs[(m % 2) * 2], B_igs[(m % 2) * 2 + 1]
            dep = [u_toks.get((m - 2, 1)), t_xn]
            tl = GD(lambda e, d=w_in, s=in_ws[m]: e.dma_start(out=d, in_=s),
                    after=dep, sem=sG2[m % 2])
            tl = GD(lambda e, d=w_g, s=gate_ws[m]: e.dma_start(out=d, in_=s),
                    sem=sG2[m % 2])
            ig_ld[m] = tl
            for half in range(2):
                pp = PA if half == 0 else PB
                hoff = 1024 * half
                deps = [tl, t_xn, u_toks.get((m - 1, half))]
                for kcb in range(DC):
                    for ncb in range(2):
                        T(lambda e, k=kcb, nn=ncb, w=w_in, po=hoff, pq=pp,
                          f=(kcb == 0), l=(kcb == DC - 1):
                          e.matmul(pq[:, 512 * nn:512 * nn + 512],
                                   w[:, k * 128:(k + 1) * 128],
                                   B_xn[:, k * S + po + 512 * nn:
                                        k * S + po + 512 * nn + 512],
                                   start=f, stop=l),
                          after=deps if (kcb == 0 and ncb == 0) else [],
                          inc=False)
                for kcb in range(DC):
                    for ncb in range(2):
                        tg_ = T(lambda e, k=kcb, nn=ncb, w=w_g, po=hoff, pq=pp,
                                f=(kcb == 0), l=(kcb == DC - 1):
                                e.matmul(pq[:, 1024 + 512 * nn:
                                            1024 + 512 * nn + 512],
                                         w[:, k * 128:(k + 1) * 128],
                                         B_xn[:, k * S + po + 512 * nn:
                                              k * S + po + 512 * nn + 512],
                                         start=f, stop=l),
                                after=[],
                                inc=(kcb == DC - 1 and ncb == 1))
                gbuf = B_g[:, 1024 * half:1024 * half + 1024]
                tsg = SC(lambda e, g=gbuf, mm=m, pq=pp: e.activation(
                    g, pq[:, 1024:2048], AF.Sigmoid,
                    bias=sb_gateb[:, mm:mm + 1]), after=[tg_])
                ut = B_u[:, m * S + hoff:m * S + hoff + 1024]
                tu = V(lambda e, g=gbuf, o_=ut, mm=m, pq=pp:
                       e.scalar_tensor_tensor(o_, pq[:, 0:1024],
                                              sb_inb[:, mm:mm + 1], g,
                                              OP.add, OP.mult),
                       after=[tsg, t_rs_done])
                u_toks[(m, half)] = tu
        t_u = sV.tok

        bcd_ld = {}
        y_toks = {}
        scan_tok = None
        for m in range(IC):
            sb3 = [B_bcds[(m % 2) * 3 + i] for i in range(3)]
            dep = [(y_toks[m - 2][0] if m >= 2 else None), t_u]
            for i, ws in enumerate([b_ws, c_ws, d_ws]):
                tl = GD(lambda e, d=sb3[i], s=ws[m]: e.dma_start(out=d, in_=s),
                        after=dep if i == 0 else [], sem=sB2[m % 2])
            bcd_ld[m] = tl
            depb = [tl, t_u, (y_toks[m - 1][0] if m >= 1 else None)]
            for nb in range(4):
                for kcb in range(IC):
                    T(lambda e, k=kcb, nn=nb, w=sb3[0], f=(kcb == 0):
                      e.matmul(PA[:, 512 * nn:512 * nn + 512],
                               w[:, k * 128:(k + 1) * 128],
                               B_u[:, k * S + 512 * nn:k * S + 512 * nn + 512],
                               start=f, stop=False),
                      after=depb if (nb == 0 and kcb == 0) else [], inc=False)
                tb = T(lambda e, nn=nb, mm=m:
                       e.matmul(PA[:, 512 * nn:512 * nn + 512],
                                bbr16[0:1, 128 * mm:128 * mm + 128],
                                ones16[0:1, 0:512], start=False, stop=True),
                       after=[t_bb], inc=(nb == 3))
            scan_dep = [tb, t_acol, scan_tok,
                        (y_toks[m - 1][1] if m >= 1 else None)]
            tsc_ = V(lambda e, mm=m: e.tensor_tensor_scan(
                B_hs, sb_acol[:, mm:mm + 1].broadcast_to([128, S]),
                PA[:, 0:S], 0.0, OP.mult, OP.add), after=scan_dep)
            scan_tok = tsc_
            depc = [tl, t_u, (y_toks[m - 1][1] if m >= 1 else None)]
            for nb in range(4):
                for kcb in range(IC):
                    tcm = T(lambda e, k=kcb, nn=nb, w=sb3[1], f=(kcb == 0),
                            l=(kcb == IC - 1):
                            e.matmul(PB[:, 512 * nn:512 * nn + 512],
                                     w[:, k * 128:(k + 1) * 128],
                                     B_u[:, k * S + 512 * nn:
                                          k * S + 512 * nn + 512],
                                     start=f, stop=l),
                            after=depc if (nb == 0 and kcb == 0) else [],
                            inc=(nb == 3 and kcb == IC - 1))
            ty0 = V(lambda e, mm=m: e.scalar_tensor_tensor(
                B_scr, PB[:, 0:S], sb_cb[:, mm:mm + 1], B_hs,
                OP.add, OP.mult),
                after=[tcm, tsc_, (y_toks[m - 1][0] if m >= 1 else None)])
            for nb in range(4):
                for kcb in range(IC):
                    tdm = T(lambda e, k=kcb, nn=nb, w=sb3[2], f=(kcb == 0),
                            l=(kcb == IC - 1):
                            e.matmul(PA[:, 512 * nn:512 * nn + 512],
                                     w[:, k * 128:(k + 1) * 128],
                                     B_u[:, k * S + 512 * nn:
                                          k * S + 512 * nn + 512],
                                     start=f, stop=l),
                            after=[tsc_] if (nb == 0 and kcb == 0) else [],
                            inc=(nb == 3 and kcb == IC - 1))
            y_t = B_y[:, m * S:(m + 1) * S]
            ty1 = V(lambda e, mm=m, o_=y_t: e.scalar_tensor_tensor(
                o_, PA[:, 0:S], sb_db[:, mm:mm + 1], B_scr, OP.add, OP.add),
                after=[tdm, ty0])
            y_toks[m] = (ty1, ty0)
        t_y = sV.tok

        out_ld = {}
        st_toks = {}
        ev_toks = {}
        for mo in range(DC):
            osl = B_outs[mo % 2]
            rb = B_rr[mo % 2]
            tl = GD(lambda e, d=osl, s=out_ws[mo]: e.dma_start(out=d, in_=s),
                    after=[ev_toks.get(mo - 2), t_u], sem=sO2[mo % 2])
            out_ld[mo] = tl
            trl = Y(lambda e, d=rb, s=rs[128 * mo:128 * mo + 128, :]:
                    e.dma_start(out=d, in_=s),
                    after=[st_toks.get(mo - 2), t_y], sem=sR2[mo % 2])
            pp = PA if mo % 2 == 0 else PB
            deps = [tl, t_y, ev_toks.get(mo - 2)]
            for nb in range(4):
                for kcb in range(IC):
                    tm = T(lambda e, k=kcb, nn=nb, w=osl, pq=pp, f=(kcb == 0),
                           l=(kcb == IC - 1):
                           e.matmul(pq[:, 512 * nn:512 * nn + 512],
                                    w[:, k * 128:(k + 1) * 128],
                                    B_y[:, k * S + 512 * nn:
                                         k * S + 512 * nn + 512],
                                    start=f, stop=l),
                           after=deps if (nb == 0 and kcb == 0) else [],
                           inc=(nb == 3 and kcb == IC - 1))
            tev = V(lambda e, mm=mo, r=rb, pq=pp: e.scalar_tensor_tensor(
                r, pq[:, 0:S], sb_outb[:, mm:mm + 1], r, OP.add, OP.add),
                after=[tm, trl])
            ev_toks[mo] = tev
            st_toks[mo] = Y(lambda e, r=rb,
                            d=dst_dram[128 * mo:128 * mo + 128, :]:
                            e.dma_start(out=d, in_=r), after=[tev],
                            sem=sS2[mo % 2])

    if phases == 1:
        ffn_phase(xT, outT, quantize=True)
    elif phases == 2:
        ffn_phase(xT, xs1, quantize=True)
        p.barrier(ALLS)
        attn_phase(xs1, outT)
    else:
        ffn_phase(xT, xs1, quantize=True)
        p.barrier(ALLS)
        attn_phase(xs1, xs2)
        p.barrier(ALLS)
        ffn_phase(xs2, outT, quantize=False)
    p.barrier(ALLS)

    with nc.Block() as block:
        def runner(eng_name):
            def run(e):
                for waits, fn, inc in p.items[eng_name]:
                    for (h, v) in waits:
                        e.wait_ge(h, v)
                    if fn is None:
                        continue
                    ins = fn(e)
                    if inc is not None:
                        ins.then_inc(inc[0], inc[1])
            return run
        block.sync(runner("sync"))
        block.vector(runner("vector"))
        block.scalar(runner("scalar"))
        block.tensor(runner("tensor"))
        block.gpsimd(runner("gpsimd"))
    return nc


def _prep_inputs(w):
    f = np.ascontiguousarray

    def slab(wm, mtiles, ktiles):
        return f(wm.reshape(mtiles, 128, ktiles, 128).transpose(0, 3, 2, 1)
                 .reshape(mtiles, 128, ktiles * 128).astype(np.float32))

    conv_w = w["conv_w"].reshape(H, 3)
    d = {
        "w1T": f(w["w1"].T.astype(np.float32)),
        "w2T": f(w["w2"].T.astype(np.float32)),
        "in_ws": slab(w["in_w"], IC, DC),
        "gate_ws": slab(w["gate_w"], IC, DC),
        "b_ws": slab(w["b_w"], IC, IC),
        "c_ws": slab(w["c_w"], IC, IC),
        "d_ws": slab(w["d_w"], IC, IC),
        "out_ws": slab(w["out_w"], DC, IC),
        "in_bc": f(w["in_b"].reshape(IC, 128).T.astype(np.float32)),
        "gate_bc": f(w["gate_b"].reshape(IC, 128).T.astype(np.float32)),
        "b_br": f(w["b_b"].reshape(1, INNER).astype(np.float32)),
        "c_bc": f(w["c_b"].reshape(IC, 128).T.astype(np.float32)),
        "d_bc": f(w["d_b"].reshape(IC, 128).T.astype(np.float32)),
        "out_bc": f(w["out_b"].reshape(DC, 128).T.astype(np.float32)),
        "nw_c": f(w["norm_w"].reshape(DC, 128).T.astype(np.float32)),
        "alog_c": f(w["a_log"].reshape(IC, 128).T.astype(np.float32)),
        "tap0_c": f(conv_w[:, 0].reshape(HC, 128).T.astype(np.float32)),
        "tap1_c": f(conv_w[:, 1].reshape(HC, 128).T.astype(np.float32)),
        "tap2_c": f(conv_w[:, 2].reshape(HC, 128).T.astype(np.float32)),
        "convb_c": f(w["conv_b"].reshape(HC, 128).T.astype(np.float32)),
        "ident_d": np.eye(128, dtype=np.float32),
    }
    return d


def kernel(L0_attn_x, L0_attn_norm_w, L0_attn_in_w, L0_attn_in_b,
           L0_attn_gate_w, L0_attn_gate_b, L0_attn_b_w, L0_attn_b_b,
           L0_attn_c_w, L0_attn_c_b, L0_attn_d_w, L0_attn_d_b,
           L0_attn_out_w, L0_attn_out_b, L0_attn_a_log,
           L0_ffn_w1, L0_ffn_conv_w, L0_ffn_conv_b, L0_ffn_w2,
           _phases=3):
    x = np.asarray(L0_attn_x, dtype=np.float32)
    w = dict(norm_w=L0_attn_norm_w, in_w=L0_attn_in_w, in_b=L0_attn_in_b,
             gate_w=L0_attn_gate_w, gate_b=L0_attn_gate_b, b_w=L0_attn_b_w,
             b_b=L0_attn_b_b, c_w=L0_attn_c_w, c_b=L0_attn_c_b,
             d_w=L0_attn_d_w, d_b=L0_attn_d_b, out_w=L0_attn_out_w,
             out_b=L0_attn_out_b, a_log=L0_attn_a_log, w1=L0_ffn_w1,
             conv_w=L0_ffn_conv_w, conv_b=L0_ffn_conv_b, w2=L0_ffn_w2)
    w = {k: np.asarray(v) for k, v in w.items()}
    shared = _prep_inputs(w)
    nc = build(phases=_phases)
    in_maps = []
    for c in range(8):
        m = dict(shared)
        m["xT"] = np.ascontiguousarray(x[c].T)
        in_maps.append(m)
    res = run_bass_kernel_spmd(nc, in_maps, core_ids=list(range(8)))
    out = np.stack([np.asarray(r["outT"]).T for r in res.results], axis=0)
    return np.ascontiguousarray(out.astype(np.float32))


# version 17
# speedup vs baseline: 1.0084x; 1.0084x over previous
import numpy as np
import concourse.bass as bass
import concourse.mybir as mybir
from concourse.bass_utils import run_bass_kernel_spmd

F32 = mybir.dt.float32
BF16 = mybir.dt.bfloat16
U8 = mybir.dt.uint8
OP = mybir.AluOpType
AF = mybir.ActivationFunctionType
AX = mybir.AxisListType.X

RC = 12582912.0
WCLIP = 1.4999999
C52 = 6755399441055744.0
X52 = C52 - RC
D, S, INNER, H = 1024, 2048, 2048, 4096
DC, IC, HC = D // 128, INNER // 128, H // 128
TW, TST, NT = 258, 256, 8

ENGINES = ("sync", "vector", "scalar", "tensor", "gpsimd")


class Sem:
    def __init__(self, nc, name):
        self.h = nc.alloc_semaphore(name)
        self.n = 0

    @property
    def tok(self):
        return (self, self.n)


class Prog:

    def __init__(self, nc):
        self.nc = nc
        self.items = {e: [] for e in ENGINES}
        self.waited = {e: {} for e in ENGINES}

    def emit(self, eng, fn, after=(), inc=None, inck=1):
        waits = []
        wm = self.waited[eng]
        for t in after:
            if t is None:
                continue
            s, v = t
            if v <= 0:
                continue
            if v > wm.get(s, 0):
                waits.append((s.h, v))
                wm[s] = v
        tok = None
        pair = None
        if inc is not None:
            inc.n += inck
            tok = (inc, inc.n)
            pair = (inc.h, inck)
        self.items[eng].append((waits, fn, pair))
        return tok

    def barrier(self, sems):
        toks = [s.tok for s in sems]
        for e in ENGINES:
            for s, v in toks:
                if v > 0 and v > self.waited[e].get(s, 0):
                    self.items[e].append(([(s.h, v)], None, None))
                    self.waited[e][s] = v


def build(phases=3, dbg=False, ntc=None):
    import os
    if ntc is None:
        ntc = int(os.environ.get("K_NTC", "99"))
    kstage = int(os.environ.get("K_STAGE", "99"))
    nc = bass.Bass("TRN2", target_bir_lowering=False)
    p = Prog(nc)

    dt_in = {}

    def din(name, shape, dtype=F32):
        dt_in[name] = nc.dram_tensor(name, list(shape), dtype,
                                     kind="ExternalInput").ap()
        return dt_in[name]

    xT = din("xT", [D, S])
    w1T = din("w1T", [D, H])
    w2T = din("w2T", [H, D])
    in_ws = din("in_ws", [IC, 128, DC * 128])
    gate_ws = din("gate_ws", [IC, 128, DC * 128])
    b_ws = din("b_ws", [IC, 128, IC * 128])
    c_ws = din("c_ws", [IC, 128, IC * 128])
    d_ws = din("d_ws", [IC, 128, IC * 128])
    out_ws = din("out_ws", [DC, 128, IC * 128])
    in_bc = din("in_bc", [128, IC])
    gate_bc = din("gate_bc", [128, IC])
    b_br = din("b_br", [1, INNER])
    c_bc = din("c_bc", [128, IC])
    d_bc = din("d_bc", [128, IC])
    out_bc = din("out_bc", [128, DC])
    nw_c = din("nw_c", [128, DC])
    alog_c = din("alog_c", [128, IC])
    tap0_c = din("tap0_c", [128, HC])
    tap1_c = din("tap1_c", [128, HC])
    tap2_c = din("tap2_c", [128, HC])
    convb_c = din("convb_c", [128, HC])
    ident_d = din("ident_d", [128, 128])

    outT = nc.dram_tensor("outT", [D, S], F32, kind="ExternalOutput").ap()
    dbg_h = nc.dram_tensor("dbg_h", [128, HC * TW], F32,
                           kind="ExternalOutput").ap() if dbg else None
    dbg_r = nc.dram_tensor("dbg_r", [128, 3 * TW], F32,
                           kind="ExternalOutput").ap() if dbg else None
    dbg_c = nc.dram_tensor("dbg_c", [128, 64], F32,
                           kind="ExternalOutput").ap() if dbg else None
    dbg_x2 = nc.dram_tensor("dbg_x2", [128, 2 * TW], F32,
                            kind="ExternalOutput").ap() if dbg else None
    dbg_xn = nc.dram_tensor("dbg_xn", [128, DC * S], BF16,
                            kind="ExternalOutput").ap() if dbg else None
    dbg_s2 = nc.dram_tensor("dbg_s2", [1, TW], F32,
                            kind="ExternalOutput").ap() if dbg else None
    xs1 = nc.dram_tensor("xs1", [D, S], F32, kind="Internal").ap()
    rs = nc.dram_tensor("rs", [D, S], F32, kind="Internal").ap()
    xs2 = nc.dram_tensor("xs2", [D, S], F32, kind="Internal").ap()
    w1q_d = nc.dram_tensor("w1q_d", [128, DC * H], BF16, kind="Internal").ap()
    w2q_d = nc.dram_tensor("w2q_d", [128, HC * D], BF16, kind="Internal").ap()
    w2r_d = nc.dram_tensor("w2r_d", [128, DC * 4096], BF16,
                           kind="Internal").ap()

    ident = nc.alloc_sbuf_tensor("ident", [128, 128], F32).ap()
    ones128 = nc.alloc_sbuf_tensor("ones128", [128, 128], F32).ap()
    onesc = nc.alloc_sbuf_tensor("onesc", [128, 1], F32).ap()
    onesc16 = nc.alloc_sbuf_tensor("onesc16", [128, 1], BF16).ap()
    ones16 = nc.alloc_sbuf_tensor("ones16", [1, 512], BF16).ap()
    bbr16 = nc.alloc_sbuf_tensor("bbr16", [1, INNER], BF16).ap()
    kc = nc.alloc_sbuf_tensor("kc", [128, 8], F32).ap()
    sb_inb = nc.alloc_sbuf_tensor("sb_inb", [128, IC], F32).ap()
    sb_gateb = nc.alloc_sbuf_tensor("sb_gateb", [128, IC], F32).ap()
    sb_cb = nc.alloc_sbuf_tensor("sb_cb", [128, IC], F32).ap()
    sb_db = nc.alloc_sbuf_tensor("sb_db", [128, IC], F32).ap()
    sb_outb = nc.alloc_sbuf_tensor("sb_outb", [128, DC], F32).ap()
    sb_nw = nc.alloc_sbuf_tensor("sb_nw", [128, DC], F32).ap()
    sb_alog = nc.alloc_sbuf_tensor("sb_alog", [128, IC], F32).ap()
    sb_acol = nc.alloc_sbuf_tensor("sb_acol", [128, IC], F32).ap()
    sb_t0 = nc.alloc_sbuf_tensor("sb_t0", [128, HC], F32).ap()
    sb_t1 = nc.alloc_sbuf_tensor("sb_t1", [128, HC], F32).ap()
    sb_w2t = nc.alloc_sbuf_tensor("sb_w2t", [128, HC], F32).ap()
    sb_cb2 = nc.alloc_sbuf_tensor("sb_cb2", [128, HC], F32).ap()
    sb_r02 = nc.alloc_sbuf_tensor("sb_r02", [128, HC], F32).ap()
    sb_r12 = nc.alloc_sbuf_tensor("sb_r12", [128, HC], F32).ap()
    sb_mw = nc.alloc_sbuf_tensor("sb_mw", [128, 8], F32).ap()
    cols = nc.alloc_sbuf_tensor("cols", [128, 64], F32).ap()
    rowb = nc.alloc_sbuf_tensor("rowb", [1, 512], F32).ap()

    AR = 198 * 1024
    backing = nc.alloc_sbuf_tensor("arena", [128, AR], U8)
    base = nc.lookup_mloc(backing).addr

    def place(name, shape, dtype, off):
        per = int(np.prod(shape[1:])) * mybir.dt.size(dtype)
        assert off >= 0 and off + per <= AR, f"{name}: {off}+{per} > {AR}"
        h = nc.virtual_sbuf_tensor(name, list(shape), dtype)
        nc._tensor(h.name, h.shape, h.dtype, type="SB")
        mloc = nc.lookup_mloc(h)
        mloc.allocated = True
        mloc.addr = base + off
        return h.ap()

    o = 0
    A_w1q = place("A_w1q", [128, DC * H], BF16, o); o += DC * H * 2
    o_w2q = o
    A_w2q = place("A_w2q", [128, HC * D], BF16, o); o += HC * D * 2
    A_n2B = [place(f"A_n2B_{k}", [128, 512], BF16, o_w2q + k * 1024)
             for k in range(HC)]
    A_wr = [place(f"A_wr_{i}", [128, 4096], BF16,
                  o_w2q + HC * 1024 + i * 8192) for i in range(2)]
    A_r4B = place("A_r4B", [128, 512], F32, o_w2q + HC * 1024 + 16384)
    o_xw3 = o_w2q + HC * 1024 + 16384 + 2048
    A_xw2 = [place(f"A_xw{i}", [128, DC * TW], F32, o + i * DC * TW * 4)
             for i in range(2)]
    o += 2 * DC * TW * 4
    A_xw2.append(place("A_xw2b", [128, DC * TW], F32, o_xw3))
    assert o_xw3 + DC * TW * 4 <= o_w2q + HC * D * 2
    o_h = o
    A_h = place("A_h", [128, HC * TW], F32, o); o += HC * TW * 4
    A_n2 = [place(f"A_n2_{m}", [128, 256], BF16, o_h + m * TW * 4)
            for m in range(HC)]
    A_wc = [place(f"A_wc{i}", [128, 1024], F32, o_h + i * 4096) for i in range(4)]
    A_n1 = place("A_n1", [128, DC * TW], BF16, o); o += DC * TW * 2
    o_qt = o
    A_qt = place("A_qt", [128, 256], F32, o); o += 1024
    A_eq = place("A_eq", [128, 256], F32, o); o += 1024
    A_qe = place("A_qe", [128, 512], F32, o_qt)
    o_t1 = o
    A_t1 = [place(f"A_t1_{i}", [128, TW], F32, o + i * TW * 4) for i in range(2)]
    o += 2 * TW * 4
    o_t2 = o
    A_t2 = [place(f"A_t2_{i}", [128, TW], F32, o + i * TW * 4) for i in range(2)]
    o += 2 * TW * 4
    A_t1b = [place(f"A_t1b_{i}", [128, TW], BF16, o_t1 + i * TW * 4)
             for i in range(2)]
    A_t2b = [place(f"A_t2b_{i}", [128, TW], BF16, o_t2 + i * TW * 2)
             for i in range(2)]
    A_mx = place("A_mx", [128, TW], F32, o); o += TW * 4
    A_mn = place("A_mn", [128, TW], F32, o); o += TW * 4
    A_r0 = place("A_r0", [128, TW], F32, o); o += TW * 4
    A_r1 = place("A_r1", [128, TW], F32, o); o += TW * 4
    A_r2 = place("A_r2", [128, TW], F32, o); o += TW * 4
    A_bt = place("A_bt", [128, 9 * 128], F32, o)
    o += 9 * 128 * 4
    A_r3 = place("A_r3", [128, 256], F32, o); o += 1024
    A_r4 = place("A_r4", [128, 256], F32, o); o += 1024
    assert o <= AR, o

    o = 0
    o_xn = o
    B_xn = place("B_xn", [128, DC * S], BF16, o); o += DC * S * 2
    o_u = o
    B_u = place("B_u", [128, IC * S], BF16, o); o += IC * S * 2
    o_y = o
    B_y = place("B_y", [128, IC * S], BF16, o); o += IC * S * 2
    B_hs = place("B_hs", [128, S], F32, o); o += S * 4
    B_scr = place("B_scr", [128, S], F32, o); o += S * 4
    o_bg = o
    B_g = place("B_g", [128, 2 * 1024], F32, o); o += 2 * 1024 * 4
    B_gb = [place(f"B_gb_{i}", [128, 512], BF16, o_bg + i * 1024)
            for i in range(2)]
    B_gxb = [place(f"B_gxb_{i}", [128, 512], BF16, o_bg + 2048 + i * 1024)
             for i in range(2)]
    assert o <= AR, o
    B_x1 = place("B_x1", [128, DC * 512], F32, o_u)
    B_xl = place("B_xl", [128, DC * 512], F32, o_u + DC * 512 * 4)
    B_rows = place("B_rows", [128, 3 * 512], F32, o_y)
    B_bt = place("B_bt", [128, 4 * 128], F32, o_y + 3 * 512 * 4)
    B_igs = [place(f"B_igs{i}", [128, DC * 128], BF16,
                   o_y + 3 * 512 * 4 + 4 * 128 * 4 + i * DC * 256)
             for i in range(4)]
    B_bcds = [place(f"B_bcds{i}", [128, IC * 128], BF16, o_xn + i * IC * 256)
              for i in range(6)]
    B_outs = [place(f"B_outs{i}", [128, IC * 128], BF16,
                    o_xn + (6 + i) * IC * 256)
              for i in range(2)]
    B_rr = [place(f"B_rr{i}", [128, S], F32, o_u + i * S * 4) for i in range(2)]

    PA = nc.alloc_psum_tensor("PA", [128, 2048], F32).ap()
    PB = nc.alloc_psum_tensor("PB", [128, 2048], F32).ap()

    sDX = Sem(nc, "sDX")
    sDW = Sem(nc, "sDW")
    sDS = Sem(nc, "sDS")
    sDG = Sem(nc, "sDG")
    sTE = Sem(nc, "sTE")
    sV = Sem(nc, "sV")
    sSC = Sem(nc, "sSC")
    sGP = Sem(nc, "sGP")
    sW = [Sem(nc, f"sW{i}") for i in range(4)]
    sG2 = [Sem(nc, "sG0"), Sem(nc, "sG1")]
    sB2 = [Sem(nc, "sB0"), Sem(nc, "sB1")]
    sO2 = [Sem(nc, "sO0"), Sem(nc, "sO1")]
    sR2 = [Sem(nc, "sR0"), Sem(nc, "sR1")]
    sMISC = Sem(nc, "sMISC")
    sS2 = [Sem(nc, "sS0"), Sem(nc, "sS1")]
    sX2 = [Sem(nc, "sX0"), Sem(nc, "sX1"), Sem(nc, "sX2b")]
    sWR = [Sem(nc, "sWR0"), Sem(nc, "sWR1")]
    ALLS = [sDX, sDW, sDS, sDG, sTE, sV, sSC, sGP,
            sW[0], sW[1], sW[2], sW[3], sG2[0], sG2[1], sB2[0], sB2[1],
            sO2[0], sO2[1], sR2[0], sR2[1], sMISC, sS2[0], sS2[1],
            sX2[0], sX2[1], sX2[2], sWR[0], sWR[1]]

    def Y(fn, after=(), sem=None):
        return p.emit("sync", fn, after, inc=sem if sem else sDX, inck=16)

    def GD(fn, after=(), sem=None):
        return p.emit("gpsimd", fn, after, inc=sem if sem else sDG, inck=16)

    def SD(fn, after=(), sem=None):
        return p.emit("scalar", fn, after, inc=sem if sem else sDW, inck=16)

    def V(fn, after=()):
        return p.emit("vector", fn, after, inc=sV)

    def SC(fn, after=()):
        return p.emit("scalar", fn, after, inc=sSC)

    def T(fn, after=(), inc=True):
        return p.emit("tensor", fn, after, inc=sTE if inc else None)

    def G(fn, after=()):
        return p.emit("gpsimd", fn, after, inc=sGP)

    for name, dst in [("ident_d", ident), ("in_bc", sb_inb),
                      ("gate_bc", sb_gateb), ("c_bc", sb_cb), ("d_bc", sb_db),
                      ("out_bc", sb_outb), ("nw_c", sb_nw), ("alog_c", sb_alog),
                      ("tap0_c", sb_t0), ("tap1_c", sb_t1), ("tap2_c", sb_w2t),
                      ("convb_c", sb_cb2)]:
        src = dt_in[name]
        Y(lambda e, d=dst, s=src: e.dma_start(out=d, in_=s))
    t_consts = sDX.tok
    V(lambda e: e.memset(kc[:, 0:1], 1.0))
    V(lambda e: e.memset(kc[:, 1:2], -1.0))
    V(lambda e: e.memset(kc[:, 2:3], -0.5))
    V(lambda e: e.memset(kc[:, 3:4], 0.0))
    V(lambda e: e.memset(kc[:, 4:5], RC))
    V(lambda e: e.memset(kc[:, 5:6], -RC))
    tk_ = V(lambda e: e.memset(kc[:, 7:8], 1.0))
    tk_ = V(lambda e: e.tensor_scalar(kc[:, 7:8], kc[:, 7:8], C52, C52,
                                      OP.add, OP.subtract), after=[tk_])
    t_ceff = V(lambda e: e.tensor_scalar(kc[:, 6:7], kc[:, 7:8], X52, RC,
                                         OP.mult, OP.add), after=[tk_])
    V(lambda e: e.memset(onesc, 1.0))
    V(lambda e: e.memset(onesc16, 1.0))
    V(lambda e: e.memset(ones128, 1.0))
    V(lambda e: e.memset(ones16, 1.0))
    V(lambda e: e.memset(rowb, 0.0))
    V(lambda e: e.memset(cols, 0.0))
    t_bb = GD(lambda e: e.dma_start(out=bbr16, in_=dt_in["b_br"]), sem=sMISC)
    tv = V(lambda e: e.reciprocal(sb_r02, sb_w2t), after=[t_consts])
    tv2 = V(lambda e: e.tensor_mul(sb_r12, sb_t1, sb_r02), after=[tv])
    V(lambda e: e.tensor_mul(sb_r02, sb_t0, sb_r02), after=[tv2])
    c0, c1 = cols[:, 0:IC], cols[:, 16:32]
    tv = V(lambda e: e.tensor_scalar(c0, sb_alog, -1.0, None, OP.mult),
           after=[t_consts])
    tv = V(lambda e: e.tensor_tensor(c0, c0, sb_alog, OP.max), after=[tv])
    ts = SC(lambda e: e.activation(c1, c0, AF.Exp, scale=kc[:, 1:2]),
            after=[tv])
    ts = SC(lambda e: e.activation(c1, c1, AF.Ln, bias=kc[:, 0:1]), after=[ts])
    tv = V(lambda e: e.tensor_scalar(c0, sb_alog, 0.0, None, OP.max),
           after=[t_consts, tv, ts])
    tv = V(lambda e: e.tensor_tensor(c0, c0, c1, OP.add), after=[tv, ts])
    t_acol = SC(lambda e: e.activation(sb_acol, c0, AF.Exp, scale=kc[:, 1:2]),
                after=[tv])

    pb_last = {2: None, 3: None}
    def rsqrt_rows(dst, src, tmp, deps):
        t1 = V(lambda e: e.tensor_scalar(src, src, 1e-30, None, OP.max),
               after=deps)
        t1 = SC(lambda e: e.activation(tmp, src, AF.Ln), after=[t1])
        return SC(lambda e: e.activation(dst, tmp, AF.Exp, scale=kc[:, 2:3]),
                  after=[t1])

    def bcast_multi(jobs, bt, deps, gen_yield=False):
        it = _bcast_multi_gen(jobs, bt, deps)
        if gen_yield:
            return it
        last = None
        for last in it:
            pass
        return last

    def _bcast_multi_gen(jobs, bt, deps):
        blocks = []
        for dst, src_col in jobs:
            nblk = src_col.shape[1]
            for c_ in range(nblk):
                blocks.append((dst, src_col, c_, min(128, dst.shape[1] - 128 * c_)))
        nb = len(blocks)
        t0 = V(lambda e, bl=list(blocks): [
            None], after=deps) if False else None
        off = 0
        copy_toks = []
        offs = []
        tprev = None
        for dst, src_col in jobs:
            nblk = src_col.shape[1]
            tprev = V(lambda e, sc=src_col, o=off, nn=nblk: e.tensor_copy(
                bt[:, o:o + nn * 128].rearrange("p (c j) -> p c j", c=nn),
                sc.unsqueeze(2).broadcast_to([128, nn, 128])),
                after=deps + [tprev])
            offs.append(off)
            off += nblk * 128
        t0 = tprev
        assert off <= bt.shape[1], (off, bt.shape)
        i = 0
        last = None
        for ji, (dst, src_col) in enumerate(jobs):
            nblk = src_col.shape[1]
            for c_ in range(nblk):
                n = min(128, dst.shape[1] - 128 * c_)
                bk = 2 + (i % 2)
                pbase = 1024 + 512 * (bk - 2) + 256
                tt = T(lambda e, o=offs[ji] + 128 * c_, pb=pbase: e.transpose(
                    PB[:, pb:pb + 128], bt[:, o:o + 128], ident),
                    after=[t0, t_consts, pb_last[bk]])
                last = V(lambda e, c=c_, nn=n, d=dst, pb=pbase: e.tensor_copy(
                    d[:, 128 * c:128 * c + nn], PB[:, pb:pb + nn]),
                    after=[tt, last])
                pb_last[bk] = last
                i += 1
                yield last

    def bcast_row(dst, src_col, nblk, bt, deps, upto=99):
        return bcast_multi([(dst, src_col[:, 0:nblk])], bt, deps)

    def row_transposes(src_row, offs, dst_cols, deps):
        toks = []
        for j, so in enumerate(offs):
            tt = T(lambda e, jj=j, s=so: e.transpose(
                PB[:, 1024 + jj:1024 + jj + 1], src_row[0:1, s:s + 128],
                ident[0:1, 0:1]),
                after=deps + [t_consts, pb_last[2]])
            toks.append(tt)
        ev = V(lambda e, n=len(offs): e.tensor_copy(
            dst_cols, PB[:, 1024:1024 + n]), after=[toks[-1]])
        pb_last[2] = ev
        return ev

    def ffn_phase(src_dram, dst_dram, quantize):
        t_w1q = t_w2q = t_w2d = None

        def gen_Q(wi):
            nonlocal t_w1q, t_w2q, t_w2d
            wT, wq, wq_dr = [(w1T, A_w1q, w1q_d),
                             (w2T, A_w2q, w2q_d)][wi]
            nch = 32
            ncol = H if wi == 0 else D
            nq = ncol // 1024
            sumc = cols[:, 32:64]
            toks = []
            for i in range(nch):
                kcb, q = divmod(i, nq)
                src = wT[128 * kcb:128 * kcb + 128, 1024 * q:1024 * q + 1024]
                dep = [toks[i - 4][1]] if i >= 4 else [t_acol]
                tl = Y(lambda e, d=A_wc[i % 4], s=src:
                       e.dma_start(out=d, in_=s), after=dep, sem=sW[i % 4])
                tr = V(lambda e, d=A_wc[i % 4], c=sumc[:, i:i + 1]:
                       e.reduce_sum(c, d, AX, apply_absolute_value=True),
                       after=[tl])
                toks.append((tl, tr))
                yield
            tv_ = V(lambda e: e.reduce_sum(cols[:, 30:31], sumc, AX),
                    after=[toks[-1][1]])
            tm_ = T(lambda e: e.matmul(PB[:, 1024:1025], ones128,
                                       cols[:, 30:31],
                                       start=True, stop=True),
                    after=[tv_, t_consts, pb_last[2]])
            tv_ = V(lambda e: e.tensor_copy(cols[:, 31:32], PB[:, 1024:1025]),
                    after=[tm_])
            pb_last[2] = tv_
            tv_ = V(lambda e, nrm=1.0 / (1024 * 4096): e.tensor_scalar(
                cols[:, 30:31], cols[:, 31:32], nrm, 1e-5,
                OP.mult, OP.max), after=[tv_])
            tv_ = V(lambda e, m=wi: e.reciprocal(sb_mw[:, m:m + 1],
                                                 cols[:, 30:31]),
                    after=[tv_])
            if wi == 0:
                tsc = V(lambda e: e.tensor_scalar(
                    sb_mw[:, 2:3], cols[:, 30:31], 1.0 / 127, None,
                    OP.mult), after=[tv_])
            else:
                tsc = V(lambda e: e.tensor_scalar(
                    sb_mw[:, 3:4], cols[:, 30:31], 0.5 / 127, None,
                    OP.mult), after=[tv_])
            qtk = []
            for i in range(nch):
                kcb, q = divmod(i, nq)
                src = wT[128 * kcb:128 * kcb + 128, 1024 * q:1024 * q + 1024]
                dep = [qtk[i - 4]] if i >= 4 else [tsc]
                tl = Y(lambda e, d=A_wc[i % 4], s=src:
                       e.dma_start(out=d, in_=s), after=dep, sem=sW[i % 4])
                tq = V(lambda e, d=A_wc[i % 4], m=wi: e.tensor_scalar(
                    d, d, sb_mw[:, m:m + 1], WCLIP, OP.mult, OP.min),
                    after=[tl, tsc])
                tq = V(lambda e, d=A_wc[i % 4]: e.tensor_scalar(
                    d, d, -WCLIP, None, OP.max), after=[tq])
                tq = V(lambda e, d=A_wc[i % 4],
                       w=wq[:, 1024 * i:1024 * (i + 1)]:
                       e.tensor_scalar(w, d, kc[:, 6:7], kc[:, 6:7],
                                       OP.add, OP.subtract),
                       after=[tq, t_ceff])
                qtk.append(tq)
                yield
            if wi == 0:
                Y(lambda e, w=wq, dr=wq_dr: e.dma_start(out=dr, in_=w),
                  after=[qtk[-1]], sem=sDS)
                t_w1q = qtk[-1]
            else:
                td = None
                for mo_ in range(DC):
                    wsrc = bass.AP(wq.tensor, wq.offset + 128 * mo_,
                                   [[wq.ap[0][0], 128], [D, HC], [1, 128]])
                    td = Y(lambda e, ws=wsrc,
                           d=w2r_d[:, mo_ * 4096:(mo_ + 1) * 4096]
                           .rearrange("p (k j) -> p k j", k=HC):
                           e.dma_start(out=d, in_=ws),
                           after=[qtk[-1]], sem=sDS)
                t_w2q = qtk[-1]
                t_w2d = td

        evtok = {}
        st_hist = {}
        state = {}
        n2hist = {}
        wr_used = {}
        base_sS2 = [sS2[0].n, sS2[1].n]
        assert NT % 2 == 0
        ctx = {"rows_user": None, "conv_last": None, "silu_last": None,
               "evac_last": None}

        def gen_A(tc):
            st = state.setdefault(tc, {})
            A_xw = A_xw2[tc % 3]
            t0 = TST * tc - 1
            lo, hi = max(t0, 0), min(t0 + TW, S)
            woff = lo - t0
            xl = None
            for dcc in range(DC):
                xl = Y(lambda e,
                       d=A_xw[:, dcc * TW + woff:dcc * TW + woff + hi - lo],
                       s=src_dram[128 * dcc:128 * dcc + 128, lo:hi]:
                       e.dma_start(out=d, in_=s),
                       after=[st_hist.get(tc - 3), t_w2d],
                       sem=sX2[tc % 3])
            t_xl = xl
            if tc == 0:
                ap0 = bass.AP(A_xw.tensor, A_xw.offset,
                              [[A_xw.ap[0][0], 128], [TW, DC], [1, 1]])
                t_xl = V(lambda e, a=ap0: e.memset(a, 0.0), after=[t_xl])
            if tc == NT - 1:
                ap1 = bass.AP(A_xw.tensor, A_xw.offset + TW - 1,
                              [[A_xw.ap[0][0], 128], [TW, DC], [1, 1]])
                t_xl = V(lambda e, a=ap1: e.memset(a, 0.0), after=[t_xl])
            yield
            mm_s = []
            for dcc in range(DC):
                x_t = A_xw[:, dcc * TW:(dcc + 1) * TW]
                x2t = A_t1b[dcc % 2]
                xbt = A_t2b[dcc % 2]
                dep = [t_xl, ctx["conv_last"]]
                if dcc >= 2:
                    dep.append(mm_s[dcc - 2])
                tcst = SC(lambda e, o_=xbt, i_=x_t: e.activation(
                    o_, i_, AF.Identity), after=dep)
                tq = SC(lambda e, o_=x2t, i_=x_t: e.activation(o_, i_,
                                                               AF.Square),
                        after=[tcst])
                T(lambda e, i_=xbt, f=(dcc == 0), l=(dcc == DC - 1):
                  e.matmul(PA[0:1, 1024:1024 + TW], onesc16, i_,
                           start=f, stop=l),
                  after=[tcst, t_consts, ctx["rows_user"],
                         ctx["evac_last"]], inc=False)
                tm2 = T(lambda e, i_=x2t, f=(dcc == 0), l=(dcc == DC - 1):
                        e.matmul(PA[0:1, 1536:1536 + TW], onesc16, i_,
                                 start=f, stop=l),
                        after=[tq, ctx["evac_last"]], inc=True)
                mm_s.append(tm2)
                if dcc % 3 == 2:
                    yield
            tv_ = V(lambda e: e.tensor_copy(rowb[0:1, 0:TW],
                                            PA[0:1, 1024:1024 + TW]),
                    after=[mm_s[-1]])
            yield
            t_s1 = row_transposes(rowb, [0, 128, 256], cols[:, 0:3], [tv_])
            tv_ = V(lambda e: e.tensor_copy(rowb[0:1, 0:TW],
                                            PA[0:1, 1536:1536 + TW]),
                    after=[tv_, t_s1])
            ctx["rows_user"] = tv_
            yield
            t_scols = row_transposes(rowb, [0, 128, 256], cols[:, 3:6],
                                     [tv_])
            xv = bass.AP(A_xw.tensor, A_xw.offset,
                         [[A_xw.ap[0][0], 128], [1, TW], [TW, DC]])
            tmx = V(lambda e, v=xv: e.tensor_reduce(A_mx, v, AX.X, OP.max),
                    after=[t_xl])
            tmn = V(lambda e, v=xv: e.tensor_reduce(A_mn, v, AX.X, OP.min),
                    after=[t_xl, tmx])
            yield
            stat_toks = {}
            for sidx, (buf, ci, rop, dep0) in enumerate(
                    [(A_mx, 6, OP.max, tmx), (A_mn, 9, OP.min, tmn)]):
                for blk in range(3):
                    n = min(128, TW - 128 * blk)
                    tt = T(lambda e, b=blk, bf=buf, nn=n: e.transpose(
                        PB[0:nn, 1280:1280 + 128],
                        bf[:, 128 * b:128 * b + nn],
                        ident), after=[dep0, pb_last[2], t_consts])
                    pb_last[2] = V(lambda e, b=blk, cc=ci, nn=n, op=rop:
                                   e.tensor_reduce(
                                       cols[0:nn, cc + b:cc + b + 1],
                                       PB[0:nn, 1280:1280 + 128], AX, op),
                                   after=[tt])
                    if blk == 1:
                        yield
                stat_toks[sidx] = pb_last[2]
                yield
            t_mxc, t_mnc = stat_toks[0], stat_toks[1]
            CW = 3
            m_c = cols[:, 12:12 + CW]; inv_c = cols[:, 15:15 + CW]
            var_c = cols[:, 18:18 + CW]; ra_c = cols[:, 21:21 + CW]
            q1_c = cols[:, 24:24 + CW]; o1_c = cols[:, 27:27 + CW]
            mq_c = cols[:, 44:44 + CW]; tmpc = cols[:, 48:48 + CW]
            tv_ = V(lambda e: e.tensor_scalar(m_c, cols[:, 0:3], 1.0 / D,
                                              None, OP.mult),
                    after=[t_scols])
            tv_ = V(lambda e: e.tensor_tensor(var_c, m_c, m_c, OP.mult),
                    after=[tv_])
            tv_ = V(lambda e: e.tensor_scalar(tmpc, cols[:, 3:6], 1.0 / D,
                                              None, OP.mult), after=[tv_])
            tv_ = V(lambda e: e.tensor_tensor(var_c, tmpc, var_c,
                                              OP.subtract), after=[tv_])
            tv_ = V(lambda e: e.tensor_scalar(var_c, var_c, 1e-5, None,
                                              OP.add), after=[tv_])
            tinv = rsqrt_rows(inv_c, var_c, tmpc, [tv_])
            yield
            tv_ = V(lambda e: e.tensor_tensor(ra_c, cols[:, 6:9], m_c,
                                              OP.subtract),
                    after=[t_mxc, tinv])
            tv_ = V(lambda e: e.tensor_tensor(tmpc, m_c, cols[:, 9:12],
                                              OP.subtract),
                    after=[t_mnc, tv_])
            tv_ = V(lambda e: e.tensor_tensor(ra_c, ra_c, tmpc, OP.max),
                    after=[tv_])
            tv_ = V(lambda e: e.tensor_tensor(ra_c, ra_c, inv_c, OP.mult),
                    after=[tv_])
            tv_ = V(lambda e: e.tensor_scalar(ra_c, ra_c, 1e-5, None,
                                              OP.max), after=[tv_])
            tq1 = V(lambda e: e.reciprocal(q1_c, ra_c), after=[tv_])
            tq1 = V(lambda e: e.tensor_tensor(q1_c, q1_c, inv_c, OP.mult),
                    after=[tq1])
            tq1 = V(lambda e: e.tensor_scalar(q1_c, q1_c, 127.0, None,
                                              OP.mult), after=[tq1])
            to1 = V(lambda e: e.tensor_scalar(o1_c, ra_c, sb_mw[:, 2:3],
                                              None, OP.mult),
                    after=[tq1, t_w1q])
            tmq = V(lambda e: e.tensor_tensor(mq_c, m_c, q1_c, OP.mult),
                    after=[to1])
            yield
            tB = bcast_multi([(A_r0, q1_c), (A_r1, o1_c), (A_r2, mq_c)],
                             A_bt, [tq1, to1, tmq], gen_yield=True)
            tB0 = None
            for tB0 in tB:
                yield
            st["tB1"] = tB0
            tv_ = tB0
            n1_toks = []
            n1s = A_t2[1]
            for dcc in range(DC):
                x_t = A_xw[:, dcc * TW:(dcc + 1) * TW]
                n1t = A_n1[:, dcc * TW:(dcc + 1) * TW]
                tv_ = V(lambda e, i_=x_t: e.tensor_tensor(
                    n1s, i_, A_r0, OP.mult),
                    after=[tB0, t_xl, tv_, ctx["conv_last"],
                           ctx["silu_last"]])
                tv_ = V(lambda e: e.tensor_tensor(n1s, n1s, A_r2,
                                                  OP.subtract),
                        after=[tv_])
                tv_ = V(lambda e, o_=n1t: e.tensor_scalar(
                    o_, n1s, kc[:, 6:7], kc[:, 6:7], OP.add, OP.subtract),
                    after=[tv_, t_ceff])
                n1_toks.append(tv_)
                if dcc % 2 == 1:
                    yield
            st["n1"] = n1_toks
            st["xw"] = A_xw

        def drive(g, n):
            if g is None:
                return
            for _ in range(n):
                if next(g, "END") == "END":
                    return

        def claim(dep):
            tm_ = dep
            for k in range(HC):
                tm_ = V(lambda e, kk=k: e.memset(A_n2B[kk], 0.0),
                        after=[tm_])
            tm_ = V(lambda e: e.memset(A_r4B, 0.0), after=[tm_])
            return V(lambda e: e.memset(A_xw2[2], 0.0), after=[tm_])

        pend = None
        if quantize:
            for _ in gen_Q(0):
                pass
            gen0 = gen_A(0)
            nq2 = 0
            for _ in gen_Q(1):
                nq2 += 1
                if nq2 <= 32:
                    if nq2 % 3 == 0 and nq2 <= 30:
                        drive(gen0, 1)
                elif nq2 % 2 == 0:
                    drive(gen0, 1)
            t_w2d = claim(t_w2d)
        else:
            t_w1q = Y(lambda e: e.dma_start(out=A_w1q, in_=w1q_d),
                      sem=sW[0])
            t_w2d = claim(None)
            gen0 = gen_A(0)
        drive(gen0, 999)
        pend = None
        for tc in range(min(NT, ntc)):
            if tc > 0:
                drive(pend, 999)
            st = state[tc]
            A_xw = st["xw"]
            n1_toks = st["n1"]
            tB1 = st["tB1"]
            h_toks = {}
            cv_toks = {}
            for m in range(HC):
                pab = m % 4
                deps = [t_w1q, evtok.get(("h", m - 4))]
                if m in (2, 3):
                    deps.append(ctx["rows_user"])
                for kcb in range(DC):
                    tm = T(lambda e, k=kcb, mm=m, b=pab, f=(kcb == 0),
                           l=(kcb == DC - 1):
                           e.matmul(PA[:, 512 * b:512 * b + TW],
                                    A_w1q[:, k * H + 128 * mm:
                                          k * H + 128 * mm + 128],
                                    A_n1[:, k * TW:(k + 1) * TW],
                                    start=f, stop=l),
                           after=(deps + [n1_toks[kcb]]) if kcb == 0
                           else [n1_toks[kcb]],
                           inc=(kcb == DC - 1))
                h_t = A_h[:, m * TW:(m + 1) * TW]
                tev = V(lambda e, o_=h_t, b=pab: e.tensor_tensor(
                    o_, PA[:, 512 * b:512 * b + TW], A_r1, OP.mult),
                    after=[tm, tB1, t_w2q])
                evtok[("h", m)] = tev
                t1b = A_t1[m % 2]
                tc1 = V(lambda e, hh=h_t, mm=m, tb=t1b:
                        e.scalar_tensor_tensor(
                    tb[:, 0:256], hh[:, 0:256], sb_r02[:, mm:mm + 1],
                    hh[:, 2:258], OP.mult, OP.add),
                    after=[tev, cv_toks.get(m - 2)])
                tc2 = V(lambda e, hh=h_t, mm=m, ta=t1b:
                        e.scalar_tensor_tensor(
                    hh[:, 1:257], hh[:, 1:257], sb_r12[:, mm:mm + 1],
                    ta[:, 0:256], OP.mult, OP.add),
                    after=[tc1])
                cv_toks[m] = tc2
                ctx["conv_last"] = tc2
                tsl = SC(lambda e, hh=h_t, mm=m: e.activation(
                    hh[:, 1:257], hh[:, 1:257], AF.Silu,
                    bias=sb_cb2[:, mm:mm + 1],
                    scale=sb_w2t[:, mm:mm + 1]), after=[tc2])
                h_toks[m] = tsl
                ctx["silu_last"] = tsl
                ctx["evac_last"] = evtok[("h", m)]
            pend = gen_A(tc + 1) if tc + 1 < min(NT, ntc) else None
            drive(pend, 6)
            hv = bass.AP(A_h.tensor, A_h.offset + 1,
                         [[A_h.ap[0][0], 128], [1, 256], [TW, HC]])
            hdep = [h_toks[m] for m in range(HC)]
            tmabs = V(lambda e: e.tensor_reduce(
                A_eq, hv, AX.X, OP.max,
                apply_absolute_value=True), after=hdep)
            dep = tmabs
            for blk in range(2):
                tt = T(lambda e, b=blk: e.transpose(
                    PB[:, 1280:1280 + 128],
                    A_eq[:, 128 * b:128 * b + 128],
                    ident), after=[dep, pb_last[2], t_consts])
                pb_last[2] = V(lambda e, b=blk: e.tensor_reduce(
                    cols[:, 32 + b:33 + b:1], PB[:, 1280:1280 + 128],
                    AX.X, OP.max), after=[tt])
                dep = pb_last[2]
            t_mh = dep
            CW2 = 2
            ra2 = cols[:, 34:34 + CW2]; q2_c = cols[:, 36:36 + CW2]
            o2_c = cols[:, 38:38 + CW2]
            tv_ = V(lambda e: e.tensor_scalar(ra2, cols[:, 32:32 + CW2], 1e-5,
                                              None, OP.max), after=[t_mh])
            tq2 = V(lambda e: e.reciprocal(q2_c, ra2), after=[tv_])
            tq2 = V(lambda e: e.tensor_scalar(q2_c, q2_c, 127.0, None,
                                              OP.mult), after=[tq2])
            to2 = V(lambda e: e.tensor_scalar(o2_c, ra2, sb_mw[:, 3:4],
                                              None, OP.mult),
                    after=[tq2, t_w2q])
            par = tc % 2
            drive(pend, 2)
            tO2 = bcast_multi([(A_r3, q2_c),
                               (A_r4B[:, 256 * par:256 * par + 256], o2_c)],
                              A_bt, [tq2, to2, t_w2d])
            drive(pend, 2)
            n2_toks = []
            tv_ = tO2
            for m in range(HC):
                h_t = A_h[:, m * TW:(m + 1) * TW]
                tv_ = V(lambda e, hh=h_t: e.tensor_tensor(
                    A_qt, hh[:, 1:257], A_r3, OP.mult),
                    after=[tO2, tv_])
                tv_ = V(lambda e, o_=A_n2B[m][:, 256 * par:256 * par + 256]:
                        e.tensor_scalar(
                    o_, A_qt, kc[:, 6:7], kc[:, 6:7],
                    OP.add, OP.subtract), after=[tv_, t_ceff, t_w2d])
                n2_toks.append(tv_)
                n2hist[(tc, m)] = tv_
                if m % 4 == 3:
                    drive(pend, 1)
            if par == 1:
                for mo in range(2):
                    Y(lambda e, s=A_wr[mo % 2],
                      ws=w2r_d[:, mo * 4096:(mo + 1) * 4096]:
                      e.dma_start(out=s, in_=ws),
                      after=[t_w2d, wr_used.get(mo - 2)], sem=sWR[mo % 2])
                st_e = st_o = None
                res_prev = None
                for mo in range(DC):
                    pbb = mo % 2
                    deps = [evtok.get(("o", mo - 2))]
                    wtok = (sWR[mo % 2], sWR[mo % 2].n)
                    for kcb in range(HC):
                        tm = T(lambda e, k=kcb, b=pbb, mm=mo, f=(kcb == 0),
                               l=(kcb == HC - 1):
                               e.matmul(PB[:, 512 * b:512 * b + 512],
                                        A_wr[mm % 2][:, 128 * k:128 * k + 128],
                                        A_n2B[k], start=f, stop=l),
                               after=(deps + [wtok, n2_toks[kcb],
                                              n2hist[(tc - 1, kcb)]])
                               if kcb == 0 else [n2_toks[kcb]],
                               inc=(kcb == HC - 1))
                    wr_used[mo] = tm
                    if mo + 2 < DC:
                        Y(lambda e, s=A_wr[mo % 2],
                          ws=w2r_d[:, (mo + 2) * 4096:(mo + 3) * 4096]:
                          e.dma_start(out=s, in_=ws),
                          after=[tm, wr_used.get(mo)], sem=sWR[mo % 2])
                    x_e = A_xw2[(tc - 1) % 3][:, mo * TW + 1:mo * TW + 257]
                    x_o = A_xw2[tc % 3][:, mo * TW + 1:mo * TW + 257]
                    tev = V(lambda e, b=pbb: e.tensor_tensor(
                        A_qe, PB[:, 512 * b:512 * b + 512],
                        A_r4B, OP.mult), after=[tm, tO2, res_prev])
                    evtok[("o", mo)] = tev
                    tev = V(lambda e, xx=x_e: e.tensor_tensor(
                        xx, A_qe[:, 0:256], xx, OP.add), after=[tev])
                    tev = V(lambda e, xx=x_o: e.tensor_tensor(
                        xx, A_qe[:, 256:512], xx, OP.add), after=[tev])
                    res_prev = tev
                    st_e = Y(lambda e, xx=x_e,
                             d=dst_dram[128 * mo:128 * mo + 128,
                                        TST * (tc - 1):TST * (tc - 1) + 256]:
                             e.dma_start(out=d, in_=xx), after=[tev],
                             sem=sS2[(tc - 1) % 2])
                    st_o = Y(lambda e, xx=x_o,
                             d=dst_dram[128 * mo:128 * mo + 128,
                                        TST * tc:TST * tc + 256]:
                             e.dma_start(out=d, in_=xx), after=[tev],
                             sem=sS2[tc % 2])
                    drive(pend, 6)
                st_hist[tc - 1] = st_e
                st_hist[tc] = st_o

    def attn_phase(src_dram, dst_dram):
        prev_st = None
        xn_done = None
        for qc in range(4):
            t0 = 512 * qc
            xl = None
            for dcc in range(DC):
                xl = Y(lambda e, d=B_x1[:, dcc * 512:dcc * 512 + 512],
                       s=src_dram[128 * dcc:128 * dcc + 128, t0:t0 + 512]:
                       e.dma_start(out=d, in_=s), after=[prev_st])
            t_xl = xl
            mm_s = []
            for dcc in range(DC):
                x_t = B_x1[:, dcc * 512:dcc * 512 + 512]
                x2t = B_gb[dcc % 2]
                xbt = B_gxb[dcc % 2]
                dep = [t_xl, xn_done]
                if dcc >= 2:
                    dep.append(mm_s[dcc - 2])
                tcst = SC(lambda e, o_=xbt, i_=x_t: e.activation(
                    o_, i_, AF.Identity), after=dep)
                tq = SC(lambda e, o_=x2t, i_=x_t: e.activation(o_, i_,
                                                               AF.Square),
                        after=[tcst])
                T(lambda e, i_=xbt, f=(dcc == 0), l=(dcc == DC - 1):
                  e.matmul(PA[0:1, 1024:1536], onesc16, i_, start=f, stop=l),
                  after=[tcst, t_consts], inc=False)
                tm2 = T(lambda e, i_=x2t, f=(dcc == 0), l=(dcc == DC - 1):
                        e.matmul(PA[0:1, 1536:2048], onesc16, i_,
                                 start=f, stop=l),
                        after=[tq], inc=True)
                mm_s.append(tm2)
            tv_ = V(lambda e: e.tensor_copy(rowb[0:1, 0:512],
                                            PA[0:1, 1024:1536]),
                    after=[mm_s[-1]])
            t_s1 = row_transposes(rowb, [0, 128, 256, 384], cols[:, 0:4],
                                  [tv_])
            tv_ = V(lambda e: e.tensor_copy(rowb[0:1, 0:512],
                                            PA[0:1, 1536:2048]),
                    after=[tv_, t_s1])
            t_scols = row_transposes(rowb, [0, 128, 256, 384], cols[:, 4:8],
                                     [tv_])
            CW = 4
            m_c = cols[:, 12:12 + CW]; inv_c = cols[:, 16:16 + CW]
            var_c = cols[:, 20:20 + CW]; tmpc = cols[:, 48:48 + CW]
            tv_ = V(lambda e: e.tensor_scalar(m_c, cols[:, 0:4], 1.0 / D,
                                              None, OP.mult), after=[t_scols])
            tv_ = V(lambda e: e.tensor_tensor(var_c, m_c, m_c, OP.mult),
                    after=[tv_])
            tv_ = V(lambda e: e.tensor_scalar(tmpc, cols[:, 4:8], 1.0 / D,
                                              None, OP.mult), after=[tv_])
            tv_ = V(lambda e: e.tensor_tensor(var_c, tmpc, var_c, OP.subtract),
                    after=[tv_])
            tv_ = V(lambda e: e.tensor_scalar(var_c, var_c, 1e-5, None,
                                              OP.add), after=[tv_])
            tinv = rsqrt_rows(inv_c, var_c, tmpc, [tv_])
            tBm = bcast_row(B_rows[:, 0:512], m_c, 4, B_bt, [tinv])
            tBi = bcast_row(B_rows[:, 512:1024], inv_c, 4, B_bt, [tinv, tBm])
            mmq = []
            tv_ = tBi
            r_st = None
            for dcc in range(DC):
                x_t = B_x1[:, dcc * 512:dcc * 512 + 512]
                xl_t = B_xl[:, dcc * 512:dcc * 512 + 512]
                tv_ = V(lambda e, xx=x_t, o_=xl_t: e.tensor_tensor(
                    o_, xx, B_rows[:, 0:512], OP.subtract),
                    after=[tBm, t_xl, tv_])
                tv_ = V(lambda e, o_=xl_t: e.tensor_tensor(
                    o_, o_, B_rows[:, 512:1024], OP.mult), after=[tBi, tv_])
                tv_ = V(lambda e, xx=x_t, o_=xl_t: e.tensor_tensor(
                    xx, xx, o_, OP.add), after=[tv_])
                r_st = Y(lambda e, xx=x_t,
                         d=rs[128 * dcc:128 * dcc + 128, t0:t0 + 512]:
                         e.dma_start(out=d, in_=xx), after=[tv_], sem=sDS)
                x2t = B_gb[dcc % 2]
                dep = [tv_]
                if dcc >= 2:
                    dep.append(mmq[dcc - 2])
                tq = SC(lambda e, o_=x2t, i_=xl_t: e.activation(
                    o_, i_, AF.Square), after=dep)
                tm2 = T(lambda e, i_=x2t, f=(dcc == 0), l=(dcc == DC - 1):
                        e.matmul(PA[0:1, 1024:1536], onesc16, i_,
                                 start=f, stop=l),
                        after=[tq], inc=True)
                mmq.append(tm2)
            prev_st = r_st
            tv_ = V(lambda e: e.tensor_copy(rowb[0:1, 0:512],
                                            PA[0:1, 1024:1536]),
                    after=[mmq[-1]])
            t_rms = row_transposes(rowb, [0, 128, 256, 384], cols[:, 0:4],
                                   [tv_])
            tv_ = V(lambda e: e.tensor_scalar(m_c, cols[:, 0:4], 1.0 / D,
                                              1e-6, OP.mult, OP.add),
                    after=[t_rms])
            tinv2 = rsqrt_rows(inv_c, m_c, tmpc, [tv_])
            tBr = bcast_row(B_rows[:, 1024:1536], inv_c, 4, B_bt,
                            [tinv2, tBi])
            tv_ = tBr
            for dcc in range(DC):
                xl_t = B_xl[:, dcc * 512:dcc * 512 + 512]
                xn_t = B_xn[:, dcc * S + t0:dcc * S + t0 + 512]
                tv_ = V(lambda e, i_=xl_t: e.tensor_tensor(
                    B_g[:, 0:512], i_, B_rows[:, 1024:1536], OP.mult),
                    after=[tBr, tv_])
                tv_ = V(lambda e, o_=xn_t, dd=dcc: e.tensor_scalar(
                    o_, B_g[:, 0:512], sb_nw[:, dd:dd + 1], None, OP.mult),
                    after=[tv_])
            xn_done = tv_
        t_rs_done = sDS.tok
        t_xn = xn_done
        if dbg:
            t_xn = Y(lambda e: e.dma_start(out=dbg_xn, in_=B_xn),
                     after=[xn_done], sem=sDW)

        ig_ld = {}
        u_toks = {}
        for m in range(IC):
            w_in, w_g = B_igs[(m % 2) * 2], B_igs[(m % 2) * 2 + 1]
            dep = [u_toks.get((m - 2, 1)), t_xn]
            tl = GD(lambda e, d=w_in, s=in_ws[m]: e.dma_start(out=d, in_=s),
                    after=dep, sem=sG2[m % 2])
            tl = GD(lambda e, d=w_g, s=gate_ws[m]: e.dma_start(out=d, in_=s),
                    sem=sG2[m % 2])
            ig_ld[m] = tl
            for half in range(2):
                pp = PA if half == 0 else PB
                hoff = 1024 * half
                deps = [tl, t_xn, u_toks.get((m - 1, half))]
                for kcb in range(DC):
                    for ncb in range(2):
                        T(lambda e, k=kcb, nn=ncb, w=w_in, po=hoff, pq=pp,
                          f=(kcb == 0), l=(kcb == DC - 1):
                          e.matmul(pq[:, 512 * nn:512 * nn + 512],
                                   w[:, k * 128:(k + 1) * 128],
                                   B_xn[:, k * S + po + 512 * nn:
                                        k * S + po + 512 * nn + 512],
                                   start=f, stop=l),
                          after=deps if (kcb == 0 and ncb == 0) else [],
                          inc=False)
                for kcb in range(DC):
                    for ncb in range(2):
                        tg_ = T(lambda e, k=kcb, nn=ncb, w=w_g, po=hoff, pq=pp,
                                f=(kcb == 0), l=(kcb == DC - 1):
                                e.matmul(pq[:, 1024 + 512 * nn:
                                            1024 + 512 * nn + 512],
                                         w[:, k * 128:(k + 1) * 128],
                                         B_xn[:, k * S + po + 512 * nn:
                                              k * S + po + 512 * nn + 512],
                                         start=f, stop=l),
                                after=[],
                                inc=(kcb == DC - 1 and ncb == 1))
                gbuf = B_g[:, 1024 * half:1024 * half + 1024]
                tsg = SC(lambda e, g=gbuf, mm=m, pq=pp: e.activation(
                    g, pq[:, 1024:2048], AF.Sigmoid,
                    bias=sb_gateb[:, mm:mm + 1]), after=[tg_])
                ut = B_u[:, m * S + hoff:m * S + hoff + 1024]
                tu = V(lambda e, g=gbuf, o_=ut, mm=m, pq=pp:
                       e.scalar_tensor_tensor(o_, pq[:, 0:1024],
                                              sb_inb[:, mm:mm + 1], g,
                                              OP.add, OP.mult),
                       after=[tsg, t_rs_done])
                u_toks[(m, half)] = tu
        t_u = sV.tok

        bcd_ld = {}
        y_toks = {}
        scan_tok = None
        for m in range(IC):
            sb3 = [B_bcds[(m % 2) * 3 + i] for i in range(3)]
            dep = [(y_toks[m - 2][0] if m >= 2 else None), t_u]
            for i, ws in enumerate([b_ws, c_ws, d_ws]):
                tl = GD(lambda e, d=sb3[i], s=ws[m]: e.dma_start(out=d, in_=s),
                        after=dep if i == 0 else [], sem=sB2[m % 2])
            bcd_ld[m] = tl
            depb = [tl, t_u, (y_toks[m - 1][0] if m >= 1 else None)]
            for nb in range(4):
                for kcb in range(IC):
                    T(lambda e, k=kcb, nn=nb, w=sb3[0], f=(kcb == 0):
                      e.matmul(PA[:, 512 * nn:512 * nn + 512],
                               w[:, k * 128:(k + 1) * 128],
                               B_u[:, k * S + 512 * nn:k * S + 512 * nn + 512],
                               start=f, stop=False),
                      after=depb if (nb == 0 and kcb == 0) else [], inc=False)
                tb = T(lambda e, nn=nb, mm=m:
                       e.matmul(PA[:, 512 * nn:512 * nn + 512],
                                bbr16[0:1, 128 * mm:128 * mm + 128],
                                ones16[0:1, 0:512], start=False, stop=True),
                       after=[t_bb], inc=(nb == 3))
            scan_dep = [tb, t_acol, scan_tok,
                        (y_toks[m - 1][1] if m >= 1 else None)]
            tsc_ = V(lambda e, mm=m: e.tensor_tensor_scan(
                B_hs, sb_acol[:, mm:mm + 1].broadcast_to([128, S]),
                PA[:, 0:S], 0.0, OP.mult, OP.add), after=scan_dep)
            scan_tok = tsc_
            depc = [tl, t_u, (y_toks[m - 1][1] if m >= 1 else None)]
            for nb in range(4):
                for kcb in range(IC):
                    tcm = T(lambda e, k=kcb, nn=nb, w=sb3[1], f=(kcb == 0),
                            l=(kcb == IC - 1):
                            e.matmul(PB[:, 512 * nn:512 * nn + 512],
                                     w[:, k * 128:(k + 1) * 128],
                                     B_u[:, k * S + 512 * nn:
                                          k * S + 512 * nn + 512],
                                     start=f, stop=l),
                            after=depc if (nb == 0 and kcb == 0) else [],
                            inc=(nb == 3 and kcb == IC - 1))
            ty0 = V(lambda e, mm=m: e.scalar_tensor_tensor(
                B_scr, PB[:, 0:S], sb_cb[:, mm:mm + 1], B_hs,
                OP.add, OP.mult),
                after=[tcm, tsc_, (y_toks[m - 1][0] if m >= 1 else None)])
            for nb in range(4):
                for kcb in range(IC):
                    tdm = T(lambda e, k=kcb, nn=nb, w=sb3[2], f=(kcb == 0),
                            l=(kcb == IC - 1):
                            e.matmul(PA[:, 512 * nn:512 * nn + 512],
                                     w[:, k * 128:(k + 1) * 128],
                                     B_u[:, k * S + 512 * nn:
                                          k * S + 512 * nn + 512],
                                     start=f, stop=l),
                            after=[tsc_] if (nb == 0 and kcb == 0) else [],
                            inc=(nb == 3 and kcb == IC - 1))
            y_t = B_y[:, m * S:(m + 1) * S]
            ty1 = V(lambda e, mm=m, o_=y_t: e.scalar_tensor_tensor(
                o_, PA[:, 0:S], sb_db[:, mm:mm + 1], B_scr, OP.add, OP.add),
                after=[tdm, ty0])
            y_toks[m] = (ty1, ty0)
        t_y = sV.tok

        out_ld = {}
        st_toks = {}
        ev_toks = {}
        for mo in range(DC):
            osl = B_outs[mo % 2]
            rb = B_rr[mo % 2]
            tl = GD(lambda e, d=osl, s=out_ws[mo]: e.dma_start(out=d, in_=s),
                    after=[ev_toks.get(mo - 2), t_u], sem=sO2[mo % 2])
            out_ld[mo] = tl
            trl = Y(lambda e, d=rb, s=rs[128 * mo:128 * mo + 128, :]:
                    e.dma_start(out=d, in_=s),
                    after=[st_toks.get(mo - 2), t_y], sem=sR2[mo % 2])
            pp = PA if mo % 2 == 0 else PB
            deps = [tl, t_y, ev_toks.get(mo - 2)]
            for nb in range(4):
                for kcb in range(IC):
                    tm = T(lambda e, k=kcb, nn=nb, w=osl, pq=pp, f=(kcb == 0),
                           l=(kcb == IC - 1):
                           e.matmul(pq[:, 512 * nn:512 * nn + 512],
                                    w[:, k * 128:(k + 1) * 128],
                                    B_y[:, k * S + 512 * nn:
                                         k * S + 512 * nn + 512],
                                    start=f, stop=l),
                           after=deps if (nb == 0 and kcb == 0) else [],
                           inc=(nb == 3 and kcb == IC - 1))
            tev = V(lambda e, mm=mo, r=rb, pq=pp: e.scalar_tensor_tensor(
                r, pq[:, 0:S], sb_outb[:, mm:mm + 1], r, OP.add, OP.add),
                after=[tm, trl])
            ev_toks[mo] = tev
            st_toks[mo] = Y(lambda e, r=rb,
                            d=dst_dram[128 * mo:128 * mo + 128, :]:
                            e.dma_start(out=d, in_=r), after=[tev],
                            sem=sS2[mo % 2])

    if phases == 1:
        ffn_phase(xT, outT, quantize=True)
    elif phases == 2:
        ffn_phase(xT, xs1, quantize=True)
        p.barrier(ALLS)
        attn_phase(xs1, outT)
    else:
        ffn_phase(xT, xs1, quantize=True)
        p.barrier(ALLS)
        attn_phase(xs1, xs2)
        p.barrier(ALLS)
        ffn_phase(xs2, outT, quantize=False)
    p.barrier(ALLS)

    with nc.Block() as block:
        def runner(eng_name):
            def run(e):
                for waits, fn, inc in p.items[eng_name]:
                    for (h, v) in waits:
                        e.wait_ge(h, v)
                    if fn is None:
                        continue
                    ins = fn(e)
                    if inc is not None:
                        ins.then_inc(inc[0], inc[1])
            return run
        block.sync(runner("sync"))
        block.vector(runner("vector"))
        block.scalar(runner("scalar"))
        block.tensor(runner("tensor"))
        block.gpsimd(runner("gpsimd"))
    return nc


def _prep_inputs(w):
    f = np.ascontiguousarray

    def slab(wm, mtiles, ktiles):
        return f(wm.reshape(mtiles, 128, ktiles, 128).transpose(0, 3, 2, 1)
                 .reshape(mtiles, 128, ktiles * 128).astype(np.float32))

    conv_w = w["conv_w"].reshape(H, 3)
    d = {
        "w1T": f(w["w1"].T.astype(np.float32)),
        "w2T": f(w["w2"].T.astype(np.float32)),
        "in_ws": slab(w["in_w"], IC, DC),
        "gate_ws": slab(w["gate_w"], IC, DC),
        "b_ws": slab(w["b_w"], IC, IC),
        "c_ws": slab(w["c_w"], IC, IC),
        "d_ws": slab(w["d_w"], IC, IC),
        "out_ws": slab(w["out_w"], DC, IC),
        "in_bc": f(w["in_b"].reshape(IC, 128).T.astype(np.float32)),
        "gate_bc": f(w["gate_b"].reshape(IC, 128).T.astype(np.float32)),
        "b_br": f(w["b_b"].reshape(1, INNER).astype(np.float32)),
        "c_bc": f(w["c_b"].reshape(IC, 128).T.astype(np.float32)),
        "d_bc": f(w["d_b"].reshape(IC, 128).T.astype(np.float32)),
        "out_bc": f(w["out_b"].reshape(DC, 128).T.astype(np.float32)),
        "nw_c": f(w["norm_w"].reshape(DC, 128).T.astype(np.float32)),
        "alog_c": f(w["a_log"].reshape(IC, 128).T.astype(np.float32)),
        "tap0_c": f(conv_w[:, 0].reshape(HC, 128).T.astype(np.float32)),
        "tap1_c": f(conv_w[:, 1].reshape(HC, 128).T.astype(np.float32)),
        "tap2_c": f(conv_w[:, 2].reshape(HC, 128).T.astype(np.float32)),
        "convb_c": f(w["conv_b"].reshape(HC, 128).T.astype(np.float32)),
        "ident_d": np.eye(128, dtype=np.float32),
    }
    return d


def kernel(L0_attn_x, L0_attn_norm_w, L0_attn_in_w, L0_attn_in_b,
           L0_attn_gate_w, L0_attn_gate_b, L0_attn_b_w, L0_attn_b_b,
           L0_attn_c_w, L0_attn_c_b, L0_attn_d_w, L0_attn_d_b,
           L0_attn_out_w, L0_attn_out_b, L0_attn_a_log,
           L0_ffn_w1, L0_ffn_conv_w, L0_ffn_conv_b, L0_ffn_w2,
           _phases=3):
    x = np.asarray(L0_attn_x, dtype=np.float32)
    w = dict(norm_w=L0_attn_norm_w, in_w=L0_attn_in_w, in_b=L0_attn_in_b,
             gate_w=L0_attn_gate_w, gate_b=L0_attn_gate_b, b_w=L0_attn_b_w,
             b_b=L0_attn_b_b, c_w=L0_attn_c_w, c_b=L0_attn_c_b,
             d_w=L0_attn_d_w, d_b=L0_attn_d_b, out_w=L0_attn_out_w,
             out_b=L0_attn_out_b, a_log=L0_attn_a_log, w1=L0_ffn_w1,
             conv_w=L0_ffn_conv_w, conv_b=L0_ffn_conv_b, w2=L0_ffn_w2)
    w = {k: np.asarray(v) for k, v in w.items()}
    shared = _prep_inputs(w)
    nc = build(phases=_phases)
    in_maps = []
    for c in range(8):
        m = dict(shared)
        m["xT"] = np.ascontiguousarray(x[c].T)
        in_maps.append(m)
    res = run_bass_kernel_spmd(nc, in_maps, core_ids=list(range(8)))
    out = np.stack([np.asarray(r["outT"]).T for r in res.results], axis=0)
    return np.ascontiguousarray(out.astype(np.float32))


# version 18
# speedup vs baseline: 1.0855x; 1.0764x over previous
import numpy as np
import concourse.bass as bass
import concourse.mybir as mybir
from concourse.bass_utils import run_bass_kernel_spmd

F32 = mybir.dt.float32
BF16 = mybir.dt.bfloat16
U8 = mybir.dt.uint8
OP = mybir.AluOpType
AF = mybir.ActivationFunctionType
AX = mybir.AxisListType.X

RC = 12582912.0
WCLIP = 1.4999999
C52 = 6755399441055744.0
X52 = C52 - RC
D, S, INNER, H = 1024, 2048, 2048, 4096
DC, IC, HC = D // 128, INNER // 128, H // 128
TW, TST, NT = 258, 256, 8

ENGINES = ("sync", "vector", "scalar", "tensor", "gpsimd")


class Sem:
    def __init__(self, nc, name):
        self.h = nc.alloc_semaphore(name)
        self.n = 0

    @property
    def tok(self):
        return (self, self.n)


class Prog:

    def __init__(self, nc):
        self.nc = nc
        self.items = {e: [] for e in ENGINES}
        self.waited = {e: {} for e in ENGINES}

    def emit(self, eng, fn, after=(), inc=None, inck=1):
        waits = []
        wm = self.waited[eng]
        for t in after:
            if t is None:
                continue
            s, v = t
            if v <= 0:
                continue
            if v > wm.get(s, 0):
                waits.append((s.h, v))
                wm[s] = v
        tok = None
        pair = None
        if inc is not None:
            inc.n += inck
            tok = (inc, inc.n)
            pair = (inc.h, inck)
        self.items[eng].append((waits, fn, pair))
        return tok

    def barrier(self, sems):
        toks = [s.tok for s in sems]
        for e in ENGINES:
            for s, v in toks:
                if v > 0 and v > self.waited[e].get(s, 0):
                    self.items[e].append(([(s.h, v)], None, None))
                    self.waited[e][s] = v


def build(phases=3, dbg=False, ntc=None):
    import os
    if ntc is None:
        ntc = int(os.environ.get("K_NTC", "99"))
    kstage = int(os.environ.get("K_STAGE", "99"))
    nc = bass.Bass("TRN2", target_bir_lowering=False)
    p = Prog(nc)

    dt_in = {}

    def din(name, shape, dtype=F32):
        dt_in[name] = nc.dram_tensor(name, list(shape), dtype,
                                     kind="ExternalInput").ap()
        return dt_in[name]

    xT = din("xT", [D, S])
    w1T = din("w1T", [D, H])
    w2T = din("w2T", [H, D])
    in_ws = din("in_ws", [IC, 128, DC * 128])
    gate_ws = din("gate_ws", [IC, 128, DC * 128])
    b_ws = din("b_ws", [IC, 128, IC * 128])
    c_ws = din("c_ws", [IC, 128, IC * 128])
    d_ws = din("d_ws", [IC, 128, IC * 128])
    out_ws = din("out_ws", [DC, 128, IC * 128])
    in_bc = din("in_bc", [128, IC])
    gate_bc = din("gate_bc", [128, IC])
    b_br = din("b_br", [1, INNER])
    c_bc = din("c_bc", [128, IC])
    d_bc = din("d_bc", [128, IC])
    out_bc = din("out_bc", [128, DC])
    nw_c = din("nw_c", [128, DC])
    alog_c = din("alog_c", [128, IC])
    tap0_c = din("tap0_c", [128, HC])
    tap1_c = din("tap1_c", [128, HC])
    tap2_c = din("tap2_c", [128, HC])
    convb_c = din("convb_c", [128, HC])
    ident_d = din("ident_d", [128, 128])

    outT = nc.dram_tensor("outT", [D, S], F32, kind="ExternalOutput").ap()
    dbg_h = nc.dram_tensor("dbg_h", [128, HC * TW], F32,
                           kind="ExternalOutput").ap() if dbg else None
    dbg_r = nc.dram_tensor("dbg_r", [128, 3 * TW], F32,
                           kind="ExternalOutput").ap() if dbg else None
    dbg_c = nc.dram_tensor("dbg_c", [128, 64], F32,
                           kind="ExternalOutput").ap() if dbg else None
    dbg_x2 = nc.dram_tensor("dbg_x2", [128, 2 * TW], F32,
                            kind="ExternalOutput").ap() if dbg else None
    dbg_xn = nc.dram_tensor("dbg_xn", [128, DC * S], BF16,
                            kind="ExternalOutput").ap() if dbg else None
    dbg_s2 = nc.dram_tensor("dbg_s2", [1, TW], F32,
                            kind="ExternalOutput").ap() if dbg else None
    xs1 = nc.dram_tensor("xs1", [D, S], F32, kind="Internal").ap()
    rs = nc.dram_tensor("rs", [D, S], F32, kind="Internal").ap()
    xs2 = nc.dram_tensor("xs2", [D, S], F32, kind="Internal").ap()
    w1q_d = nc.dram_tensor("w1q_d", [128, DC * H], BF16, kind="Internal").ap()
    w2q_d = nc.dram_tensor("w2q_d", [128, HC * D], BF16, kind="Internal").ap()

    ident = nc.alloc_sbuf_tensor("ident", [128, 128], F32).ap()
    ones128 = nc.alloc_sbuf_tensor("ones128", [128, 128], F32).ap()
    onesc = nc.alloc_sbuf_tensor("onesc", [128, 1], F32).ap()
    onesc16 = nc.alloc_sbuf_tensor("onesc16", [128, 1], BF16).ap()
    ones16 = nc.alloc_sbuf_tensor("ones16", [1, 512], BF16).ap()
    bbr16 = nc.alloc_sbuf_tensor("bbr16", [1, INNER], BF16).ap()
    kc = nc.alloc_sbuf_tensor("kc", [128, 8], F32).ap()
    sb_inb = nc.alloc_sbuf_tensor("sb_inb", [128, IC], F32).ap()
    sb_gateb = nc.alloc_sbuf_tensor("sb_gateb", [128, IC], F32).ap()
    sb_cb = nc.alloc_sbuf_tensor("sb_cb", [128, IC], F32).ap()
    sb_db = nc.alloc_sbuf_tensor("sb_db", [128, IC], F32).ap()
    sb_outb = nc.alloc_sbuf_tensor("sb_outb", [128, DC], F32).ap()
    sb_nw = nc.alloc_sbuf_tensor("sb_nw", [128, DC], F32).ap()
    sb_alog = nc.alloc_sbuf_tensor("sb_alog", [128, IC], F32).ap()
    sb_acol = nc.alloc_sbuf_tensor("sb_acol", [128, IC], F32).ap()
    sb_t0 = nc.alloc_sbuf_tensor("sb_t0", [128, HC], F32).ap()
    sb_t1 = nc.alloc_sbuf_tensor("sb_t1", [128, HC], F32).ap()
    sb_w2t = nc.alloc_sbuf_tensor("sb_w2t", [128, HC], F32).ap()
    sb_cb2 = nc.alloc_sbuf_tensor("sb_cb2", [128, HC], F32).ap()
    sb_r02 = nc.alloc_sbuf_tensor("sb_r02", [128, HC], F32).ap()
    sb_r12 = nc.alloc_sbuf_tensor("sb_r12", [128, HC], F32).ap()
    sb_mw = nc.alloc_sbuf_tensor("sb_mw", [128, 8], F32).ap()
    cols = nc.alloc_sbuf_tensor("cols", [128, 64], F32).ap()
    rowb = nc.alloc_sbuf_tensor("rowb", [1, 512], F32).ap()

    AR = 198 * 1024
    backing = nc.alloc_sbuf_tensor("arena", [128, AR], U8)
    base = nc.lookup_mloc(backing).addr

    def place(name, shape, dtype, off):
        per = int(np.prod(shape[1:])) * mybir.dt.size(dtype)
        assert off >= 0 and off + per <= AR, f"{name}: {off}+{per} > {AR}"
        h = nc.virtual_sbuf_tensor(name, list(shape), dtype)
        nc._tensor(h.name, h.shape, h.dtype, type="SB")
        mloc = nc.lookup_mloc(h)
        mloc.allocated = True
        mloc.addr = base + off
        return h.ap()

    o = 0
    A_w1q = place("A_w1q", [128, DC * H], BF16, o); o += DC * H * 2
    A_w2q = place("A_w2q", [128, HC * D], BF16, o); o += HC * D * 2
    A_xw2 = [place(f"A_xw{i}", [128, DC * TW], F32, o + i * DC * TW * 4)
             for i in range(2)]
    o += 2 * DC * TW * 4
    o_h = o
    A_h = place("A_h", [128, HC * TW], F32, o); o += HC * TW * 4
    A_n2 = [place(f"A_n2_{m}", [128, 256], BF16, o_h + m * TW * 4)
            for m in range(HC)]
    A_n2A = place("A_n2A", [128, HC * TW * 2], BF16, o_h)
    A_wc = [place(f"A_wc{i}", [128, 1024], F32, o_h + i * 4096) for i in range(4)]
    A_n1 = place("A_n1", [128, DC * TW], BF16, o); o += DC * TW * 2
    A_qt = place("A_qt", [128, 256], F32, o); o += 1024
    A_eq = place("A_eq", [128, 256], F32, o); o += 1024
    o_t1 = o
    A_t1 = [place(f"A_t1_{i}", [128, TW], F32, o + i * TW * 4) for i in range(2)]
    o += 2 * TW * 4
    o_t2 = o
    A_t2 = [place(f"A_t2_{i}", [128, TW], F32, o + i * TW * 4) for i in range(2)]
    o += 2 * TW * 4
    A_t1b = [place(f"A_t1b_{i}", [128, TW], BF16, o_t1 + i * TW * 4)
             for i in range(2)]
    A_t2b = [place(f"A_t2b_{i}", [128, TW], BF16, o_t2 + i * TW * 2)
             for i in range(2)]
    A_mx = place("A_mx", [128, TW], F32, o); o += TW * 4
    A_mn = place("A_mn", [128, TW], F32, o); o += TW * 4
    A_r0 = place("A_r0", [128, TW], F32, o); o += TW * 4
    A_r1 = place("A_r1", [128, TW], F32, o); o += TW * 4
    A_r2 = place("A_r2", [128, TW], F32, o); o += TW * 4
    A_bt = place("A_bt", [128, 9 * 128], F32, o)
    o += 9 * 128 * 4
    A_r3 = place("A_r3", [128, 256], F32, o); o += 1024
    A_r4 = place("A_r4", [128, 256], F32, o); o += 1024
    assert o <= AR, o

    o = 0
    o_xn = o
    B_xn = place("B_xn", [128, DC * S], BF16, o); o += DC * S * 2
    o_u = o
    B_u = place("B_u", [128, IC * S], BF16, o); o += IC * S * 2
    o_y = o
    B_y = place("B_y", [128, IC * S], BF16, o); o += IC * S * 2
    B_hs = place("B_hs", [128, S], F32, o); o += S * 4
    B_scr = place("B_scr", [128, S], F32, o); o += S * 4
    o_bg = o
    B_g = place("B_g", [128, 2 * 1024], F32, o); o += 2 * 1024 * 4
    B_gb = [place(f"B_gb_{i}", [128, 512], BF16, o_bg + i * 1024)
            for i in range(2)]
    B_gxb = [place(f"B_gxb_{i}", [128, 512], BF16, o_bg + 2048 + i * 1024)
             for i in range(2)]
    assert o <= AR, o
    B_x1 = place("B_x1", [128, DC * 512], F32, o_u)
    B_xl = place("B_xl", [128, DC * 512], F32, o_u + DC * 512 * 4)
    B_rows = place("B_rows", [128, 3 * 512], F32, o_y)
    B_bt = place("B_bt", [128, 4 * 128], F32, o_y + 3 * 512 * 4)
    B_igs = [place(f"B_igs{i}", [128, DC * 128], BF16,
                   o_y + 3 * 512 * 4 + 4 * 128 * 4 + i * DC * 256)
             for i in range(4)]
    B_bcds = [place(f"B_bcds{i}", [128, IC * 128], BF16, o_xn + i * IC * 256)
              for i in range(6)]
    B_outs = [place(f"B_outs{i}", [128, IC * 128], BF16,
                    o_xn + (6 + i) * IC * 256)
              for i in range(2)]
    B_rr = [place(f"B_rr{i}", [128, S], F32, o_u + i * S * 4) for i in range(2)]

    PA = nc.alloc_psum_tensor("PA", [128, 2048], F32).ap()
    PB = nc.alloc_psum_tensor("PB", [128, 2048], F32).ap()

    sDX = Sem(nc, "sDX")
    sDW = Sem(nc, "sDW")
    sDS = Sem(nc, "sDS")
    sDG = Sem(nc, "sDG")
    sTE = Sem(nc, "sTE")
    sV = Sem(nc, "sV")
    sSC = Sem(nc, "sSC")
    sGP = Sem(nc, "sGP")
    sW = [Sem(nc, f"sW{i}") for i in range(4)]
    sG2 = [Sem(nc, "sG0"), Sem(nc, "sG1")]
    sB2 = [Sem(nc, "sB0"), Sem(nc, "sB1")]
    sO2 = [Sem(nc, "sO0"), Sem(nc, "sO1")]
    sR2 = [Sem(nc, "sR0"), Sem(nc, "sR1")]
    sMISC = Sem(nc, "sMISC")
    sS2 = [Sem(nc, "sS0"), Sem(nc, "sS1")]
    sX2 = [Sem(nc, "sX0"), Sem(nc, "sX1")]
    ALLS = [sDX, sDW, sDS, sDG, sTE, sV, sSC, sGP,
            sW[0], sW[1], sW[2], sW[3], sG2[0], sG2[1], sB2[0], sB2[1],
            sO2[0], sO2[1], sR2[0], sR2[1], sMISC, sS2[0], sS2[1],
            sX2[0], sX2[1]]

    def Y(fn, after=(), sem=None):
        return p.emit("sync", fn, after, inc=sem if sem else sDX, inck=16)

    def GD(fn, after=(), sem=None):
        return p.emit("gpsimd", fn, after, inc=sem if sem else sDG, inck=16)

    def V(fn, after=()):
        return p.emit("vector", fn, after, inc=sV)

    def SC(fn, after=()):
        return p.emit("scalar", fn, after, inc=sSC)

    def T(fn, after=(), inc=True):
        return p.emit("tensor", fn, after, inc=sTE if inc else None)

    def G(fn, after=()):
        return p.emit("gpsimd", fn, after, inc=sGP)

    for name, dst in [("ident_d", ident), ("in_bc", sb_inb),
                      ("gate_bc", sb_gateb), ("c_bc", sb_cb), ("d_bc", sb_db),
                      ("out_bc", sb_outb), ("nw_c", sb_nw), ("alog_c", sb_alog),
                      ("tap0_c", sb_t0), ("tap1_c", sb_t1), ("tap2_c", sb_w2t),
                      ("convb_c", sb_cb2)]:
        src = dt_in[name]
        Y(lambda e, d=dst, s=src: e.dma_start(out=d, in_=s))
    t_consts = sDX.tok
    V(lambda e: e.memset(kc[:, 0:1], 1.0))
    V(lambda e: e.memset(kc[:, 1:2], -1.0))
    V(lambda e: e.memset(kc[:, 2:3], -0.5))
    V(lambda e: e.memset(kc[:, 3:4], 0.0))
    V(lambda e: e.memset(kc[:, 4:5], RC))
    V(lambda e: e.memset(kc[:, 5:6], -RC))
    tk_ = V(lambda e: e.memset(kc[:, 7:8], 1.0))
    tk_ = V(lambda e: e.tensor_scalar(kc[:, 7:8], kc[:, 7:8], C52, C52,
                                      OP.add, OP.subtract), after=[tk_])
    t_ceff = V(lambda e: e.tensor_scalar(kc[:, 6:7], kc[:, 7:8], X52, RC,
                                         OP.mult, OP.add), after=[tk_])
    V(lambda e: e.memset(onesc, 1.0))
    V(lambda e: e.memset(onesc16, 1.0))
    V(lambda e: e.memset(ones128, 1.0))
    V(lambda e: e.memset(ones16, 1.0))
    V(lambda e: e.memset(rowb, 0.0))
    V(lambda e: e.memset(cols, 0.0))
    t_bb = GD(lambda e: e.dma_start(out=bbr16, in_=dt_in["b_br"]), sem=sMISC)
    tv = V(lambda e: e.reciprocal(sb_r02, sb_w2t), after=[t_consts])
    tv2 = V(lambda e: e.tensor_mul(sb_r12, sb_t1, sb_r02), after=[tv])
    V(lambda e: e.tensor_mul(sb_r02, sb_t0, sb_r02), after=[tv2])
    c0, c1 = cols[:, 0:IC], cols[:, 16:32]
    tv = V(lambda e: e.tensor_scalar(c0, sb_alog, -1.0, None, OP.mult),
           after=[t_consts])
    tv = V(lambda e: e.tensor_tensor(c0, c0, sb_alog, OP.max), after=[tv])
    ts = SC(lambda e: e.activation(c1, c0, AF.Exp, scale=kc[:, 1:2]),
            after=[tv])
    ts = SC(lambda e: e.activation(c1, c1, AF.Ln, bias=kc[:, 0:1]), after=[ts])
    tv = V(lambda e: e.tensor_scalar(c0, sb_alog, 0.0, None, OP.max),
           after=[t_consts, tv, ts])
    tv = V(lambda e: e.tensor_tensor(c0, c0, c1, OP.add), after=[tv, ts])
    t_acol = SC(lambda e: e.activation(sb_acol, c0, AF.Exp, scale=kc[:, 1:2]),
                after=[tv])

    pb_last = {2: None, 3: None}
    def rsqrt_rows(dst, src, tmp, deps):
        t1 = V(lambda e: e.tensor_scalar(src, src, 1e-30, None, OP.max),
               after=deps)
        t1 = SC(lambda e: e.activation(tmp, src, AF.Ln), after=[t1])
        return SC(lambda e: e.activation(dst, tmp, AF.Exp, scale=kc[:, 2:3]),
                  after=[t1])

    def bcast_multi(jobs, bt, deps, gen_yield=False):
        it = _bcast_multi_gen(jobs, bt, deps)
        if gen_yield:
            return it
        last = None
        for last in it:
            pass
        return last

    def _bcast_multi_gen(jobs, bt, deps):
        blocks = []
        for dst, src_col in jobs:
            nblk = src_col.shape[1]
            for c_ in range(nblk):
                blocks.append((dst, src_col, c_, min(128, dst.shape[1] - 128 * c_)))
        nb = len(blocks)
        t0 = V(lambda e, bl=list(blocks): [
            None], after=deps) if False else None
        off = 0
        copy_toks = []
        offs = []
        tprev = None
        for dst, src_col in jobs:
            nblk = src_col.shape[1]
            tprev = V(lambda e, sc=src_col, o=off, nn=nblk: e.tensor_copy(
                bt[:, o:o + nn * 128].rearrange("p (c j) -> p c j", c=nn),
                sc.unsqueeze(2).broadcast_to([128, nn, 128])),
                after=deps + [tprev])
            offs.append(off)
            off += nblk * 128
        t0 = tprev
        assert off <= bt.shape[1], (off, bt.shape)
        i = 0
        last = None
        for ji, (dst, src_col) in enumerate(jobs):
            nblk = src_col.shape[1]
            for c_ in range(nblk):
                n = min(128, dst.shape[1] - 128 * c_)
                bk = 2 + (i % 2)
                pbase = 1024 + 512 * (bk - 2) + 256
                tt = T(lambda e, o=offs[ji] + 128 * c_, pb=pbase: e.transpose(
                    PB[:, pb:pb + 128], bt[:, o:o + 128], ident),
                    after=[t0, t_consts, pb_last[bk]])
                last = V(lambda e, c=c_, nn=n, d=dst, pb=pbase: e.tensor_copy(
                    d[:, 128 * c:128 * c + nn], PB[:, pb:pb + nn]),
                    after=[tt, last])
                pb_last[bk] = last
                i += 1
                yield last

    def bcast_row(dst, src_col, nblk, bt, deps, upto=99):
        return bcast_multi([(dst, src_col[:, 0:nblk])], bt, deps)

    def row_transposes(src_row, offs, dst_cols, deps):
        toks = []
        for j, so in enumerate(offs):
            tt = T(lambda e, jj=j, s=so: e.transpose(
                PB[:, 1024 + jj:1024 + jj + 1], src_row[0:1, s:s + 128],
                ident[0:1, 0:1]),
                after=deps + [t_consts, pb_last[2]])
            toks.append(tt)
        ev = V(lambda e, n=len(offs): e.tensor_copy(
            dst_cols, PB[:, 1024:1024 + n]), after=[toks[-1]])
        pb_last[2] = ev
        return ev

    def ffn_phase(src_dram, dst_dram, quantize):
        t_w1q = t_w2q = None

        def gen_Q(wi):
            nonlocal t_w1q, t_w2q
            wT, wq, wq_dr = [(w1T, A_w1q, w1q_d),
                             (w2T, A_w2q, w2q_d)][wi]
            nch = 32
            ncol = H if wi == 0 else D
            nq = ncol // 1024
            sumc = cols[:, 32:64]
            toks = []
            for i in range(nch):
                kcb, q = divmod(i, nq)
                src = wT[128 * kcb:128 * kcb + 128, 1024 * q:1024 * q + 1024]
                dep = [toks[i - 4][1]] if i >= 4 else [t_acol]
                tl = Y(lambda e, d=A_wc[i % 4], s=src:
                       e.dma_start(out=d, in_=s), after=dep, sem=sW[i % 4])
                tr = V(lambda e, d=A_wc[i % 4], c=sumc[:, i:i + 1]:
                       e.reduce_sum(c, d, AX, apply_absolute_value=True),
                       after=[tl])
                toks.append((tl, tr))
                yield
            tv_ = V(lambda e: e.reduce_sum(cols[:, 30:31], sumc, AX),
                    after=[toks[-1][1]])
            tm_ = T(lambda e: e.matmul(PB[:, 1024:1025], ones128,
                                       cols[:, 30:31],
                                       start=True, stop=True),
                    after=[tv_, t_consts, pb_last[2]])
            tv_ = V(lambda e: e.tensor_copy(cols[:, 31:32], PB[:, 1024:1025]),
                    after=[tm_])
            pb_last[2] = tv_
            tv_ = V(lambda e, nrm=1.0 / (1024 * 4096): e.tensor_scalar(
                cols[:, 30:31], cols[:, 31:32], nrm, 1e-5,
                OP.mult, OP.max), after=[tv_])
            tv_ = V(lambda e, m=wi: e.reciprocal(sb_mw[:, m:m + 1],
                                                 cols[:, 30:31]),
                    after=[tv_])
            if wi == 0:
                tsc = V(lambda e: e.tensor_scalar(
                    sb_mw[:, 2:3], cols[:, 30:31], 1.0 / 127, None,
                    OP.mult), after=[tv_])
            else:
                tsc = V(lambda e: e.tensor_scalar(
                    sb_mw[:, 3:4], cols[:, 30:31], 0.5 / 127, None,
                    OP.mult), after=[tv_])
            qtk = []
            for i in range(nch):
                kcb, q = divmod(i, nq)
                src = wT[128 * kcb:128 * kcb + 128, 1024 * q:1024 * q + 1024]
                dep = [qtk[i - 4]] if i >= 4 else [tsc]
                tl = Y(lambda e, d=A_wc[i % 4], s=src:
                       e.dma_start(out=d, in_=s), after=dep, sem=sW[i % 4])
                tq = V(lambda e, d=A_wc[i % 4], m=wi: e.tensor_scalar(
                    d, d, sb_mw[:, m:m + 1], WCLIP, OP.mult, OP.min),
                    after=[tl, tsc])
                tq = V(lambda e, d=A_wc[i % 4]: e.tensor_scalar(
                    d, d, -WCLIP, None, OP.max), after=[tq])
                tq = V(lambda e, d=A_wc[i % 4],
                       w=wq[:, 1024 * i:1024 * (i + 1)]:
                       e.tensor_scalar(w, d, kc[:, 6:7], kc[:, 6:7],
                                       OP.add, OP.subtract),
                       after=[tq, t_ceff])
                qtk.append(tq)
                yield
            Y(lambda e, w=wq, dr=wq_dr: e.dma_start(out=dr, in_=w),
              after=[qtk[-1]], sem=sDS)
            if wi == 0:
                t_w1q = qtk[-1]
            else:
                t_w2q = qtk[-1]

        evtok = {}
        st_hist = {}
        state = {}
        ctx = {"rows_user": None, "conv_last": None, "silu_last": None,
               "evac_last": None}

        def gen_A(tc):
            st = state.setdefault(tc, {})
            A_xw = A_xw2[tc % 2]
            t0 = TST * tc - 1
            lo, hi = max(t0, 0), min(t0 + TW, S)
            woff = lo - t0
            xl = None
            for dcc in range(DC):
                xl = Y(lambda e,
                       d=A_xw[:, dcc * TW + woff:dcc * TW + woff + hi - lo],
                       s=src_dram[128 * dcc:128 * dcc + 128, lo:hi]:
                       e.dma_start(out=d, in_=s),
                       after=[st_hist.get(tc - 2)], sem=sX2[tc % 2])
            t_xl = xl
            if tc == 0:
                ap0 = bass.AP(A_xw.tensor, A_xw.offset,
                              [[A_xw.ap[0][0], 128], [TW, DC], [1, 1]])
                t_xl = V(lambda e, a=ap0: e.memset(a, 0.0), after=[t_xl])
            if tc == NT - 1:
                ap1 = bass.AP(A_xw.tensor, A_xw.offset + TW - 1,
                              [[A_xw.ap[0][0], 128], [TW, DC], [1, 1]])
                t_xl = V(lambda e, a=ap1: e.memset(a, 0.0), after=[t_xl])
            yield
            mm_s = []
            for dcc in range(DC):
                x_t = A_xw[:, dcc * TW:(dcc + 1) * TW]
                x2t = A_t1b[dcc % 2]
                xbt = A_t2b[dcc % 2]
                dep = [t_xl, ctx["conv_last"]]
                if dcc >= 2:
                    dep.append(mm_s[dcc - 2])
                tcst = SC(lambda e, o_=xbt, i_=x_t: e.activation(
                    o_, i_, AF.Identity), after=dep)
                tq = SC(lambda e, o_=x2t, i_=x_t: e.activation(o_, i_,
                                                               AF.Square),
                        after=[tcst])
                T(lambda e, i_=xbt, f=(dcc == 0), l=(dcc == DC - 1):
                  e.matmul(PA[0:1, 1024:1024 + TW], onesc16, i_,
                           start=f, stop=l),
                  after=[tcst, t_consts, ctx["rows_user"],
                         ctx["evac_last"]], inc=False)
                tm2 = T(lambda e, i_=x2t, f=(dcc == 0), l=(dcc == DC - 1):
                        e.matmul(PA[0:1, 1536:1536 + TW], onesc16, i_,
                                 start=f, stop=l),
                        after=[tq, ctx["evac_last"]], inc=True)
                mm_s.append(tm2)
                if dcc % 3 == 2:
                    yield
            tv_ = V(lambda e: e.tensor_copy(rowb[0:1, 0:TW],
                                            PA[0:1, 1024:1024 + TW]),
                    after=[mm_s[-1]])
            yield
            t_s1 = row_transposes(rowb, [0, 128, 256], cols[:, 0:3], [tv_])
            tv_ = V(lambda e: e.tensor_copy(rowb[0:1, 0:TW],
                                            PA[0:1, 1536:1536 + TW]),
                    after=[tv_, t_s1])
            ctx["rows_user"] = tv_
            yield
            t_scols = row_transposes(rowb, [0, 128, 256], cols[:, 3:6],
                                     [tv_])
            xv = bass.AP(A_xw.tensor, A_xw.offset,
                         [[A_xw.ap[0][0], 128], [1, TW], [TW, DC]])
            tmx = V(lambda e, v=xv: e.tensor_reduce(A_mx, v, AX.X, OP.max),
                    after=[t_xl])
            tmn = V(lambda e, v=xv: e.tensor_reduce(A_mn, v, AX.X, OP.min),
                    after=[t_xl, tmx])
            yield
            stat_toks = {}
            for sidx, (buf, ci, rop, dep0) in enumerate(
                    [(A_mx, 6, OP.max, tmx), (A_mn, 9, OP.min, tmn)]):
                for blk in range(3):
                    n = min(128, TW - 128 * blk)
                    tt = T(lambda e, b=blk, bf=buf, nn=n: e.transpose(
                        PB[0:nn, 1280:1280 + 128],
                        bf[:, 128 * b:128 * b + nn],
                        ident), after=[dep0, pb_last[2], t_consts])
                    pb_last[2] = V(lambda e, b=blk, cc=ci, nn=n, op=rop:
                                   e.tensor_reduce(
                                       cols[0:nn, cc + b:cc + b + 1],
                                       PB[0:nn, 1280:1280 + 128], AX, op),
                                   after=[tt])
                    if blk == 1:
                        yield
                stat_toks[sidx] = pb_last[2]
                yield
            t_mxc, t_mnc = stat_toks[0], stat_toks[1]
            CW = 3
            m_c = cols[:, 12:12 + CW]; inv_c = cols[:, 15:15 + CW]
            var_c = cols[:, 18:18 + CW]; ra_c = cols[:, 21:21 + CW]
            q1_c = cols[:, 24:24 + CW]; o1_c = cols[:, 27:27 + CW]
            mq_c = cols[:, 44:44 + CW]; tmpc = cols[:, 48:48 + CW]
            tv_ = V(lambda e: e.tensor_scalar(m_c, cols[:, 0:3], 1.0 / D,
                                              None, OP.mult),
                    after=[t_scols])
            tv_ = V(lambda e: e.tensor_tensor(var_c, m_c, m_c, OP.mult),
                    after=[tv_])
            tv_ = V(lambda e: e.tensor_scalar(tmpc, cols[:, 3:6], 1.0 / D,
                                              None, OP.mult), after=[tv_])
            tv_ = V(lambda e: e.tensor_tensor(var_c, tmpc, var_c,
                                              OP.subtract), after=[tv_])
            tv_ = V(lambda e: e.tensor_scalar(var_c, var_c, 1e-5, None,
                                              OP.add), after=[tv_])
            tinv = rsqrt_rows(inv_c, var_c, tmpc, [tv_])
            yield
            tv_ = V(lambda e: e.tensor_tensor(ra_c, cols[:, 6:9], m_c,
                                              OP.subtract),
                    after=[t_mxc, tinv])
            tv_ = V(lambda e: e.tensor_tensor(tmpc, m_c, cols[:, 9:12],
                                              OP.subtract),
                    after=[t_mnc, tv_])
            tv_ = V(lambda e: e.tensor_tensor(ra_c, ra_c, tmpc, OP.max),
                    after=[tv_])
            tv_ = V(lambda e: e.tensor_tensor(ra_c, ra_c, inv_c, OP.mult),
                    after=[tv_])
            tv_ = V(lambda e: e.tensor_scalar(ra_c, ra_c, 1e-5, None,
                                              OP.max), after=[tv_])
            tq1 = V(lambda e: e.reciprocal(q1_c, ra_c), after=[tv_])
            tq1 = V(lambda e: e.tensor_tensor(q1_c, q1_c, inv_c, OP.mult),
                    after=[tq1])
            tq1 = V(lambda e: e.tensor_scalar(q1_c, q1_c, 127.0, None,
                                              OP.mult), after=[tq1])
            to1 = V(lambda e: e.tensor_scalar(o1_c, ra_c, sb_mw[:, 2:3],
                                              None, OP.mult),
                    after=[tq1, t_w1q])
            tmq = V(lambda e: e.tensor_tensor(mq_c, m_c, q1_c, OP.mult),
                    after=[to1])
            yield
            tB = bcast_multi([(A_r0, q1_c), (A_r1, o1_c), (A_r2, mq_c)],
                             A_bt, [tq1, to1, tmq], gen_yield=True)
            tB0 = None
            for tB0 in tB:
                yield
            st["tB1"] = tB0
            tv_ = tB0
            n1_toks = []
            n1s = A_t2[1]
            for dcc in range(DC):
                x_t = A_xw[:, dcc * TW:(dcc + 1) * TW]
                n1t = A_n1[:, dcc * TW:(dcc + 1) * TW]
                tv_ = V(lambda e, i_=x_t: e.tensor_tensor(
                    n1s, i_, A_r0, OP.mult),
                    after=[tB0, t_xl, tv_, ctx["conv_last"],
                           ctx["silu_last"]])
                tv_ = V(lambda e: e.tensor_tensor(n1s, n1s, A_r2,
                                                  OP.subtract),
                        after=[tv_])
                tv_ = V(lambda e, o_=n1t: e.tensor_scalar(
                    o_, n1s, kc[:, 6:7], kc[:, 6:7], OP.add, OP.subtract),
                    after=[tv_, t_ceff])
                n1_toks.append(tv_)
                if dcc % 2 == 1:
                    yield
            st["n1"] = n1_toks
            st["xw"] = A_xw

        def drive(g, n):
            if g is None:
                return
            for _ in range(n):
                if next(g, "END") == "END":
                    return

        pend = None
        if quantize:
            for _ in gen_Q(0):
                pass
            gen0 = gen_A(0)
            nq2 = 0
            for _ in gen_Q(1):
                nq2 += 1
                if nq2 <= 32:
                    if nq2 % 3 == 0 and nq2 <= 30:
                        drive(gen0, 1)
                elif nq2 % 2 == 0:
                    drive(gen0, 1)
        else:
            t_w1q = Y(lambda e: e.dma_start(out=A_w1q, in_=w1q_d),
                      sem=sW[0])
            t_w2q = Y(lambda e: e.dma_start(out=A_w2q, in_=w2q_d),
                      sem=sW[1])
            gen0 = gen_A(0)
        drive(gen0, 999)
        pend = None
        for tc in range(min(NT, ntc)):
            if tc > 0:
                drive(pend, 999)
            st = state[tc]
            A_xw = st["xw"]
            n1_toks = st["n1"]
            tB1 = st["tB1"]
            h_toks = {}
            cv_toks = {}
            for m in range(HC):
                pab = m % 4
                deps = [t_w1q, evtok.get(("h", m - 4))]
                if m in (2, 3):
                    deps.append(ctx["rows_user"])
                for kcb in range(DC):
                    tm = T(lambda e, k=kcb, mm=m, b=pab, f=(kcb == 0),
                           l=(kcb == DC - 1):
                           e.matmul(PA[:, 512 * b:512 * b + TW],
                                    A_w1q[:, k * H + 128 * mm:
                                          k * H + 128 * mm + 128],
                                    A_n1[:, k * TW:(k + 1) * TW],
                                    start=f, stop=l),
                           after=(deps + [n1_toks[kcb]]) if kcb == 0
                           else [n1_toks[kcb]],
                           inc=(kcb == DC - 1))
                h_t = A_h[:, m * TW:(m + 1) * TW]
                tev = V(lambda e, o_=h_t, b=pab: e.tensor_tensor(
                    o_, PA[:, 512 * b:512 * b + TW], A_r1, OP.mult),
                    after=[tm, tB1, t_w2q])
                evtok[("h", m)] = tev
                t1b = A_t1[m % 2]
                tc1 = V(lambda e, hh=h_t, mm=m, tb=t1b:
                        e.scalar_tensor_tensor(
                    tb[:, 0:256], hh[:, 0:256], sb_r02[:, mm:mm + 1],
                    hh[:, 2:258], OP.mult, OP.add),
                    after=[tev, cv_toks.get(m - 2)])
                tc2 = V(lambda e, hh=h_t, mm=m, ta=t1b:
                        e.scalar_tensor_tensor(
                    hh[:, 1:257], hh[:, 1:257], sb_r12[:, mm:mm + 1],
                    ta[:, 0:256], OP.mult, OP.add),
                    after=[tc1])
                cv_toks[m] = tc2
                ctx["conv_last"] = tc2
                tsl = SC(lambda e, hh=h_t, mm=m: e.activation(
                    hh[:, 1:257], hh[:, 1:257], AF.Silu,
                    bias=sb_cb2[:, mm:mm + 1],
                    scale=sb_w2t[:, mm:mm + 1]), after=[tc2])
                h_toks[m] = tsl
                ctx["silu_last"] = tsl
                ctx["evac_last"] = evtok[("h", m)]
            pend = gen_A(tc + 1) if tc + 1 < min(NT, ntc) else None
            drive(pend, 6)
            hv = bass.AP(A_h.tensor, A_h.offset + 1,
                         [[A_h.ap[0][0], 128], [1, 256], [TW, HC]])
            hdep = [h_toks[m] for m in range(HC)]
            tmabs = V(lambda e: e.tensor_reduce(
                A_eq, hv, AX.X, OP.max,
                apply_absolute_value=True), after=hdep)
            dep = tmabs
            for blk in range(2):
                tt = T(lambda e, b=blk: e.transpose(
                    PB[:, 1280:1280 + 128],
                    A_eq[:, 128 * b:128 * b + 128],
                    ident), after=[dep, pb_last[2], t_consts])
                pb_last[2] = V(lambda e, b=blk: e.tensor_reduce(
                    cols[:, 32 + b:33 + b:1], PB[:, 1280:1280 + 128],
                    AX.X, OP.max), after=[tt])
                dep = pb_last[2]
            t_mh = dep
            CW2 = 2
            ra2 = cols[:, 34:34 + CW2]; q2_c = cols[:, 36:36 + CW2]
            o2_c = cols[:, 38:38 + CW2]
            tv_ = V(lambda e: e.tensor_scalar(ra2, cols[:, 32:32 + CW2], 1e-5,
                                              None, OP.max), after=[t_mh])
            tq2 = V(lambda e: e.reciprocal(q2_c, ra2), after=[tv_])
            tq2 = V(lambda e: e.tensor_scalar(q2_c, q2_c, 127.0, None,
                                              OP.mult), after=[tq2])
            to2 = V(lambda e: e.tensor_scalar(o2_c, ra2, sb_mw[:, 3:4],
                                              None, OP.mult),
                    after=[tq2, t_w2q])
            drive(pend, 2)
            tO2 = bcast_multi([(A_r3, q2_c), (A_r4, o2_c)], A_bt,
                              [tq2, to2])
            drive(pend, 2)
            n2_toks = []
            tv_ = tO2
            for g in range(HC // 4):
                qv_ = bass.AP(A_h.tensor, A_h.offset + g * 4 * TW + 1,
                              [[A_h.ap[0][0], 128], [TW, 4], [1, 256]])
                nv_ = bass.AP(A_n2A.tensor, A_n2A.offset + g * 4 * TW * 2,
                              [[A_n2A.ap[0][0], 128], [TW * 2, 4], [1, 256]])
                r3b_ = A_r3.unsqueeze(1).broadcast_to([128, 4, 256])
                tv_ = V(lambda e, h_=qv_, r_=r3b_: e.tensor_tensor(
                    h_, h_, r_, OP.mult), after=[tO2, tv_])
                tv_ = V(lambda e, h_=qv_, n_=nv_: e.tensor_scalar(
                    n_, h_, kc[:, 6:7], kc[:, 6:7],
                    OP.add, OP.subtract), after=[tv_, t_ceff])
                n2_toks.append(tv_)
                drive(pend, 1)
            st_last = None
            res_prev = None
            for mo in range(DC):
                pbb = mo % 2
                deps = [t_w2q, evtok.get(("o", mo - 2))]
                for kcb in range(HC):
                    tm = T(lambda e, k=kcb, b=pbb, mm=mo, f=(kcb == 0),
                           l=(kcb == HC - 1):
                           e.matmul(PB[:, 512 * b:512 * b + 256],
                                    A_w2q[:, k * D + 128 * mm:
                                          k * D + 128 * mm + 128],
                                    A_n2A[:, k * TW * 2:k * TW * 2 + 256],
                                    start=f, stop=l),
                           after=(deps + [n2_toks[kcb // 4]]) if kcb == 0
                           else [n2_toks[kcb // 4]],
                           inc=(kcb == HC - 1))
                x_t = A_xw[:, mo * TW + 1:mo * TW + 257]
                tev = V(lambda e, b=pbb: e.tensor_tensor(
                    A_eq, PB[:, 512 * b:512 * b + 256],
                    A_r4, OP.mult), after=[tm, tO2, res_prev])
                evtok[("o", mo)] = tev
                tev = V(lambda e, xx=x_t: e.tensor_tensor(
                    xx, A_eq, xx, OP.add), after=[tev])
                res_prev = tev
                st_last = Y(lambda e, xx=x_t,
                            d=dst_dram[128 * mo:128 * mo + 128,
                                       TST * tc:TST * tc + 256]:
                            e.dma_start(out=d, in_=xx), after=[tev],
                            sem=sS2[tc % 2])
                drive(pend, 6)
            st_hist[tc] = st_last

    def attn_phase(src_dram, dst_dram):
        prev_st = None
        xn_done = None
        for qc in range(4):
            t0 = 512 * qc
            xl = None
            for dcc in range(DC):
                xl = Y(lambda e, d=B_x1[:, dcc * 512:dcc * 512 + 512],
                       s=src_dram[128 * dcc:128 * dcc + 128, t0:t0 + 512]:
                       e.dma_start(out=d, in_=s), after=[prev_st])
            t_xl = xl
            mm_s = []
            for dcc in range(DC):
                x_t = B_x1[:, dcc * 512:dcc * 512 + 512]
                x2t = B_gb[dcc % 2]
                xbt = B_gxb[dcc % 2]
                dep = [t_xl, xn_done]
                if dcc >= 2:
                    dep.append(mm_s[dcc - 2])
                tcst = SC(lambda e, o_=xbt, i_=x_t: e.activation(
                    o_, i_, AF.Identity), after=dep)
                tq = SC(lambda e, o_=x2t, i_=x_t: e.activation(o_, i_,
                                                               AF.Square),
                        after=[tcst])
                T(lambda e, i_=xbt, f=(dcc == 0), l=(dcc == DC - 1):
                  e.matmul(PA[0:1, 1024:1536], onesc16, i_, start=f, stop=l),
                  after=[tcst, t_consts], inc=False)
                tm2 = T(lambda e, i_=x2t, f=(dcc == 0), l=(dcc == DC - 1):
                        e.matmul(PA[0:1, 1536:2048], onesc16, i_,
                                 start=f, stop=l),
                        after=[tq], inc=True)
                mm_s.append(tm2)
            tv_ = V(lambda e: e.tensor_copy(rowb[0:1, 0:512],
                                            PA[0:1, 1024:1536]),
                    after=[mm_s[-1]])
            t_s1 = row_transposes(rowb, [0, 128, 256, 384], cols[:, 0:4],
                                  [tv_])
            tv_ = V(lambda e: e.tensor_copy(rowb[0:1, 0:512],
                                            PA[0:1, 1536:2048]),
                    after=[tv_, t_s1])
            t_scols = row_transposes(rowb, [0, 128, 256, 384], cols[:, 4:8],
                                     [tv_])
            CW = 4
            m_c = cols[:, 12:12 + CW]; inv_c = cols[:, 16:16 + CW]
            var_c = cols[:, 20:20 + CW]; tmpc = cols[:, 48:48 + CW]
            tv_ = V(lambda e: e.tensor_scalar(m_c, cols[:, 0:4], 1.0 / D,
                                              None, OP.mult), after=[t_scols])
            tv_ = V(lambda e: e.tensor_tensor(var_c, m_c, m_c, OP.mult),
                    after=[tv_])
            tv_ = V(lambda e: e.tensor_scalar(tmpc, cols[:, 4:8], 1.0 / D,
                                              None, OP.mult), after=[tv_])
            tv_ = V(lambda e: e.tensor_tensor(var_c, tmpc, var_c, OP.subtract),
                    after=[tv_])
            tv_ = V(lambda e: e.tensor_scalar(var_c, var_c, 1e-5, None,
                                              OP.add), after=[tv_])
            tinv = rsqrt_rows(inv_c, var_c, tmpc, [tv_])
            tBm = bcast_row(B_rows[:, 0:512], m_c, 4, B_bt, [tinv])
            tBi = bcast_row(B_rows[:, 512:1024], inv_c, 4, B_bt, [tinv, tBm])
            mmq = []
            tv_ = tBi
            r_st = None
            for dcc in range(DC):
                x_t = B_x1[:, dcc * 512:dcc * 512 + 512]
                xl_t = B_xl[:, dcc * 512:dcc * 512 + 512]
                tv_ = V(lambda e, xx=x_t, o_=xl_t: e.tensor_tensor(
                    o_, xx, B_rows[:, 0:512], OP.subtract),
                    after=[tBm, t_xl, tv_])
                tv_ = V(lambda e, o_=xl_t: e.tensor_tensor(
                    o_, o_, B_rows[:, 512:1024], OP.mult), after=[tBi, tv_])
                tv_ = V(lambda e, xx=x_t, o_=xl_t: e.tensor_tensor(
                    xx, xx, o_, OP.add), after=[tv_])
                r_st = Y(lambda e, xx=x_t,
                         d=rs[128 * dcc:128 * dcc + 128, t0:t0 + 512]:
                         e.dma_start(out=d, in_=xx), after=[tv_], sem=sDS)
                x2t = B_gb[dcc % 2]
                dep = [tv_]
                if dcc >= 2:
                    dep.append(mmq[dcc - 2])
                tq = SC(lambda e, o_=x2t, i_=xl_t: e.activation(
                    o_, i_, AF.Square), after=dep)
                tm2 = T(lambda e, i_=x2t, f=(dcc == 0), l=(dcc == DC - 1):
                        e.matmul(PA[0:1, 1024:1536], onesc16, i_,
                                 start=f, stop=l),
                        after=[tq], inc=True)
                mmq.append(tm2)
            prev_st = r_st
            tv_ = V(lambda e: e.tensor_copy(rowb[0:1, 0:512],
                                            PA[0:1, 1024:1536]),
                    after=[mmq[-1]])
            t_rms = row_transposes(rowb, [0, 128, 256, 384], cols[:, 0:4],
                                   [tv_])
            tv_ = V(lambda e: e.tensor_scalar(m_c, cols[:, 0:4], 1.0 / D,
                                              1e-6, OP.mult, OP.add),
                    after=[t_rms])
            tinv2 = rsqrt_rows(inv_c, m_c, tmpc, [tv_])
            tBr = bcast_row(B_rows[:, 1024:1536], inv_c, 4, B_bt,
                            [tinv2, tBi])
            tv_ = tBr
            for dcc in range(DC):
                xl_t = B_xl[:, dcc * 512:dcc * 512 + 512]
                xn_t = B_xn[:, dcc * S + t0:dcc * S + t0 + 512]
                tv_ = V(lambda e, i_=xl_t: e.tensor_tensor(
                    B_g[:, 0:512], i_, B_rows[:, 1024:1536], OP.mult),
                    after=[tBr, tv_])
                tv_ = V(lambda e, o_=xn_t, dd=dcc: e.tensor_scalar(
                    o_, B_g[:, 0:512], sb_nw[:, dd:dd + 1], None, OP.mult),
                    after=[tv_])
            xn_done = tv_
        t_rs_done = sDS.tok
        t_xn = xn_done
        if dbg:
            t_xn = Y(lambda e: e.dma_start(out=dbg_xn, in_=B_xn),
                     after=[xn_done], sem=sDW)

        ig_ld = {}
        u_toks = {}
        for m in range(IC):
            w_in, w_g = B_igs[(m % 2) * 2], B_igs[(m % 2) * 2 + 1]
            dep = [u_toks.get((m - 2, 1)), t_xn]
            tl = GD(lambda e, d=w_in, s=in_ws[m]: e.dma_start(out=d, in_=s),
                    after=dep, sem=sG2[m % 2])
            tl = GD(lambda e, d=w_g, s=gate_ws[m]: e.dma_start(out=d, in_=s),
                    sem=sG2[m % 2])
            ig_ld[m] = tl
            for half in range(2):
                pp = PA if half == 0 else PB
                hoff = 1024 * half
                deps = [tl, t_xn, u_toks.get((m - 1, half))]
                for kcb in range(DC):
                    for ncb in range(2):
                        T(lambda e, k=kcb, nn=ncb, w=w_in, po=hoff, pq=pp,
                          f=(kcb == 0), l=(kcb == DC - 1):
                          e.matmul(pq[:, 512 * nn:512 * nn + 512],
                                   w[:, k * 128:(k + 1) * 128],
                                   B_xn[:, k * S + po + 512 * nn:
                                        k * S + po + 512 * nn + 512],
                                   start=f, stop=l),
                          after=deps if (kcb == 0 and ncb == 0) else [],
                          inc=False)
                for kcb in range(DC):
                    for ncb in range(2):
                        tg_ = T(lambda e, k=kcb, nn=ncb, w=w_g, po=hoff, pq=pp,
                                f=(kcb == 0), l=(kcb == DC - 1):
                                e.matmul(pq[:, 1024 + 512 * nn:
                                            1024 + 512 * nn + 512],
                                         w[:, k * 128:(k + 1) * 128],
                                         B_xn[:, k * S + po + 512 * nn:
                                              k * S + po + 512 * nn + 512],
                                         start=f, stop=l),
                                after=[],
                                inc=(kcb == DC - 1 and ncb == 1))
                gbuf = B_g[:, 1024 * half:1024 * half + 1024]
                tsg = SC(lambda e, g=gbuf, mm=m, pq=pp: e.activation(
                    g, pq[:, 1024:2048], AF.Sigmoid,
                    bias=sb_gateb[:, mm:mm + 1]), after=[tg_])
                ut = B_u[:, m * S + hoff:m * S + hoff + 1024]
                tu = V(lambda e, g=gbuf, o_=ut, mm=m, pq=pp:
                       e.scalar_tensor_tensor(o_, pq[:, 0:1024],
                                              sb_inb[:, mm:mm + 1], g,
                                              OP.add, OP.mult),
                       after=[tsg, t_rs_done])
                u_toks[(m, half)] = tu
        t_u = sV.tok

        bcd_ld = {}
        y_toks = {}
        scan_tok = None
        for m in range(IC):
            sb3 = [B_bcds[(m % 2) * 3 + i] for i in range(3)]
            dep = [(y_toks[m - 2][0] if m >= 2 else None), t_u]
            for i, ws in enumerate([b_ws, c_ws, d_ws]):
                tl = GD(lambda e, d=sb3[i], s=ws[m]: e.dma_start(out=d, in_=s),
                        after=dep if i == 0 else [], sem=sB2[m % 2])
            bcd_ld[m] = tl
            depb = [tl, t_u, (y_toks[m - 1][0] if m >= 1 else None)]
            for nb in range(4):
                for kcb in range(IC):
                    T(lambda e, k=kcb, nn=nb, w=sb3[0], f=(kcb == 0):
                      e.matmul(PA[:, 512 * nn:512 * nn + 512],
                               w[:, k * 128:(k + 1) * 128],
                               B_u[:, k * S + 512 * nn:k * S + 512 * nn + 512],
                               start=f, stop=False),
                      after=depb if (nb == 0 and kcb == 0) else [], inc=False)
                tb = T(lambda e, nn=nb, mm=m:
                       e.matmul(PA[:, 512 * nn:512 * nn + 512],
                                bbr16[0:1, 128 * mm:128 * mm + 128],
                                ones16[0:1, 0:512], start=False, stop=True),
                       after=[t_bb], inc=(nb == 3))
            scan_dep = [tb, t_acol, scan_tok,
                        (y_toks[m - 1][1] if m >= 1 else None)]
            tsc_ = V(lambda e, mm=m: e.tensor_tensor_scan(
                B_hs, sb_acol[:, mm:mm + 1].broadcast_to([128, S]),
                PA[:, 0:S], 0.0, OP.mult, OP.add), after=scan_dep)
            scan_tok = tsc_
            depc = [tl, t_u, (y_toks[m - 1][1] if m >= 1 else None)]
            for nb in range(4):
                for kcb in range(IC):
                    tcm = T(lambda e, k=kcb, nn=nb, w=sb3[1], f=(kcb == 0),
                            l=(kcb == IC - 1):
                            e.matmul(PB[:, 512 * nn:512 * nn + 512],
                                     w[:, k * 128:(k + 1) * 128],
                                     B_u[:, k * S + 512 * nn:
                                          k * S + 512 * nn + 512],
                                     start=f, stop=l),
                            after=depc if (nb == 0 and kcb == 0) else [],
                            inc=(nb == 3 and kcb == IC - 1))
            ty0 = V(lambda e, mm=m: e.scalar_tensor_tensor(
                B_scr, PB[:, 0:S], sb_cb[:, mm:mm + 1], B_hs,
                OP.add, OP.mult),
                after=[tcm, tsc_, (y_toks[m - 1][0] if m >= 1 else None)])
            for nb in range(4):
                for kcb in range(IC):
                    tdm = T(lambda e, k=kcb, nn=nb, w=sb3[2], f=(kcb == 0),
                            l=(kcb == IC - 1):
                            e.matmul(PA[:, 512 * nn:512 * nn + 512],
                                     w[:, k * 128:(k + 1) * 128],
                                     B_u[:, k * S + 512 * nn:
                                          k * S + 512 * nn + 512],
                                     start=f, stop=l),
                            after=[tsc_] if (nb == 0 and kcb == 0) else [],
                            inc=(nb == 3 and kcb == IC - 1))
            y_t = B_y[:, m * S:(m + 1) * S]
            ty1 = V(lambda e, mm=m, o_=y_t: e.scalar_tensor_tensor(
                o_, PA[:, 0:S], sb_db[:, mm:mm + 1], B_scr, OP.add, OP.add),
                after=[tdm, ty0])
            y_toks[m] = (ty1, ty0)
        t_y = sV.tok

        out_ld = {}
        st_toks = {}
        ev_toks = {}
        for mo in range(DC):
            osl = B_outs[mo % 2]
            rb = B_rr[mo % 2]
            tl = GD(lambda e, d=osl, s=out_ws[mo]: e.dma_start(out=d, in_=s),
                    after=[ev_toks.get(mo - 2), t_u], sem=sO2[mo % 2])
            out_ld[mo] = tl
            trl = Y(lambda e, d=rb, s=rs[128 * mo:128 * mo + 128, :]:
                    e.dma_start(out=d, in_=s),
                    after=[st_toks.get(mo - 2), t_y], sem=sR2[mo % 2])
            pp = PA if mo % 2 == 0 else PB
            deps = [tl, t_y, ev_toks.get(mo - 2)]
            for nb in range(4):
                for kcb in range(IC):
                    tm = T(lambda e, k=kcb, nn=nb, w=osl, pq=pp, f=(kcb == 0),
                           l=(kcb == IC - 1):
                           e.matmul(pq[:, 512 * nn:512 * nn + 512],
                                    w[:, k * 128:(k + 1) * 128],
                                    B_y[:, k * S + 512 * nn:
                                         k * S + 512 * nn + 512],
                                    start=f, stop=l),
                           after=deps if (nb == 0 and kcb == 0) else [],
                           inc=(nb == 3 and kcb == IC - 1))
            tev = V(lambda e, mm=mo, r=rb, pq=pp: e.scalar_tensor_tensor(
                r, pq[:, 0:S], sb_outb[:, mm:mm + 1], r, OP.add, OP.add),
                after=[tm, trl])
            ev_toks[mo] = tev
            st_toks[mo] = Y(lambda e, r=rb,
                            d=dst_dram[128 * mo:128 * mo + 128, :]:
                            e.dma_start(out=d, in_=r), after=[tev],
                            sem=sS2[mo % 2])

    if phases == 1:
        ffn_phase(xT, outT, quantize=True)
    elif phases == 2:
        ffn_phase(xT, xs1, quantize=True)
        p.barrier(ALLS)
        attn_phase(xs1, outT)
    else:
        ffn_phase(xT, xs1, quantize=True)
        p.barrier(ALLS)
        attn_phase(xs1, xs2)
        p.barrier(ALLS)
        ffn_phase(xs2, outT, quantize=False)
    p.barrier(ALLS)

    with nc.Block() as block:
        def runner(eng_name):
            def run(e):
                for waits, fn, inc in p.items[eng_name]:
                    for (h, v) in waits:
                        e.wait_ge(h, v)
                    if fn is None:
                        continue
                    ins = fn(e)
                    if inc is not None:
                        ins.then_inc(inc[0], inc[1])
            return run
        block.sync(runner("sync"))
        block.vector(runner("vector"))
        block.scalar(runner("scalar"))
        block.tensor(runner("tensor"))
        block.gpsimd(runner("gpsimd"))
    return nc


def _prep_inputs(w):
    f = np.ascontiguousarray

    def slab(wm, mtiles, ktiles):
        return f(wm.reshape(mtiles, 128, ktiles, 128).transpose(0, 3, 2, 1)
                 .reshape(mtiles, 128, ktiles * 128).astype(np.float32))

    conv_w = w["conv_w"].reshape(H, 3)
    d = {
        "w1T": f(w["w1"].T.astype(np.float32)),
        "w2T": f(w["w2"].T.astype(np.float32)),
        "in_ws": slab(w["in_w"], IC, DC),
        "gate_ws": slab(w["gate_w"], IC, DC),
        "b_ws": slab(w["b_w"], IC, IC),
        "c_ws": slab(w["c_w"], IC, IC),
        "d_ws": slab(w["d_w"], IC, IC),
        "out_ws": slab(w["out_w"], DC, IC),
        "in_bc": f(w["in_b"].reshape(IC, 128).T.astype(np.float32)),
        "gate_bc": f(w["gate_b"].reshape(IC, 128).T.astype(np.float32)),
        "b_br": f(w["b_b"].reshape(1, INNER).astype(np.float32)),
        "c_bc": f(w["c_b"].reshape(IC, 128).T.astype(np.float32)),
        "d_bc": f(w["d_b"].reshape(IC, 128).T.astype(np.float32)),
        "out_bc": f(w["out_b"].reshape(DC, 128).T.astype(np.float32)),
        "nw_c": f(w["norm_w"].reshape(DC, 128).T.astype(np.float32)),
        "alog_c": f(w["a_log"].reshape(IC, 128).T.astype(np.float32)),
        "tap0_c": f(conv_w[:, 0].reshape(HC, 128).T.astype(np.float32)),
        "tap1_c": f(conv_w[:, 1].reshape(HC, 128).T.astype(np.float32)),
        "tap2_c": f(conv_w[:, 2].reshape(HC, 128).T.astype(np.float32)),
        "convb_c": f(w["conv_b"].reshape(HC, 128).T.astype(np.float32)),
        "ident_d": np.eye(128, dtype=np.float32),
    }
    return d


def kernel(L0_attn_x, L0_attn_norm_w, L0_attn_in_w, L0_attn_in_b,
           L0_attn_gate_w, L0_attn_gate_b, L0_attn_b_w, L0_attn_b_b,
           L0_attn_c_w, L0_attn_c_b, L0_attn_d_w, L0_attn_d_b,
           L0_attn_out_w, L0_attn_out_b, L0_attn_a_log,
           L0_ffn_w1, L0_ffn_conv_w, L0_ffn_conv_b, L0_ffn_w2,
           _phases=3):
    x = np.asarray(L0_attn_x, dtype=np.float32)
    w = dict(norm_w=L0_attn_norm_w, in_w=L0_attn_in_w, in_b=L0_attn_in_b,
             gate_w=L0_attn_gate_w, gate_b=L0_attn_gate_b, b_w=L0_attn_b_w,
             b_b=L0_attn_b_b, c_w=L0_attn_c_w, c_b=L0_attn_c_b,
             d_w=L0_attn_d_w, d_b=L0_attn_d_b, out_w=L0_attn_out_w,
             out_b=L0_attn_out_b, a_log=L0_attn_a_log, w1=L0_ffn_w1,
             conv_w=L0_ffn_conv_w, conv_b=L0_ffn_conv_b, w2=L0_ffn_w2)
    w = {k: np.asarray(v) for k, v in w.items()}
    shared = _prep_inputs(w)
    nc = build(phases=_phases)
    in_maps = []
    for c in range(8):
        m = dict(shared)
        m["xT"] = np.ascontiguousarray(x[c].T)
        in_maps.append(m)
    res = run_bass_kernel_spmd(nc, in_maps, core_ids=list(range(8)))
    out = np.stack([np.asarray(r["outT"]).T for r in res.results], axis=0)
    return np.ascontiguousarray(out.astype(np.float32))


# version 19 (final)
# speedup vs baseline: 1.1073x; 1.0202x over previous
import numpy as np
import concourse.bass as bass
import concourse.mybir as mybir
from concourse.bass_utils import run_bass_kernel_spmd

F32 = mybir.dt.float32
BF16 = mybir.dt.bfloat16
U8 = mybir.dt.uint8
OP = mybir.AluOpType
AF = mybir.ActivationFunctionType
AX = mybir.AxisListType.X

RC = 12582912.0
WCLIP = 1.4999999
C52 = 6755399441055744.0
X52 = C52 - RC
D, S, INNER, H = 1024, 2048, 2048, 4096
DC, IC, HC = D // 128, INNER // 128, H // 128
TW, TST, NT = 258, 256, 8

ENGINES = ("sync", "vector", "scalar", "tensor", "gpsimd")


class Sem:
    def __init__(self, nc, name):
        self.h = nc.alloc_semaphore(name)
        self.n = 0

    @property
    def tok(self):
        return (self, self.n)


class Prog:

    def __init__(self, nc):
        self.nc = nc
        self.items = {e: [] for e in ENGINES}
        self.waited = {e: {} for e in ENGINES}

    def emit(self, eng, fn, after=(), inc=None, inck=1):
        waits = []
        wm = self.waited[eng]
        for t in after:
            if t is None:
                continue
            s, v = t
            if v <= 0:
                continue
            if v > wm.get(s, 0):
                waits.append((s.h, v))
                wm[s] = v
        tok = None
        pair = None
        if inc is not None:
            inc.n += inck
            tok = (inc, inc.n)
            pair = (inc.h, inck)
        self.items[eng].append((waits, fn, pair))
        return tok

    def barrier(self, sems):
        toks = [s.tok for s in sems]
        for e in ENGINES:
            for s, v in toks:
                if v > 0 and v > self.waited[e].get(s, 0):
                    self.items[e].append(([(s.h, v)], None, None))
                    self.waited[e][s] = v


def build(phases=3, dbg=False, ntc=None):
    import os
    if ntc is None:
        ntc = int(os.environ.get("K_NTC", "99"))
    kstage = int(os.environ.get("K_STAGE", "99"))
    nc = bass.Bass("TRN2", target_bir_lowering=False)
    p = Prog(nc)

    dt_in = {}

    def din(name, shape, dtype=F32):
        dt_in[name] = nc.dram_tensor(name, list(shape), dtype,
                                     kind="ExternalInput").ap()
        return dt_in[name]

    xT = din("xT", [D, S])
    w1T = din("w1T", [D, H])
    w2T = din("w2T", [H, D])
    in_ws = din("in_ws", [IC, 128, DC * 128])
    gate_ws = din("gate_ws", [IC, 128, DC * 128])
    b_ws = din("b_ws", [IC, 128, IC * 128])
    c_ws = din("c_ws", [IC, 128, IC * 128])
    d_ws = din("d_ws", [IC, 128, IC * 128])
    out_ws = din("out_ws", [DC, 128, IC * 128])
    in_bc = din("in_bc", [128, IC])
    gate_bc = din("gate_bc", [128, IC])
    b_br = din("b_br", [1, INNER])
    c_bc = din("c_bc", [128, IC])
    d_bc = din("d_bc", [128, IC])
    out_bc = din("out_bc", [128, DC])
    nw_c = din("nw_c", [128, DC])
    alog_c = din("alog_c", [128, IC])
    tap0_c = din("tap0_c", [128, HC])
    tap1_c = din("tap1_c", [128, HC])
    tap2_c = din("tap2_c", [128, HC])
    convb_c = din("convb_c", [128, HC])
    ident_d = din("ident_d", [128, 128])

    outT = nc.dram_tensor("outT", [D, S], F32, kind="ExternalOutput").ap()
    dbg_h = nc.dram_tensor("dbg_h", [128, HC * TW], F32,
                           kind="ExternalOutput").ap() if dbg else None
    dbg_r = nc.dram_tensor("dbg_r", [128, 3 * TW], F32,
                           kind="ExternalOutput").ap() if dbg else None
    dbg_c = nc.dram_tensor("dbg_c", [128, 64], F32,
                           kind="ExternalOutput").ap() if dbg else None
    dbg_x2 = nc.dram_tensor("dbg_x2", [128, 2 * TW], F32,
                            kind="ExternalOutput").ap() if dbg else None
    dbg_xn = nc.dram_tensor("dbg_xn", [128, DC * S], BF16,
                            kind="ExternalOutput").ap() if dbg else None
    dbg_s2 = nc.dram_tensor("dbg_s2", [1, TW], F32,
                            kind="ExternalOutput").ap() if dbg else None
    xs1 = nc.dram_tensor("xs1", [D, S], F32, kind="Internal").ap()
    rs = nc.dram_tensor("rs", [D, S], F32, kind="Internal").ap()
    xs2 = nc.dram_tensor("xs2", [D, S], F32, kind="Internal").ap()
    w1q_d = nc.dram_tensor("w1q_d", [128, DC * H], BF16, kind="Internal").ap()
    w2q_d = nc.dram_tensor("w2q_d", [128, HC * D], BF16, kind="Internal").ap()

    ident = nc.alloc_sbuf_tensor("ident", [128, 128], F32).ap()
    ones128 = nc.alloc_sbuf_tensor("ones128", [128, 128], F32).ap()
    onesc = nc.alloc_sbuf_tensor("onesc", [128, 1], F32).ap()
    onesc16 = nc.alloc_sbuf_tensor("onesc16", [128, 1], BF16).ap()
    ones16 = nc.alloc_sbuf_tensor("ones16", [1, 512], BF16).ap()
    bbr16 = nc.alloc_sbuf_tensor("bbr16", [1, INNER], BF16).ap()
    kc = nc.alloc_sbuf_tensor("kc", [128, 8], F32).ap()
    sb_inb = nc.alloc_sbuf_tensor("sb_inb", [128, IC], F32).ap()
    sb_gateb = nc.alloc_sbuf_tensor("sb_gateb", [128, IC], F32).ap()
    sb_cb = nc.alloc_sbuf_tensor("sb_cb", [128, IC], F32).ap()
    sb_db = nc.alloc_sbuf_tensor("sb_db", [128, IC], F32).ap()
    sb_outb = nc.alloc_sbuf_tensor("sb_outb", [128, DC], F32).ap()
    sb_nw = nc.alloc_sbuf_tensor("sb_nw", [128, DC], F32).ap()
    sb_alog = nc.alloc_sbuf_tensor("sb_alog", [128, IC], F32).ap()
    sb_acol = nc.alloc_sbuf_tensor("sb_acol", [128, IC], F32).ap()
    sb_t0 = nc.alloc_sbuf_tensor("sb_t0", [128, HC], F32).ap()
    sb_t1 = nc.alloc_sbuf_tensor("sb_t1", [128, HC], F32).ap()
    sb_w2t = nc.alloc_sbuf_tensor("sb_w2t", [128, HC], F32).ap()
    sb_cb2 = nc.alloc_sbuf_tensor("sb_cb2", [128, HC], F32).ap()
    sb_r02 = nc.alloc_sbuf_tensor("sb_r02", [128, HC], F32).ap()
    sb_r12 = nc.alloc_sbuf_tensor("sb_r12", [128, HC], F32).ap()
    sb_mw = nc.alloc_sbuf_tensor("sb_mw", [128, 8], F32).ap()
    cols = nc.alloc_sbuf_tensor("cols", [128, 64], F32).ap()
    rowb = nc.alloc_sbuf_tensor("rowb", [1, 512], F32).ap()

    AR = 198 * 1024
    backing = nc.alloc_sbuf_tensor("arena", [128, AR], U8)
    base = nc.lookup_mloc(backing).addr

    def place(name, shape, dtype, off):
        per = int(np.prod(shape[1:])) * mybir.dt.size(dtype)
        assert off >= 0 and off + per <= AR, f"{name}: {off}+{per} > {AR}"
        h = nc.virtual_sbuf_tensor(name, list(shape), dtype)
        nc._tensor(h.name, h.shape, h.dtype, type="SB")
        mloc = nc.lookup_mloc(h)
        mloc.allocated = True
        mloc.addr = base + off
        return h.ap()

    o = 0
    A_w1q = place("A_w1q", [128, DC * H], BF16, o); o += DC * H * 2
    A_w2q = place("A_w2q", [128, HC * D], BF16, o); o += HC * D * 2
    A_xw2 = [place(f"A_xw{i}", [128, DC * TW], F32, o + i * DC * TW * 4)
             for i in range(2)]
    o += 2 * DC * TW * 4
    o_h = o
    A_h = place("A_h", [128, HC * TW], F32, o); o += HC * TW * 4
    A_n2 = [place(f"A_n2_{m}", [128, 256], BF16, o_h + m * TW * 4)
            for m in range(HC)]
    A_n2A = place("A_n2A", [128, HC * TW * 2], BF16, o_h)
    A_wc = [place(f"A_wc{i}", [128, 1024], F32, o_h + i * 4096) for i in range(4)]
    A_n1 = place("A_n1", [128, DC * TW], BF16, o); o += DC * TW * 2
    A_qt = place("A_qt", [128, 256], F32, o); o += 1024
    A_eq = place("A_eq", [128, 256], F32, o); o += 1024
    o_t1 = o
    A_t1 = [place(f"A_t1_{i}", [128, TW], F32, o + i * TW * 4) for i in range(2)]
    o += 2 * TW * 4
    o_t2 = o
    A_t2 = [place(f"A_t2_{i}", [128, TW], F32, o + i * TW * 4) for i in range(2)]
    o += 2 * TW * 4
    A_t1b = [place(f"A_t1b_{i}", [128, TW], BF16, o_t1 + i * TW * 4)
             for i in range(2)]
    A_t2b = [place(f"A_t2b_{i}", [128, TW], BF16, o_t2 + i * TW * 2)
             for i in range(2)]
    A_q4 = place("A_q4", [128, 4 * TW], F32, o_t1)
    A_mx = place("A_mx", [128, TW], F32, o); o += TW * 4
    A_mn = place("A_mn", [128, TW], F32, o); o += TW * 4
    A_r0 = place("A_r0", [128, TW], F32, o); o += TW * 4
    A_r1 = place("A_r1", [128, TW], F32, o); o += TW * 4
    A_r2 = place("A_r2", [128, TW], F32, o); o += TW * 4
    A_bt = place("A_bt", [128, 9 * 128], F32, o)
    o += 9 * 128 * 4
    A_r3 = place("A_r3", [128, 256], F32, o); o += 1024
    A_r4 = place("A_r4", [128, 256], F32, o); o += 1024
    assert o <= AR, o

    o = 0
    o_xn = o
    B_xn = place("B_xn", [128, DC * S], BF16, o); o += DC * S * 2
    o_u = o
    B_u = place("B_u", [128, IC * S], BF16, o); o += IC * S * 2
    o_y = o
    B_y = place("B_y", [128, IC * S], BF16, o); o += IC * S * 2
    B_hs = place("B_hs", [128, S], F32, o); o += S * 4
    B_scr = place("B_scr", [128, S], F32, o); o += S * 4
    o_bg = o
    B_g = place("B_g", [128, 2 * 1024], F32, o); o += 2 * 1024 * 4
    B_gb = [place(f"B_gb_{i}", [128, 512], BF16, o_bg + i * 1024)
            for i in range(2)]
    B_gxb = [place(f"B_gxb_{i}", [128, 512], BF16, o_bg + 2048 + i * 1024)
             for i in range(2)]
    assert o <= AR, o
    B_x1 = place("B_x1", [128, DC * 512], F32, o_u)
    B_xl = place("B_xl", [128, DC * 512], F32, o_u + DC * 512 * 4)
    B_rows = place("B_rows", [128, 3 * 512], F32, o_y)
    B_bt = place("B_bt", [128, 4 * 128], F32, o_y + 3 * 512 * 4)
    B_igs = [place(f"B_igs{i}", [128, DC * 128], BF16,
                   o_y + 3 * 512 * 4 + 4 * 128 * 4 + i * DC * 256)
             for i in range(4)]
    B_bcds = [place(f"B_bcds{i}", [128, IC * 128], BF16, o_xn + i * IC * 256)
              for i in range(6)]
    B_outs = [place(f"B_outs{i}", [128, IC * 128], BF16,
                    o_xn + (6 + i) * IC * 256)
              for i in range(2)]
    B_rr = [place(f"B_rr{i}", [128, S], F32, o_u + i * S * 4) for i in range(2)]

    PA = nc.alloc_psum_tensor("PA", [128, 2048], F32).ap()
    PB = nc.alloc_psum_tensor("PB", [128, 2048], F32).ap()

    sDX = Sem(nc, "sDX")
    sDW = Sem(nc, "sDW")
    sDS = Sem(nc, "sDS")
    sDG = Sem(nc, "sDG")
    sTE = Sem(nc, "sTE")
    sV = Sem(nc, "sV")
    sSC = Sem(nc, "sSC")
    sGP = Sem(nc, "sGP")
    sW = [Sem(nc, f"sW{i}") for i in range(4)]
    sG2 = [Sem(nc, "sG0"), Sem(nc, "sG1")]
    sB2 = [Sem(nc, "sB0"), Sem(nc, "sB1")]
    sO2 = [Sem(nc, "sO0"), Sem(nc, "sO1")]
    sR2 = [Sem(nc, "sR0"), Sem(nc, "sR1")]
    sMISC = Sem(nc, "sMISC")
    sS2 = [Sem(nc, "sS0"), Sem(nc, "sS1")]
    sX2 = [Sem(nc, "sX0"), Sem(nc, "sX1")]
    ALLS = [sDX, sDW, sDS, sDG, sTE, sV, sSC, sGP,
            sW[0], sW[1], sW[2], sW[3], sG2[0], sG2[1], sB2[0], sB2[1],
            sO2[0], sO2[1], sR2[0], sR2[1], sMISC, sS2[0], sS2[1],
            sX2[0], sX2[1]]

    def Y(fn, after=(), sem=None):
        return p.emit("sync", fn, after, inc=sem if sem else sDX, inck=16)

    def GD(fn, after=(), sem=None):
        return p.emit("gpsimd", fn, after, inc=sem if sem else sDG, inck=16)

    def V(fn, after=()):
        return p.emit("vector", fn, after, inc=sV)

    def SC(fn, after=()):
        return p.emit("scalar", fn, after, inc=sSC)

    def T(fn, after=(), inc=True):
        return p.emit("tensor", fn, after, inc=sTE if inc else None)

    def G(fn, after=()):
        return p.emit("gpsimd", fn, after, inc=sGP)

    for name, dst in [("ident_d", ident), ("in_bc", sb_inb),
                      ("gate_bc", sb_gateb), ("c_bc", sb_cb), ("d_bc", sb_db),
                      ("out_bc", sb_outb), ("nw_c", sb_nw), ("alog_c", sb_alog),
                      ("tap0_c", sb_t0), ("tap1_c", sb_t1), ("tap2_c", sb_w2t),
                      ("convb_c", sb_cb2)]:
        src = dt_in[name]
        Y(lambda e, d=dst, s=src: e.dma_start(out=d, in_=s))
    t_consts = sDX.tok
    V(lambda e: e.memset(kc[:, 0:1], 1.0))
    V(lambda e: e.memset(kc[:, 1:2], -1.0))
    V(lambda e: e.memset(kc[:, 2:3], -0.5))
    V(lambda e: e.memset(kc[:, 3:4], 0.0))
    V(lambda e: e.memset(kc[:, 4:5], RC))
    V(lambda e: e.memset(kc[:, 5:6], -RC))
    tk_ = V(lambda e: e.memset(kc[:, 7:8], 1.0))
    tk_ = V(lambda e: e.tensor_scalar(kc[:, 7:8], kc[:, 7:8], C52, C52,
                                      OP.add, OP.subtract), after=[tk_])
    t_ceff = V(lambda e: e.tensor_scalar(kc[:, 6:7], kc[:, 7:8], X52, RC,
                                         OP.mult, OP.add), after=[tk_])
    V(lambda e: e.memset(onesc, 1.0))
    V(lambda e: e.memset(onesc16, 1.0))
    V(lambda e: e.memset(ones128, 1.0))
    V(lambda e: e.memset(ones16, 1.0))
    V(lambda e: e.memset(rowb, 0.0))
    V(lambda e: e.memset(cols, 0.0))
    t_bb = GD(lambda e: e.dma_start(out=bbr16, in_=dt_in["b_br"]), sem=sMISC)
    tv = V(lambda e: e.reciprocal(sb_r02, sb_w2t), after=[t_consts])
    tv2 = V(lambda e: e.tensor_mul(sb_r12, sb_t1, sb_r02), after=[tv])
    V(lambda e: e.tensor_mul(sb_r02, sb_t0, sb_r02), after=[tv2])
    c0, c1 = cols[:, 0:IC], cols[:, 16:32]
    tv = V(lambda e: e.tensor_scalar(c0, sb_alog, -1.0, None, OP.mult),
           after=[t_consts])
    tv = V(lambda e: e.tensor_tensor(c0, c0, sb_alog, OP.max), after=[tv])
    ts = SC(lambda e: e.activation(c1, c0, AF.Exp, scale=kc[:, 1:2]),
            after=[tv])
    ts = SC(lambda e: e.activation(c1, c1, AF.Ln, bias=kc[:, 0:1]), after=[ts])
    tv = V(lambda e: e.tensor_scalar(c0, sb_alog, 0.0, None, OP.max),
           after=[t_consts, tv, ts])
    tv = V(lambda e: e.tensor_tensor(c0, c0, c1, OP.add), after=[tv, ts])
    t_acol = SC(lambda e: e.activation(sb_acol, c0, AF.Exp, scale=kc[:, 1:2]),
                after=[tv])

    pb_last = {2: None, 3: None}
    def rsqrt_rows(dst, src, tmp, deps):
        t1 = V(lambda e: e.tensor_scalar(src, src, 1e-30, None, OP.max),
               after=deps)
        t1 = SC(lambda e: e.activation(tmp, src, AF.Ln), after=[t1])
        return SC(lambda e: e.activation(dst, tmp, AF.Exp, scale=kc[:, 2:3]),
                  after=[t1])

    def bcast_multi(jobs, bt, deps, gen_yield=False):
        it = _bcast_multi_gen(jobs, bt, deps)
        if gen_yield:
            return it
        last = None
        for last in it:
            pass
        return last

    def _bcast_multi_gen(jobs, bt, deps):
        blocks = []
        for dst, src_col in jobs:
            nblk = src_col.shape[1]
            for c_ in range(nblk):
                blocks.append((dst, src_col, c_, min(128, dst.shape[1] - 128 * c_)))
        nb = len(blocks)
        t0 = V(lambda e, bl=list(blocks): [
            None], after=deps) if False else None
        off = 0
        copy_toks = []
        offs = []
        tprev = None
        for dst, src_col in jobs:
            nblk = src_col.shape[1]
            tprev = V(lambda e, sc=src_col, o=off, nn=nblk: e.tensor_copy(
                bt[:, o:o + nn * 128].rearrange("p (c j) -> p c j", c=nn),
                sc.unsqueeze(2).broadcast_to([128, nn, 128])),
                after=deps + [tprev])
            offs.append(off)
            off += nblk * 128
        t0 = tprev
        assert off <= bt.shape[1], (off, bt.shape)
        i = 0
        last = None
        for ji, (dst, src_col) in enumerate(jobs):
            nblk = src_col.shape[1]
            for c_ in range(nblk):
                n = min(128, dst.shape[1] - 128 * c_)
                bk = 2 + (i % 2)
                pbase = 1024 + 512 * (bk - 2) + 256
                tt = T(lambda e, o=offs[ji] + 128 * c_, pb=pbase: e.transpose(
                    PB[:, pb:pb + 128], bt[:, o:o + 128], ident),
                    after=[t0, t_consts, pb_last[bk]])
                last = V(lambda e, c=c_, nn=n, d=dst, pb=pbase: e.tensor_copy(
                    d[:, 128 * c:128 * c + nn], PB[:, pb:pb + nn]),
                    after=[tt, last])
                pb_last[bk] = last
                i += 1
                yield last

    def bcast_row(dst, src_col, nblk, bt, deps, upto=99):
        return bcast_multi([(dst, src_col[:, 0:nblk])], bt, deps)

    def row_transposes(src_row, offs, dst_cols, deps):
        toks = []
        for j, so in enumerate(offs):
            tt = T(lambda e, jj=j, s=so: e.transpose(
                PB[:, 1024 + jj:1024 + jj + 1], src_row[0:1, s:s + 128],
                ident[0:1, 0:1]),
                after=deps + [t_consts, pb_last[2]])
            toks.append(tt)
        ev = V(lambda e, n=len(offs): e.tensor_copy(
            dst_cols, PB[:, 1024:1024 + n]), after=[toks[-1]])
        pb_last[2] = ev
        return ev

    def ffn_phase(src_dram, dst_dram, quantize):
        t_w1q = t_w2q = None

        def gen_Q(wi):
            nonlocal t_w1q, t_w2q
            wT, wq, wq_dr = [(w1T, A_w1q, w1q_d),
                             (w2T, A_w2q, w2q_d)][wi]
            nch = 32
            ncol = H if wi == 0 else D
            nq = ncol // 1024
            sumc = cols[:, 32:64]
            toks = []
            for i in range(nch):
                kcb, q = divmod(i, nq)
                src = wT[128 * kcb:128 * kcb + 128, 1024 * q:1024 * q + 1024]
                dep = [toks[i - 4][1]] if i >= 4 else [t_acol]
                tl = Y(lambda e, d=A_wc[i % 4], s=src:
                       e.dma_start(out=d, in_=s), after=dep, sem=sW[i % 4])
                tr = V(lambda e, d=A_wc[i % 4], c=sumc[:, i:i + 1]:
                       e.reduce_sum(c, d, AX, apply_absolute_value=True),
                       after=[tl])
                toks.append((tl, tr))
                yield
            tv_ = V(lambda e: e.reduce_sum(cols[:, 30:31], sumc, AX),
                    after=[toks[-1][1]])
            tm_ = T(lambda e: e.matmul(PB[:, 1024:1025], ones128,
                                       cols[:, 30:31],
                                       start=True, stop=True),
                    after=[tv_, t_consts, pb_last[2]])
            tv_ = V(lambda e: e.tensor_copy(cols[:, 31:32], PB[:, 1024:1025]),
                    after=[tm_])
            pb_last[2] = tv_
            tv_ = V(lambda e, nrm=1.0 / (1024 * 4096): e.tensor_scalar(
                cols[:, 30:31], cols[:, 31:32], nrm, 1e-5,
                OP.mult, OP.max), after=[tv_])
            tv_ = V(lambda e, m=wi: e.reciprocal(sb_mw[:, m:m + 1],
                                                 cols[:, 30:31]),
                    after=[tv_])
            if wi == 0:
                tsc = V(lambda e: e.tensor_scalar(
                    sb_mw[:, 2:3], cols[:, 30:31], 1.0 / 127, None,
                    OP.mult), after=[tv_])
            else:
                tsc = V(lambda e: e.tensor_scalar(
                    sb_mw[:, 3:4], cols[:, 30:31], 0.5 / 127, None,
                    OP.mult), after=[tv_])
            qtk = []
            for i in range(nch):
                kcb, q = divmod(i, nq)
                src = wT[128 * kcb:128 * kcb + 128, 1024 * q:1024 * q + 1024]
                dep = [qtk[i - 4]] if i >= 4 else [tsc]
                tl = Y(lambda e, d=A_wc[i % 4], s=src:
                       e.dma_start(out=d, in_=s), after=dep, sem=sW[i % 4])
                tq = V(lambda e, d=A_wc[i % 4], m=wi: e.tensor_scalar(
                    d, d, sb_mw[:, m:m + 1], WCLIP, OP.mult, OP.min),
                    after=[tl, tsc])
                tq = V(lambda e, d=A_wc[i % 4]: e.tensor_scalar(
                    d, d, -WCLIP, None, OP.max), after=[tq])
                tq = V(lambda e, d=A_wc[i % 4],
                       w=wq[:, 1024 * i:1024 * (i + 1)]:
                       e.tensor_scalar(w, d, kc[:, 6:7], kc[:, 6:7],
                                       OP.add, OP.subtract),
                       after=[tq, t_ceff])
                qtk.append(tq)
                yield
            Y(lambda e, w=wq, dr=wq_dr: e.dma_start(out=dr, in_=w),
              after=[qtk[-1]], sem=sDS)
            if wi == 0:
                t_w1q = qtk[-1]
            else:
                t_w2q = qtk[-1]

        evtok = {}
        st_hist = {}
        state = {}
        ctx = {"rows_user": None, "conv_last": None, "silu_last": None,
               "evac_last": None}

        def gen_A(tc):
            st = state.setdefault(tc, {})
            A_xw = A_xw2[tc % 2]
            t0 = TST * tc - 1
            lo, hi = max(t0, 0), min(t0 + TW, S)
            woff = lo - t0
            xl = None
            for dcc in range(DC):
                xl = Y(lambda e,
                       d=A_xw[:, dcc * TW + woff:dcc * TW + woff + hi - lo],
                       s=src_dram[128 * dcc:128 * dcc + 128, lo:hi]:
                       e.dma_start(out=d, in_=s),
                       after=[st_hist.get(tc - 2)], sem=sX2[tc % 2])
            t_xl = xl
            if tc == 0:
                ap0 = bass.AP(A_xw.tensor, A_xw.offset,
                              [[A_xw.ap[0][0], 128], [TW, DC], [1, 1]])
                t_xl = V(lambda e, a=ap0: e.memset(a, 0.0), after=[t_xl])
            if tc == NT - 1:
                ap1 = bass.AP(A_xw.tensor, A_xw.offset + TW - 1,
                              [[A_xw.ap[0][0], 128], [TW, DC], [1, 1]])
                t_xl = V(lambda e, a=ap1: e.memset(a, 0.0), after=[t_xl])
            yield
            mm_s = []
            for dcc in range(DC):
                x_t = A_xw[:, dcc * TW:(dcc + 1) * TW]
                x2t = A_t1b[dcc % 2]
                xbt = A_t2b[dcc % 2]
                dep = [t_xl, ctx["conv_last"]]
                if dcc >= 2:
                    dep.append(mm_s[dcc - 2])
                tcst = SC(lambda e, o_=xbt, i_=x_t: e.activation(
                    o_, i_, AF.Identity), after=dep)
                tq = SC(lambda e, o_=x2t, i_=x_t: e.activation(o_, i_,
                                                               AF.Square),
                        after=[tcst])
                T(lambda e, i_=xbt, f=(dcc == 0), l=(dcc == DC - 1):
                  e.matmul(PA[0:1, 1024:1024 + TW], onesc16, i_,
                           start=f, stop=l),
                  after=[tcst, t_consts, ctx["rows_user"],
                         ctx["evac_last"]], inc=False)
                tm2 = T(lambda e, i_=x2t, f=(dcc == 0), l=(dcc == DC - 1):
                        e.matmul(PA[0:1, 1536:1536 + TW], onesc16, i_,
                                 start=f, stop=l),
                        after=[tq, ctx["evac_last"]], inc=True)
                mm_s.append(tm2)
                if dcc % 3 == 2:
                    yield
            tv_ = V(lambda e: e.tensor_copy(rowb[0:1, 0:TW],
                                            PA[0:1, 1024:1024 + TW]),
                    after=[mm_s[-1]])
            yield
            t_s1 = row_transposes(rowb, [0, 128, 256], cols[:, 0:3], [tv_])
            tv_ = V(lambda e: e.tensor_copy(rowb[0:1, 0:TW],
                                            PA[0:1, 1536:1536 + TW]),
                    after=[tv_, t_s1])
            ctx["rows_user"] = tv_
            yield
            t_scols = row_transposes(rowb, [0, 128, 256], cols[:, 3:6],
                                     [tv_])
            xv = bass.AP(A_xw.tensor, A_xw.offset,
                         [[A_xw.ap[0][0], 128], [1, TW], [TW, DC]])
            tmx = V(lambda e, v=xv: e.tensor_reduce(A_mx, v, AX.X, OP.max),
                    after=[t_xl])
            tmn = V(lambda e, v=xv: e.tensor_reduce(A_mn, v, AX.X, OP.min),
                    after=[t_xl, tmx])
            yield
            stat_toks = {}
            for sidx, (buf, ci, rop, dep0) in enumerate(
                    [(A_mx, 6, OP.max, tmx), (A_mn, 9, OP.min, tmn)]):
                for blk in range(3):
                    n = min(128, TW - 128 * blk)
                    tt = T(lambda e, b=blk, bf=buf, nn=n: e.transpose(
                        PB[0:nn, 1280:1280 + 128],
                        bf[:, 128 * b:128 * b + nn],
                        ident), after=[dep0, pb_last[2], t_consts])
                    pb_last[2] = V(lambda e, b=blk, cc=ci, nn=n, op=rop:
                                   e.tensor_reduce(
                                       cols[0:nn, cc + b:cc + b + 1],
                                       PB[0:nn, 1280:1280 + 128], AX, op),
                                   after=[tt])
                    if blk == 1:
                        yield
                stat_toks[sidx] = pb_last[2]
                yield
            t_mxc, t_mnc = stat_toks[0], stat_toks[1]
            CW = 3
            m_c = cols[:, 12:12 + CW]; inv_c = cols[:, 15:15 + CW]
            var_c = cols[:, 18:18 + CW]; ra_c = cols[:, 21:21 + CW]
            q1_c = cols[:, 24:24 + CW]; o1_c = cols[:, 27:27 + CW]
            mq_c = cols[:, 44:44 + CW]; tmpc = cols[:, 48:48 + CW]
            tv_ = V(lambda e: e.tensor_scalar(m_c, cols[:, 0:3], 1.0 / D,
                                              None, OP.mult),
                    after=[t_scols])
            tv_ = V(lambda e: e.tensor_tensor(var_c, m_c, m_c, OP.mult),
                    after=[tv_])
            tv_ = V(lambda e: e.tensor_scalar(tmpc, cols[:, 3:6], 1.0 / D,
                                              None, OP.mult), after=[tv_])
            tv_ = V(lambda e: e.tensor_tensor(var_c, tmpc, var_c,
                                              OP.subtract), after=[tv_])
            tv_ = V(lambda e: e.tensor_scalar(var_c, var_c, 1e-5, None,
                                              OP.add), after=[tv_])
            tinv = rsqrt_rows(inv_c, var_c, tmpc, [tv_])
            yield
            tv_ = V(lambda e: e.tensor_tensor(ra_c, cols[:, 6:9], m_c,
                                              OP.subtract),
                    after=[t_mxc, tinv])
            tv_ = V(lambda e: e.tensor_tensor(tmpc, m_c, cols[:, 9:12],
                                              OP.subtract),
                    after=[t_mnc, tv_])
            tv_ = V(lambda e: e.tensor_tensor(ra_c, ra_c, tmpc, OP.max),
                    after=[tv_])
            tv_ = V(lambda e: e.tensor_tensor(ra_c, ra_c, inv_c, OP.mult),
                    after=[tv_])
            tv_ = V(lambda e: e.tensor_scalar(ra_c, ra_c, 1e-5, None,
                                              OP.max), after=[tv_])
            tq1 = V(lambda e: e.reciprocal(q1_c, ra_c), after=[tv_])
            tq1 = V(lambda e: e.tensor_tensor(q1_c, q1_c, inv_c, OP.mult),
                    after=[tq1])
            tq1 = V(lambda e: e.tensor_scalar(q1_c, q1_c, 127.0, None,
                                              OP.mult), after=[tq1])
            to1 = V(lambda e: e.tensor_scalar(o1_c, ra_c, sb_mw[:, 2:3],
                                              None, OP.mult),
                    after=[tq1, t_w1q])
            tmq = V(lambda e: e.tensor_tensor(mq_c, m_c, q1_c, OP.mult),
                    after=[to1])
            yield
            tB = bcast_multi([(A_r0, q1_c), (A_r1, o1_c), (A_r2, mq_c)],
                             A_bt, [tq1, to1, tmq], gen_yield=True)
            tB0 = None
            for tB0 in tB:
                yield
            st["tB1"] = tB0
            tv_ = tB0
            n1_toks = []
            for g1_ in range(DC // 4):
                xg_ = bass.AP(A_xw.tensor, A_xw.offset + g1_ * 4 * TW,
                              [[A_xw.ap[0][0], 128], [TW, 4], [1, TW]])
                qg_ = bass.AP(A_q4.tensor, A_q4.offset,
                              [[A_q4.ap[0][0], 128], [TW, 4], [1, TW]])
                ng_ = bass.AP(A_n1.tensor, A_n1.offset + g1_ * 4 * TW,
                              [[A_n1.ap[0][0], 128], [TW, 4], [1, TW]])
                r0b_ = A_r0.unsqueeze(1).broadcast_to([128, 4, TW])
                r2b_ = A_r2.unsqueeze(1).broadcast_to([128, 4, TW])
                tv_ = V(lambda e, x_=xg_, q_=qg_, r_=r0b_: e.tensor_tensor(
                    q_, x_, r_, OP.mult),
                    after=[tB0, t_xl, tv_, ctx["conv_last"],
                           ctx["silu_last"]])
                tv_ = V(lambda e, q_=qg_, r_=r2b_: e.tensor_tensor(
                    q_, q_, r_, OP.subtract), after=[tv_])
                tv_ = V(lambda e, q_=qg_, n_=ng_: e.tensor_scalar(
                    n_, q_, kc[:, 6:7], kc[:, 6:7], OP.add, OP.subtract),
                    after=[tv_, t_ceff])
                n1_toks.append(tv_)
                yield
            st["n1"] = n1_toks
            st["xw"] = A_xw

        def drive(g, n):
            if g is None:
                return
            for _ in range(n):
                if next(g, "END") == "END":
                    return

        pend = None
        if quantize:
            for _ in gen_Q(0):
                pass
            gen0 = gen_A(0)
            nq2 = 0
            for _ in gen_Q(1):
                nq2 += 1
                if nq2 <= 32:
                    if nq2 % 3 == 0 and nq2 <= 30:
                        drive(gen0, 1)
                elif nq2 % 2 == 0:
                    drive(gen0, 1)
        else:
            t_w1q = Y(lambda e: e.dma_start(out=A_w1q, in_=w1q_d),
                      sem=sW[0])
            t_w2q = Y(lambda e: e.dma_start(out=A_w2q, in_=w2q_d),
                      sem=sW[1])
            gen0 = gen_A(0)
        drive(gen0, 999)
        pend = None
        for tc in range(min(NT, ntc)):
            if tc > 0:
                drive(pend, 999)
            st = state[tc]
            A_xw = st["xw"]
            n1_toks = st["n1"]
            tB1 = st["tB1"]
            h_toks = {}
            cv_toks = {}
            for m in range(HC):
                pab = m % 4
                deps = [t_w1q, evtok.get(("h", m - 4))]
                if m in (2, 3):
                    deps.append(ctx["rows_user"])
                for kcb in range(DC):
                    tm = T(lambda e, k=kcb, mm=m, b=pab, f=(kcb == 0),
                           l=(kcb == DC - 1):
                           e.matmul(PA[:, 512 * b:512 * b + TW],
                                    A_w1q[:, k * H + 128 * mm:
                                          k * H + 128 * mm + 128],
                                    A_n1[:, k * TW:(k + 1) * TW],
                                    start=f, stop=l),
                           after=(deps + [n1_toks[kcb // 4]]) if kcb == 0
                           else [n1_toks[kcb // 4]],
                           inc=(kcb == DC - 1))
                h_t = A_h[:, m * TW:(m + 1) * TW]
                tev = V(lambda e, o_=h_t, b=pab: e.tensor_tensor(
                    o_, PA[:, 512 * b:512 * b + TW], A_r1, OP.mult),
                    after=[tm, tB1, t_w2q])
                evtok[("h", m)] = tev
                t1b = A_t1[m % 2]
                tc1 = V(lambda e, hh=h_t, mm=m, tb=t1b:
                        e.scalar_tensor_tensor(
                    tb[:, 0:256], hh[:, 0:256], sb_r02[:, mm:mm + 1],
                    hh[:, 2:258], OP.mult, OP.add),
                    after=[tev, cv_toks.get(m - 2)])
                tc2 = V(lambda e, hh=h_t, mm=m, ta=t1b:
                        e.scalar_tensor_tensor(
                    hh[:, 1:257], hh[:, 1:257], sb_r12[:, mm:mm + 1],
                    ta[:, 0:256], OP.mult, OP.add),
                    after=[tc1])
                cv_toks[m] = tc2
                ctx["conv_last"] = tc2
                tsl = SC(lambda e, hh=h_t, mm=m: e.activation(
                    hh[:, 1:257], hh[:, 1:257], AF.Silu,
                    bias=sb_cb2[:, mm:mm + 1],
                    scale=sb_w2t[:, mm:mm + 1]), after=[tc2])
                h_toks[m] = tsl
                ctx["silu_last"] = tsl
                ctx["evac_last"] = evtok[("h", m)]
            pend = gen_A(tc + 1) if tc + 1 < min(NT, ntc) else None
            drive(pend, 6)
            hv = bass.AP(A_h.tensor, A_h.offset + 1,
                         [[A_h.ap[0][0], 128], [1, 256], [TW, HC]])
            hdep = [h_toks[m] for m in range(HC)]
            tmabs = V(lambda e: e.tensor_reduce(
                A_eq, hv, AX.X, OP.max,
                apply_absolute_value=True), after=hdep)
            dep = tmabs
            for blk in range(2):
                tt = T(lambda e, b=blk: e.transpose(
                    PB[:, 1280:1280 + 128],
                    A_eq[:, 128 * b:128 * b + 128],
                    ident), after=[dep, pb_last[2], t_consts])
                pb_last[2] = V(lambda e, b=blk: e.tensor_reduce(
                    cols[:, 32 + b:33 + b:1], PB[:, 1280:1280 + 128],
                    AX.X, OP.max), after=[tt])
                dep = pb_last[2]
            t_mh = dep
            CW2 = 2
            ra2 = cols[:, 34:34 + CW2]; q2_c = cols[:, 36:36 + CW2]
            o2_c = cols[:, 38:38 + CW2]
            tv_ = V(lambda e: e.tensor_scalar(ra2, cols[:, 32:32 + CW2], 1e-5,
                                              None, OP.max), after=[t_mh])
            tq2 = V(lambda e: e.reciprocal(q2_c, ra2), after=[tv_])
            tq2 = V(lambda e: e.tensor_scalar(q2_c, q2_c, 127.0, None,
                                              OP.mult), after=[tq2])
            to2 = V(lambda e: e.tensor_scalar(o2_c, ra2, sb_mw[:, 3:4],
                                              None, OP.mult),
                    after=[tq2, t_w2q])
            drive(pend, 2)
            tO2 = bcast_multi([(A_r3, q2_c), (A_r4, o2_c)], A_bt,
                              [tq2, to2])
            drive(pend, 2)
            n2_toks = []
            tv_ = tO2
            for g in range(HC // 4):
                qv_ = bass.AP(A_h.tensor, A_h.offset + g * 4 * TW + 1,
                              [[A_h.ap[0][0], 128], [TW, 4], [1, 256]])
                nv_ = bass.AP(A_n2A.tensor, A_n2A.offset + g * 4 * TW * 2,
                              [[A_n2A.ap[0][0], 128], [TW * 2, 4], [1, 256]])
                r3b_ = A_r3.unsqueeze(1).broadcast_to([128, 4, 256])
                tv_ = V(lambda e, h_=qv_, r_=r3b_: e.tensor_tensor(
                    h_, h_, r_, OP.mult), after=[tO2, tv_])
                tv_ = V(lambda e, h_=qv_, n_=nv_: e.tensor_scalar(
                    n_, h_, kc[:, 6:7], kc[:, 6:7],
                    OP.add, OP.subtract), after=[tv_, t_ceff])
                n2_toks.append(tv_)
                drive(pend, 1)
            st_last = None
            res_prev = None
            for mo in range(DC):
                pbb = mo % 2
                deps = [t_w2q, evtok.get(("o", mo - 2))]
                for kcb in range(HC):
                    tm = T(lambda e, k=kcb, b=pbb, mm=mo, f=(kcb == 0),
                           l=(kcb == HC - 1):
                           e.matmul(PB[:, 512 * b:512 * b + 256],
                                    A_w2q[:, k * D + 128 * mm:
                                          k * D + 128 * mm + 128],
                                    A_n2A[:, k * TW * 2:k * TW * 2 + 256],
                                    start=f, stop=l),
                           after=(deps + [n2_toks[kcb // 4]]) if kcb == 0
                           else [n2_toks[kcb // 4]],
                           inc=(kcb == HC - 1))
                x_t = A_xw[:, mo * TW + 1:mo * TW + 257]
                tev = V(lambda e, b=pbb: e.tensor_tensor(
                    A_eq, PB[:, 512 * b:512 * b + 256],
                    A_r4, OP.mult), after=[tm, tO2, res_prev])
                evtok[("o", mo)] = tev
                tev = V(lambda e, xx=x_t: e.tensor_tensor(
                    xx, A_eq, xx, OP.add), after=[tev])
                res_prev = tev
                st_last = Y(lambda e, xx=x_t,
                            d=dst_dram[128 * mo:128 * mo + 128,
                                       TST * tc:TST * tc + 256]:
                            e.dma_start(out=d, in_=xx), after=[tev],
                            sem=sS2[tc % 2])
                drive(pend, 6)
            st_hist[tc] = st_last

    def attn_phase(src_dram, dst_dram):
        prev_st = None
        xn_done = None
        for qc in range(4):
            t0 = 512 * qc
            xl = None
            for dcc in range(DC):
                xl = Y(lambda e, d=B_x1[:, dcc * 512:dcc * 512 + 512],
                       s=src_dram[128 * dcc:128 * dcc + 128, t0:t0 + 512]:
                       e.dma_start(out=d, in_=s), after=[prev_st])
            t_xl = xl
            mm_s = []
            for dcc in range(DC):
                x_t = B_x1[:, dcc * 512:dcc * 512 + 512]
                x2t = B_gb[dcc % 2]
                xbt = B_gxb[dcc % 2]
                dep = [t_xl, xn_done]
                if dcc >= 2:
                    dep.append(mm_s[dcc - 2])
                tcst = SC(lambda e, o_=xbt, i_=x_t: e.activation(
                    o_, i_, AF.Identity), after=dep)
                tq = SC(lambda e, o_=x2t, i_=x_t: e.activation(o_, i_,
                                                               AF.Square),
                        after=[tcst])
                T(lambda e, i_=xbt, f=(dcc == 0), l=(dcc == DC - 1):
                  e.matmul(PA[0:1, 1024:1536], onesc16, i_, start=f, stop=l),
                  after=[tcst, t_consts], inc=False)
                tm2 = T(lambda e, i_=x2t, f=(dcc == 0), l=(dcc == DC - 1):
                        e.matmul(PA[0:1, 1536:2048], onesc16, i_,
                                 start=f, stop=l),
                        after=[tq], inc=True)
                mm_s.append(tm2)
            tv_ = V(lambda e: e.tensor_copy(rowb[0:1, 0:512],
                                            PA[0:1, 1024:1536]),
                    after=[mm_s[-1]])
            t_s1 = row_transposes(rowb, [0, 128, 256, 384], cols[:, 0:4],
                                  [tv_])
            tv_ = V(lambda e: e.tensor_copy(rowb[0:1, 0:512],
                                            PA[0:1, 1536:2048]),
                    after=[tv_, t_s1])
            t_scols = row_transposes(rowb, [0, 128, 256, 384], cols[:, 4:8],
                                     [tv_])
            CW = 4
            m_c = cols[:, 12:12 + CW]; inv_c = cols[:, 16:16 + CW]
            var_c = cols[:, 20:20 + CW]; tmpc = cols[:, 48:48 + CW]
            tv_ = V(lambda e: e.tensor_scalar(m_c, cols[:, 0:4], 1.0 / D,
                                              None, OP.mult), after=[t_scols])
            tv_ = V(lambda e: e.tensor_tensor(var_c, m_c, m_c, OP.mult),
                    after=[tv_])
            tv_ = V(lambda e: e.tensor_scalar(tmpc, cols[:, 4:8], 1.0 / D,
                                              None, OP.mult), after=[tv_])
            tv_ = V(lambda e: e.tensor_tensor(var_c, tmpc, var_c, OP.subtract),
                    after=[tv_])
            tv_ = V(lambda e: e.tensor_scalar(var_c, var_c, 1e-5, None,
                                              OP.add), after=[tv_])
            tinv = rsqrt_rows(inv_c, var_c, tmpc, [tv_])
            tBm = bcast_row(B_rows[:, 0:512], m_c, 4, B_bt, [tinv])
            tBi = bcast_row(B_rows[:, 512:1024], inv_c, 4, B_bt, [tinv, tBm])
            mmq = []
            tv_ = tBi
            r_st = None
            for dcc in range(DC):
                x_t = B_x1[:, dcc * 512:dcc * 512 + 512]
                xl_t = B_xl[:, dcc * 512:dcc * 512 + 512]
                tv_ = V(lambda e, xx=x_t, o_=xl_t: e.tensor_tensor(
                    o_, xx, B_rows[:, 0:512], OP.subtract),
                    after=[tBm, t_xl, tv_])
                tv_ = V(lambda e, o_=xl_t: e.tensor_tensor(
                    o_, o_, B_rows[:, 512:1024], OP.mult), after=[tBi, tv_])
                tv_ = V(lambda e, xx=x_t, o_=xl_t: e.tensor_tensor(
                    xx, xx, o_, OP.add), after=[tv_])
                r_st = Y(lambda e, xx=x_t,
                         d=rs[128 * dcc:128 * dcc + 128, t0:t0 + 512]:
                         e.dma_start(out=d, in_=xx), after=[tv_], sem=sDS)
                x2t = B_gb[dcc % 2]
                dep = [tv_]
                if dcc >= 2:
                    dep.append(mmq[dcc - 2])
                tq = SC(lambda e, o_=x2t, i_=xl_t: e.activation(
                    o_, i_, AF.Square), after=dep)
                tm2 = T(lambda e, i_=x2t, f=(dcc == 0), l=(dcc == DC - 1):
                        e.matmul(PA[0:1, 1024:1536], onesc16, i_,
                                 start=f, stop=l),
                        after=[tq], inc=True)
                mmq.append(tm2)
            prev_st = r_st
            tv_ = V(lambda e: e.tensor_copy(rowb[0:1, 0:512],
                                            PA[0:1, 1024:1536]),
                    after=[mmq[-1]])
            t_rms = row_transposes(rowb, [0, 128, 256, 384], cols[:, 0:4],
                                   [tv_])
            tv_ = V(lambda e: e.tensor_scalar(m_c, cols[:, 0:4], 1.0 / D,
                                              1e-6, OP.mult, OP.add),
                    after=[t_rms])
            tinv2 = rsqrt_rows(inv_c, m_c, tmpc, [tv_])
            tBr = bcast_row(B_rows[:, 1024:1536], inv_c, 4, B_bt,
                            [tinv2, tBi])
            tv_ = tBr
            for dcc in range(DC):
                xl_t = B_xl[:, dcc * 512:dcc * 512 + 512]
                xn_t = B_xn[:, dcc * S + t0:dcc * S + t0 + 512]
                tv_ = V(lambda e, i_=xl_t: e.tensor_tensor(
                    B_g[:, 0:512], i_, B_rows[:, 1024:1536], OP.mult),
                    after=[tBr, tv_])
                tv_ = V(lambda e, o_=xn_t, dd=dcc: e.tensor_scalar(
                    o_, B_g[:, 0:512], sb_nw[:, dd:dd + 1], None, OP.mult),
                    after=[tv_])
            xn_done = tv_
        t_rs_done = sDS.tok
        t_xn = xn_done
        if dbg:
            t_xn = Y(lambda e: e.dma_start(out=dbg_xn, in_=B_xn),
                     after=[xn_done], sem=sDW)

        ig_ld = {}
        u_toks = {}
        for m in range(IC):
            w_in, w_g = B_igs[(m % 2) * 2], B_igs[(m % 2) * 2 + 1]
            dep = [u_toks.get((m - 2, 1)), t_xn]
            tl = GD(lambda e, d=w_in, s=in_ws[m]: e.dma_start(out=d, in_=s),
                    after=dep, sem=sG2[m % 2])
            tl = GD(lambda e, d=w_g, s=gate_ws[m]: e.dma_start(out=d, in_=s),
                    sem=sG2[m % 2])
            ig_ld[m] = tl
            for half in range(2):
                pp = PA if half == 0 else PB
                hoff = 1024 * half
                deps = [tl, t_xn, u_toks.get((m - 1, half))]
                for kcb in range(DC):
                    for ncb in range(2):
                        T(lambda e, k=kcb, nn=ncb, w=w_in, po=hoff, pq=pp,
                          f=(kcb == 0), l=(kcb == DC - 1):
                          e.matmul(pq[:, 512 * nn:512 * nn + 512],
                                   w[:, k * 128:(k + 1) * 128],
                                   B_xn[:, k * S + po + 512 * nn:
                                        k * S + po + 512 * nn + 512],
                                   start=f, stop=l),
                          after=deps if (kcb == 0 and ncb == 0) else [],
                          inc=False)
                for kcb in range(DC):
                    for ncb in range(2):
                        tg_ = T(lambda e, k=kcb, nn=ncb, w=w_g, po=hoff, pq=pp,
                                f=(kcb == 0), l=(kcb == DC - 1):
                                e.matmul(pq[:, 1024 + 512 * nn:
                                            1024 + 512 * nn + 512],
                                         w[:, k * 128:(k + 1) * 128],
                                         B_xn[:, k * S + po + 512 * nn:
                                              k * S + po + 512 * nn + 512],
                                         start=f, stop=l),
                                after=[],
                                inc=(kcb == DC - 1 and ncb == 1))
                gbuf = B_g[:, 1024 * half:1024 * half + 1024]
                tsg = SC(lambda e, g=gbuf, mm=m, pq=pp: e.activation(
                    g, pq[:, 1024:2048], AF.Sigmoid,
                    bias=sb_gateb[:, mm:mm + 1]), after=[tg_])
                ut = B_u[:, m * S + hoff:m * S + hoff + 1024]
                tu = V(lambda e, g=gbuf, o_=ut, mm=m, pq=pp:
                       e.scalar_tensor_tensor(o_, pq[:, 0:1024],
                                              sb_inb[:, mm:mm + 1], g,
                                              OP.add, OP.mult),
                       after=[tsg, t_rs_done])
                u_toks[(m, half)] = tu
        t_u = sV.tok

        bcd_ld = {}
        y_toks = {}
        scan_tok = None
        for m in range(IC):
            sb3 = [B_bcds[(m % 2) * 3 + i] for i in range(3)]
            dep = [(y_toks[m - 2][0] if m >= 2 else None), t_u]
            for i, ws in enumerate([b_ws, c_ws, d_ws]):
                tl = GD(lambda e, d=sb3[i], s=ws[m]: e.dma_start(out=d, in_=s),
                        after=dep if i == 0 else [], sem=sB2[m % 2])
            bcd_ld[m] = tl
            depb = [tl, t_u, (y_toks[m - 1][0] if m >= 1 else None)]
            for nb in range(4):
                for kcb in range(IC):
                    T(lambda e, k=kcb, nn=nb, w=sb3[0], f=(kcb == 0):
                      e.matmul(PA[:, 512 * nn:512 * nn + 512],
                               w[:, k * 128:(k + 1) * 128],
                               B_u[:, k * S + 512 * nn:k * S + 512 * nn + 512],
                               start=f, stop=False),
                      after=depb if (nb == 0 and kcb == 0) else [], inc=False)
                tb = T(lambda e, nn=nb, mm=m:
                       e.matmul(PA[:, 512 * nn:512 * nn + 512],
                                bbr16[0:1, 128 * mm:128 * mm + 128],
                                ones16[0:1, 0:512], start=False, stop=True),
                       after=[t_bb], inc=(nb == 3))
            scan_dep = [tb, t_acol, scan_tok,
                        (y_toks[m - 1][1] if m >= 1 else None)]
            tsc_ = V(lambda e, mm=m: e.tensor_tensor_scan(
                B_hs, sb_acol[:, mm:mm + 1].broadcast_to([128, S]),
                PA[:, 0:S], 0.0, OP.mult, OP.add), after=scan_dep)
            scan_tok = tsc_
            depc = [tl, t_u, (y_toks[m - 1][1] if m >= 1 else None)]
            for nb in range(4):
                for kcb in range(IC):
                    tcm = T(lambda e, k=kcb, nn=nb, w=sb3[1], f=(kcb == 0),
                            l=(kcb == IC - 1):
                            e.matmul(PB[:, 512 * nn:512 * nn + 512],
                                     w[:, k * 128:(k + 1) * 128],
                                     B_u[:, k * S + 512 * nn:
                                          k * S + 512 * nn + 512],
                                     start=f, stop=l),
                            after=depc if (nb == 0 and kcb == 0) else [],
                            inc=(nb == 3 and kcb == IC - 1))
            ty0 = V(lambda e, mm=m: e.scalar_tensor_tensor(
                B_scr, PB[:, 0:S], sb_cb[:, mm:mm + 1], B_hs,
                OP.add, OP.mult),
                after=[tcm, tsc_, (y_toks[m - 1][0] if m >= 1 else None)])
            for nb in range(4):
                for kcb in range(IC):
                    tdm = T(lambda e, k=kcb, nn=nb, w=sb3[2], f=(kcb == 0),
                            l=(kcb == IC - 1):
                            e.matmul(PA[:, 512 * nn:512 * nn + 512],
                                     w[:, k * 128:(k + 1) * 128],
                                     B_u[:, k * S + 512 * nn:
                                          k * S + 512 * nn + 512],
                                     start=f, stop=l),
                            after=[tsc_] if (nb == 0 and kcb == 0) else [],
                            inc=(nb == 3 and kcb == IC - 1))
            y_t = B_y[:, m * S:(m + 1) * S]
            ty1 = V(lambda e, mm=m, o_=y_t: e.scalar_tensor_tensor(
                o_, PA[:, 0:S], sb_db[:, mm:mm + 1], B_scr, OP.add, OP.add),
                after=[tdm, ty0])
            y_toks[m] = (ty1, ty0)
        t_y = sV.tok

        out_ld = {}
        st_toks = {}
        ev_toks = {}
        for mo in range(DC):
            osl = B_outs[mo % 2]
            rb = B_rr[mo % 2]
            tl = GD(lambda e, d=osl, s=out_ws[mo]: e.dma_start(out=d, in_=s),
                    after=[ev_toks.get(mo - 2), t_u], sem=sO2[mo % 2])
            out_ld[mo] = tl
            trl = Y(lambda e, d=rb, s=rs[128 * mo:128 * mo + 128, :]:
                    e.dma_start(out=d, in_=s),
                    after=[st_toks.get(mo - 2), t_y], sem=sR2[mo % 2])
            pp = PA if mo % 2 == 0 else PB
            deps = [tl, t_y, ev_toks.get(mo - 2)]
            for nb in range(4):
                for kcb in range(IC):
                    tm = T(lambda e, k=kcb, nn=nb, w=osl, pq=pp, f=(kcb == 0),
                           l=(kcb == IC - 1):
                           e.matmul(pq[:, 512 * nn:512 * nn + 512],
                                    w[:, k * 128:(k + 1) * 128],
                                    B_y[:, k * S + 512 * nn:
                                         k * S + 512 * nn + 512],
                                    start=f, stop=l),
                           after=deps if (nb == 0 and kcb == 0) else [],
                           inc=(nb == 3 and kcb == IC - 1))
            tev = V(lambda e, mm=mo, r=rb, pq=pp: e.scalar_tensor_tensor(
                r, pq[:, 0:S], sb_outb[:, mm:mm + 1], r, OP.add, OP.add),
                after=[tm, trl])
            ev_toks[mo] = tev
            st_toks[mo] = Y(lambda e, r=rb,
                            d=dst_dram[128 * mo:128 * mo + 128, :]:
                            e.dma_start(out=d, in_=r), after=[tev],
                            sem=sS2[mo % 2])

    if phases == 1:
        ffn_phase(xT, outT, quantize=True)
    elif phases == 2:
        ffn_phase(xT, xs1, quantize=True)
        p.barrier(ALLS)
        attn_phase(xs1, outT)
    else:
        ffn_phase(xT, xs1, quantize=True)
        p.barrier(ALLS)
        attn_phase(xs1, xs2)
        p.barrier(ALLS)
        ffn_phase(xs2, outT, quantize=False)
    p.barrier(ALLS)

    with nc.Block() as block:
        def runner(eng_name):
            def run(e):
                for waits, fn, inc in p.items[eng_name]:
                    for (h, v) in waits:
                        e.wait_ge(h, v)
                    if fn is None:
                        continue
                    ins = fn(e)
                    if inc is not None:
                        ins.then_inc(inc[0], inc[1])
            return run
        block.sync(runner("sync"))
        block.vector(runner("vector"))
        block.scalar(runner("scalar"))
        block.tensor(runner("tensor"))
        block.gpsimd(runner("gpsimd"))
    return nc


def _prep_inputs(w):
    f = np.ascontiguousarray

    def slab(wm, mtiles, ktiles):
        return f(wm.reshape(mtiles, 128, ktiles, 128).transpose(0, 3, 2, 1)
                 .reshape(mtiles, 128, ktiles * 128).astype(np.float32))

    conv_w = w["conv_w"].reshape(H, 3)
    d = {
        "w1T": f(w["w1"].T.astype(np.float32)),
        "w2T": f(w["w2"].T.astype(np.float32)),
        "in_ws": slab(w["in_w"], IC, DC),
        "gate_ws": slab(w["gate_w"], IC, DC),
        "b_ws": slab(w["b_w"], IC, IC),
        "c_ws": slab(w["c_w"], IC, IC),
        "d_ws": slab(w["d_w"], IC, IC),
        "out_ws": slab(w["out_w"], DC, IC),
        "in_bc": f(w["in_b"].reshape(IC, 128).T.astype(np.float32)),
        "gate_bc": f(w["gate_b"].reshape(IC, 128).T.astype(np.float32)),
        "b_br": f(w["b_b"].reshape(1, INNER).astype(np.float32)),
        "c_bc": f(w["c_b"].reshape(IC, 128).T.astype(np.float32)),
        "d_bc": f(w["d_b"].reshape(IC, 128).T.astype(np.float32)),
        "out_bc": f(w["out_b"].reshape(DC, 128).T.astype(np.float32)),
        "nw_c": f(w["norm_w"].reshape(DC, 128).T.astype(np.float32)),
        "alog_c": f(w["a_log"].reshape(IC, 128).T.astype(np.float32)),
        "tap0_c": f(conv_w[:, 0].reshape(HC, 128).T.astype(np.float32)),
        "tap1_c": f(conv_w[:, 1].reshape(HC, 128).T.astype(np.float32)),
        "tap2_c": f(conv_w[:, 2].reshape(HC, 128).T.astype(np.float32)),
        "convb_c": f(w["conv_b"].reshape(HC, 128).T.astype(np.float32)),
        "ident_d": np.eye(128, dtype=np.float32),
    }
    return d


def kernel(L0_attn_x, L0_attn_norm_w, L0_attn_in_w, L0_attn_in_b,
           L0_attn_gate_w, L0_attn_gate_b, L0_attn_b_w, L0_attn_b_b,
           L0_attn_c_w, L0_attn_c_b, L0_attn_d_w, L0_attn_d_b,
           L0_attn_out_w, L0_attn_out_b, L0_attn_a_log,
           L0_ffn_w1, L0_ffn_conv_w, L0_ffn_conv_b, L0_ffn_w2,
           _phases=3):
    x = np.asarray(L0_attn_x, dtype=np.float32)
    w = dict(norm_w=L0_attn_norm_w, in_w=L0_attn_in_w, in_b=L0_attn_in_b,
             gate_w=L0_attn_gate_w, gate_b=L0_attn_gate_b, b_w=L0_attn_b_w,
             b_b=L0_attn_b_b, c_w=L0_attn_c_w, c_b=L0_attn_c_b,
             d_w=L0_attn_d_w, d_b=L0_attn_d_b, out_w=L0_attn_out_w,
             out_b=L0_attn_out_b, a_log=L0_attn_a_log, w1=L0_ffn_w1,
             conv_w=L0_ffn_conv_w, conv_b=L0_ffn_conv_b, w2=L0_ffn_w2)
    w = {k: np.asarray(v) for k, v in w.items()}
    shared = _prep_inputs(w)
    nc = build(phases=_phases)
    in_maps = []
    for c in range(8):
        m = dict(shared)
        m["xT"] = np.ascontiguousarray(x[c].T)
        in_maps.append(m)
    res = run_bass_kernel_spmd(nc, in_maps, core_ids=list(range(8)))
    out = np.stack([np.asarray(r["outT"]).T for r in res.results], axis=0)
    return np.ascontiguousarray(out.astype(np.float32))
